# version 08
# baseline (speedup 1.0000x reference)
import os

os.environ.setdefault("GAUGE_NTFF_IGNORE_DMA", "1")

import numpy as np
import concourse.bass as bass
import concourse.mybir as mybir
from concourse.bass_utils import run_bass_kernel_spmd
from concourse.alu_op_type import AluOpType as Op
from concourse.bass import IndirectOffsetOnAxis
from contextlib import ExitStack

F32 = mybir.dt.float32
BF16 = mybir.dt.bfloat16
I32 = mybir.dt.int32
AF = mybir.ActivationFunctionType
AX = mybir.AxisListType

NCORES = 8
B = 4
LFULL = 8192
C = 1024
LC = LFULL // NCORES
TOK = B * LC
NCH = C // 128
EPS = 1e-5
NHEAD = 16


def _rope_perm():
    perm = np.zeros(C, dtype=np.int64)
    for chk in range(NCH):
        for half in range(2):
            for hh in range(2):
                for j in range(32):
                    perm[128 * chk + 64 * half + 32 * hh + j] = (
                        128 * chk + 64 * hh + 2 * j + half)
    return perm


def _v_perm():
    perm = np.zeros(C, dtype=np.int64)
    for chk in range(NCH):
        for hh in range(2):
            for half in range(2):
                for j in range(32):
                    perm[128 * chk + 64 * hh + 32 * half + j] = (
                        128 * chk + 64 * hh + 2 * j + half)
    return perm


def _rope_cos_sin():
    scale = np.arange(0, 64, 2, dtype=np.float32)[:32]
    with np.errstate(over="ignore"):
        freqs = np.float32(1.0) / np.float32(10000.0) ** scale
    ang = np.arange(LFULL, dtype=np.float32)[:, None] * freqs[None, :]
    return np.cos(ang), np.sin(ang)


def shard_x(x):
    out = []
    for k in range(NCORES):
        sl = x[:, k * LC:(k + 1) * LC, :]
        out.append(np.ascontiguousarray(sl.transpose(2, 0, 1).reshape(C, TOK)))
    return out


def unshard_x(parts):
    xs = [np.asarray(p).reshape(C, B, LC).transpose(1, 2, 0) for p in parts]
    return np.concatenate(xs, axis=1)


def col_layout(v):
    v = np.asarray(v)
    r = v.shape[-1]
    return np.ascontiguousarray(v.reshape(r // 128, 128).T)


def a0_host_inputs(inputs, core):
    perm = _rope_perm()
    cos, sin = _rope_cos_sin()
    l0 = core * LC
    cos_c, sin_c = cos[l0:l0 + LC], sin[l0:l0 + LC]
    csfm = np.zeros((128, 2 * LC), np.float32)
    csfm[0:64, 0:LC] = cos_c[:, np.tile(np.arange(32), 2)].T
    csfm[64:128, 0:LC] = sin_c[:, np.tile(np.arange(32), 2)].T
    csfm[0:64, LC:2 * LC] = sin_c[:, np.tile(np.arange(32), 2)].T
    csfm[64:128, LC:2 * LC] = cos_c[:, np.tile(np.arange(32), 2)].T
    cst = np.concatenate([cos_c[:, np.tile(np.arange(32), 2)],
                          sin_c[:, np.tile(np.arange(32), 2)]], axis=1)
    cstok = np.ascontiguousarray(
        cst.reshape(LC // 128, 128, 128).transpose(1, 0, 2).reshape(128, LC))
    cones = np.ones((128, 128), np.float32)
    cones[1, 1] = EPS
    cones[1, 2] = -0.5
    vperm = _v_perm()
    d = {
        "wq": np.ascontiguousarray(inputs["L0_attn_Wq"][:, perm]),
        "wk": np.ascontiguousarray(inputs["L0_attn_Wk"][:, perm]),
        "wv": np.ascontiguousarray(inputs["L0_attn_Wv"][:, vperm]),
        "wo": np.ascontiguousarray(inputs["L0_attn_Wo"][vperm, :]),
        "bq": col_layout(inputs["L0_attn_bq"][perm]),
        "bo": col_layout(inputs["L0_attn_bo"]),
        "bkv": np.ascontiguousarray(np.stack([inputs["L0_attn_bk"][perm],
                                              inputs["L0_attn_bv"][vperm]])),
        "csfm": np.ascontiguousarray(csfm),
        "cstok": np.ascontiguousarray(cstok),
        "cones": cones,
    }
    return d



class Sems:

    GLOBAL = {"st", "cc", "cst", "cstg", "chv", "chs", "chg"}

    def __init__(self, nc, ctx):
        self.nc = nc
        self.ctx = ctx
        self.h = {}
        self.c = {}
        self.alias = None

    def begin_phase(self):
        self.alias = {}

    def _phys(self, name):
        if self.alias is None or name in self.GLOBAL:
            return name
        if name not in self.alias:
            self.alias[name] = "S%d" % len(self.alias)
        return self.alias[name]

    def _get(self, phys):
        if phys not in self.h:
            self.h[phys] = self.ctx.enter_context(self.nc.semaphore(phys))
            self.c[phys] = 0
        return self.h[phys]

    def handle(self, name):
        return self._get(self._phys(name))

    def inc(self, inst, name, amt=1):
        phys = self._phys(name)
        inst.then_inc(self._get(phys), amt)
        self.c[phys] += amt
        return self.c[phys]

    def dma(self, inst, name):
        return self.inc(inst, name, 16)

    def wait(self, eng, name, val):
        eng.wait_ge(self.handle(name), val)

    def val(self, name):
        phys = self._phys(name)
        self._get(phys)
        return self.c[phys]




CHAIN_OPS = {
    "tensor_copy", "tensor_tensor", "tensor_add", "tensor_sub", "tensor_mul",
    "tensor_max", "tensor_scalar", "tensor_single_scalar", "tensor_scalar_add",
    "tensor_scalar_sub", "tensor_scalar_mul", "tensor_scalar_max",
    "tensor_scalar_min", "tensor_relu", "tensor_reduce", "reduce_max",
    "reduce_sum", "scalar_tensor_tensor", "reciprocal", "memset", "memzero",
    "select", "copy_predicated", "stream_shuffle", "transpose", "iota",
    "activation", "copy", "add", "mul", "sqrt", "square", "sign",
    "bn_stats", "bn_aggr", "max", "max_index", "pool", "pool_avg", "pool_max",
}


class Chained:

    def __init__(self, eng, sems, name):
        self._e = eng
        self._s = sems
        self._n = name
        self._pend = None
        key = "_last_" + name
        if not hasattr(sems, key):
            setattr(sems, key, None)
        self._key = key

    def willinc(self, name, amt=1):
        self._pend = (name, amt)

    def __getattr__(self, attr):
        f = getattr(self._e, attr)
        if attr in CHAIN_OPS:
            def wrapped(*a, **k):
                last = getattr(self._s, self._key)
                if last is not None:
                    self._e.wait_ge(self._s._get(last[0]), last[1])
                ins = f(*a, **k)
                if self._pend is not None:
                    nm, amt = self._pend
                    self._pend = None
                else:
                    nm, amt = self._n, 1
                phys = self._s._phys(nm)
                v = self._s.inc(ins, nm, amt)
                setattr(self._s, self._key, (phys, v))
                return ins
            return wrapped
        return f


def chain_block(block, sems):
    def vec(f):
        block.vector(lambda v: f(Chained(v, sems, "chv")))

    def sca(f):
        block.scalar(lambda s: f(Chained(s, sems, "chs")))

    def gps(f):
        block.gpsimd(lambda g: f(Chained(g, sems, "chg")))

    class BB:
        vector = staticmethod(vec)
        scalar = staticmethod(sca)
        gpsimd = staticmethod(gps)
        tensor = staticmethod(block.tensor)
        sync = staticmethod(block.sync)
    return BB



def emit_ln(nc, block, sems, t, x_tiles, out_bf, ntok, xs0=0,
            gcol=None, bcol=None, out_f32=None, tag="ln", ld_edge=None,
            wr_edges=(), bc_edges=(), ve_edges=()):
    ngrp = ntok // 512
    sq2, rowtmp, lntmp = t["sq2"], t["rowtmp"], t["lntmp"]
    murow, vrow = t["murow"], t["vrow"]
    ps_sum, ps_sq = t["ps_sum"], t["ps_sq"]
    pbc0, pbc1 = t["ps_bc0"][:, :], t["ps_bc1"][:, :]
    SQ, MM, STA, STV, BC, NM = (tag + s for s in
                                ("sq", "mm", "sta", "stv", "bc", "nm"))
    sq0, mm0, sta0, stv0, bc0, nm0 = (sems.val(s) for s in
                                      (SQ, MM, STA, STV, BC, NM))
    inv_c = 1.0 / C

    cstv = sems.val("cst")

    def f_ac(s):
        sems.wait(s, "cst", cstv)
        if ld_edge is not None:
            sems.wait(s, *ld_edge)
        for g in range(ngrp):
            sl = slice(xs0 + g * 512, xs0 + (g + 1) * 512)
            for chk in range(NCH):
                k = g * NCH + chk
                if k >= 2:
                    kk = k - 2
                    sems.wait(s, MM, mm0 + kk + 1)
                s.willinc(SQ)
                s.activation(sq2[k % 2][:, :], x_tiles[chk][:, sl], AF.Square)
            sems.wait(s, STV, stv0 + g + 1)
            s.activation(rowtmp[0:1, :], vrow[0:1, :], AF.Ln, bias=EPS)
            s.willinc(STA)
            s.activation(vrow[0:1, :], rowtmp[0:1, :], AF.Exp, scale=-0.5)
    block.scalar(f_ac)

    def f_mm(tt):
        for e in bc_edges:
            sems.wait(tt, *e)
        for g in range(ngrp):
            sl = slice(xs0 + g * 512, xs0 + (g + 1) * 512)
            if g >= 1:
                sems.wait(tt, STV, stv0 + g)
            for chk in range(NCH):
                k = g * NCH + chk
                sems.wait(tt, SQ, sq0 + k + 1)
                tt.matmul(ps_sum[:, :], t["ones_col"][:, 0:1],
                          x_tiles[chk][:, sl],
                          start=(chk == 0), stop=(chk == NCH - 1))
                sems.inc(tt.matmul(ps_sq[:, :], t["ones_col"][:, 0:1],
                                   sq2[k % 2][:, :],
                                   start=(chk == 0), stop=(chk == NCH - 1)), MM)
            sems.wait(tt, STA, sta0 + g + 1)
            if g >= 1:
                sems.wait(tt, NM, nm0 + g)
            tt.matmul(pbc0, t["ones_row"][:, :], murow[0:1, :],
                      start=True, stop=True)
            sems.inc(tt.matmul(pbc1, t["ones_row"][:, :],
                               vrow[0:1, :], start=True, stop=True), BC)
    block.tensor(f_mm)

    def f_ve(v):
        for e in wr_edges:
            sems.wait(v, *e)
        for e in ve_edges:
            sems.wait(v, *e)
        for g in range(ngrp):
            sl = slice(xs0 + g * 512, xs0 + (g + 1) * 512)
            so = slice(g * 512, (g + 1) * 512)
            sems.wait(v, MM, mm0 + g * NCH + NCH)
            v.tensor_scalar_mul(murow[0:1, :], ps_sum[:, :], inv_c)
            v.tensor_scalar_mul(vrow[0:1, :], ps_sq[:, :], inv_c)
            v.tensor_mul(rowtmp[0:1, :], murow[0:1, :], murow[0:1, :])
            v.willinc(STV)
            v.tensor_sub(vrow[0:1, :], vrow[0:1, :], rowtmp[0:1, :])
            sems.wait(v, BC, bc0 + g + 1)
            for chk in range(NCH):
                last_of_group = (chk == NCH - 1)
                v.tensor_sub(lntmp[:, :], x_tiles[chk][:, sl], pbc0)
                if gcol is None and out_f32 is None:
                    if last_of_group:
                        v.willinc(NM)
                    v.tensor_mul(out_bf[chk][:, so], lntmp[:, :], pbc1)
                elif gcol is None:
                    v.tensor_mul(out_f32[chk][:, so], lntmp[:, :], pbc1)
                    if last_of_group:
                        v.willinc(NM)
                    v.tensor_copy(out_bf[chk][:, so], out_f32[chk][:, so])
                else:
                    v.tensor_mul(lntmp[:, :], lntmp[:, :], pbc1)
                    if out_f32 is not None and out_bf is not None:
                        v.tensor_scalar(out_f32[chk][:, so], lntmp[:, :],
                                        gcol[:, chk:chk + 1],
                                        bcol[:, chk:chk + 1], Op.mult, Op.add)
                        if last_of_group:
                            v.willinc(NM)
                        v.tensor_copy(out_bf[chk][:, so], out_f32[chk][:, so])
                    elif out_bf is None:
                        if last_of_group:
                            v.willinc(NM)
                        v.tensor_scalar(out_f32[chk][:, so], lntmp[:, :],
                                        gcol[:, chk:chk + 1],
                                        bcol[:, chk:chk + 1], Op.mult, Op.add)
                    else:
                        if last_of_group:
                            v.willinc(NM)
                        v.tensor_scalar(out_bf[chk][:, so], lntmp[:, :],
                                        gcol[:, chk:chk + 1],
                                        bcol[:, chk:chk + 1],
                                        Op.mult, Op.add)
    block.vector(f_ve)
    return (NM, sems.val(NM))


def alloc_ln_scratch(nc, ctx, t):
    t["cvals"] = ctx.enter_context(nc.sbuf_tensor("cvals", [1, 3], F32))
    nc.const_aps.register(F32, EPS, t["cvals"][0:1, 0:1])
    nc.const_aps.register(F32, -0.5, t["cvals"][0:1, 1:2])
    nc.const_aps.register(F32, 0.125, t["cvals"][0:1, 2:3])
    t["sq2"] = [ctx.enter_context(nc.sbuf_tensor(f"sq{i}", [128, 512], F32))
                for i in range(2)]
    t["murow"] = ctx.enter_context(nc.sbuf_tensor("murow", [1, 512], F32))
    t["vrow"] = ctx.enter_context(nc.sbuf_tensor("vrow", [1, 512], F32))
    t["rowtmp"] = ctx.enter_context(nc.sbuf_tensor("rowtmp", [1, 512], F32))
    t["lntmp"] = ctx.enter_context(nc.sbuf_tensor("lntmp", [128, 512], F32))
    t["ones_col"] = ctx.enter_context(nc.sbuf_tensor("ones_col", [128, 1], F32))
    t["ones_row"] = ctx.enter_context(nc.sbuf_tensor("ones_row", [1, 128], F32))
    t["ones_row_bf"] = ctx.enter_context(
        nc.sbuf_tensor("ones_row_bf", [1, 128], BF16))
    t["psf"] = [ctx.enter_context(nc.psum_tensor(f"psf{i}", [128, 512], F32))
                for i in range(4)]
    t["ps_sum"] = t["psf"][0][0:1, :]
    t["ps_sq"] = t["psf"][1][0:1, :]
    t["ps_bc0"] = t["psf"][2]
    t["ps_bc1"] = t["psf"][3]


def emit_const_loads(block, sems, t, cones_d, env=None):
    if env is not None:
        if env.consts_loaded:
            return
        env.consts_loaded = True

    def f(sy):
        sems.dma(sy.dma_start(out=t["cvals"][:, :], in_=cones_d[1:2, 1:4]),
                 "cst")
        sems.dma(sy.dma_start(out=t["ones_col"][:, :], in_=cones_d[:, 0:1],
                              allow_slow_non_contiguous=True), "cst")
        sems.dma(sy.dma_start(out=t["ones_row"][:, :], in_=cones_d[0:1, :]),
                 "cst")
    block.sync(f)

    def g(gg):
        sems.dma(gg.dma_start(out=t["ones_row_bf"][:, :], in_=cones_d[0:1, :]),
                 "cstg")
    block.gpsimd(g)




class Env:

    def __init__(self, nc, ctx, sems, t, block, pp4):
        self.nc = nc
        self.ctx = ctx
        self.sems = sems
        self.t = t
        self.block = block
        self.pp4 = pp4
        self.dcache = {}
        self.nphase = 0
        self.pfx = ""
        self.consts_loaded = False

    def enter_phase(self):
        self.pfx = f"ph{self.nphase}_"
        self.nphase += 1
        self.sems.begin_phase()
        if self.nphase > 1:
            st0 = self.sems.val("st")
            for sec in (self.block.vector, self.block.scalar,
                        self.block.gpsimd, self.block.tensor,
                        self.block.sync):
                sec(lambda e, st0=st0: self.sems.wait(e, "st", st0))

    def D(self, name, shape, dt=F32, kind="ExternalInput"):
        if name not in self.dcache:
            self.dcache[name] = self.nc.dram_tensor(name, list(shape), dt,
                                                    kind=kind)
        return self.dcache[name]

    def Dap(self, name, shape, dt=F32, kind="ExternalInput"):
        return self.D(name, shape, dt, kind).ap()


def make_env(nc, ctx):
    sems = Sems(nc, ctx)
    t = {}
    alloc_ln_scratch(nc, ctx, t)
    pp4 = [ctx.enter_context(nc.psum_tensor(f"pp4_{i}", [128, 512], F32))
           for i in range(4)]
    block = chain_block(ctx.enter_context(nc.Block()), sems)
    return Env(nc, ctx, sems, t, block, pp4)


def _standalone(emit, host_fn, extra=None):
    nc = bass.Bass("TRN2", target_bir_lowering=False, debug=False)
    with ExitStack() as ctx:
        env = make_env(nc, ctx)
        x_in = env.Dap("x", [C, TOK], F32, "ExternalInput")
        x_out = env.Dap("xo", [C, TOK], F32, "ExternalOutput")
        emit(env, x_in, x_out)

        def f_fin(g, env=env):
            env.sems.wait(g, "st", env.sems.val("st"))
        env.block.gpsimd(f_fin)
    return nc


def emit_a0(env, x_d, xo_d):
    env.enter_phase()
    nc, sems, t, block, pp4 = env.nc, env.sems, env.t, env.block, env.pp4
    wq_d = env.Dap("wq", [C, C])
    wk_d = env.Dap("wk", [C, C])
    wv_d = env.Dap("wv", [C, C])
    wo_d = env.Dap("wo", [C, C])
    bq_d = env.Dap("bq", [128, NCH])
    bo_d = env.Dap("bo", [128, NCH])
    bkv_d = env.Dap("bkv", [2, C])
    csfm_d = env.Dap("csfm", [128, 2 * LC])
    cstok_d = env.Dap("cstok", [128, LC])
    cones_d = env.Dap("cones", [128, 128])

    kv_in = env.D("kv_in", [64, B * NHEAD * 64], F32, "Internal")
    kv_out = env.D("kv_out", [64, B * NHEAD * 64], F32, "Internal")
    q_scr = env.D("q_scr", [C, TOK], BF16, "Internal")
    ST0 = sems.val("st")

    with ExitStack() as ctx:
        def sb(name, shape, dt=F32):
            return ctx.enter_context(nc.sbuf_tensor(env.pfx + name, shape,
                                                    dt))

        xt = [sb(f"xt{i}", [128, LC]) for i in range(NCH)]
        lnx = [sb(f"lnx{i}", [128, LC], BF16) for i in range(NCH)]
        wq_s = sb("wq_s", [128, NCH * C], BF16)
        wk_s = sb("wk_s", [128, NCH * C], BF16)
        wv_s = sb("wv_s", [128, NCH * C], BF16)
        bq_s = sb("bq_s", [128, NCH])
        bo_s = sb("bo_s", [128, NCH])
        bk_s = sb("bk_s", [1, C], BF16)
        bv_s = sb("bv_s", [1, C], BF16)
        csfm_s = sb("csfm_s", [128, 2 * LC])
        cstok_s = sb("cstok_s", [128, LC])
        qf32 = sb("qf32", [128, 512])
        rt1 = sb("rt1", [64, 512])
        rt2 = sb("rt2", [64, 512])
        qst = sb("qst", [128, LC], BF16)
        kf32 = sb("kf32", [128, C])
        st1 = sb("st1", [128, 512])
        st2 = sb("st2", [128, 512])
        kbf = sb("kbf", [128, C], BF16)
        vbf = sb("vbf", [128, C], BF16)
        kvsb = sb("kvsb", [64, B * NHEAD * 64])
        kv2 = sb("kv2", [128, B * NCH * 128], BF16)
        qld = [sb(f"qld{i}", [128, LC], BF16) for i in range(NCH)]
        ctxs = [sb(f"ctx{i}", [128, LC], BF16) for i in range(NCH)]

        pk = [t["ps_bc0"][:, :], t["ps_bc1"][:, :]]
        pv = [pp4[0][:, :], pp4[1][:, :]]
        pkv = [pp4[2][0:64, :], pp4[3][0:64, :]]

        emit_const_loads(block, sems, t, cones_d, env)

        def f_ld0(sy):
            sems.wait(sy, "st", ST0)
            sems.dma(sy.dma_start(out=bq_s[:, :], in_=bq_d), "cst")
            sems.dma(sy.dma_start(out=bo_s[:, :], in_=bo_d), "cst")
            sems.dma(sy.dma_start(out=csfm_s[:, :], in_=csfm_d), "cst")
            sems.dma(sy.dma_start(out=cstok_s[:, :], in_=cstok_d), "cst")
        block.sync(f_ld0)

        def f_wd0(g):
            sems.wait(g, "st", ST0)
            for w_d, w_s in ((wq_d, wq_s), (wk_d, wk_s), (wv_d, wv_s)):
                sems.dma(g.dma_start(
                    out=w_s[:, :].rearrange("p (ch i) -> p ch i", i=C),
                    in_=w_d.rearrange("(ch p) i -> p ch i", p=128)), "wgt")
            sems.dma(g.dma_start(out=bk_s[:, :], in_=bkv_d[0:1, :]), "wgt")
            sems.dma(g.dma_start(out=bv_s[:, :], in_=bkv_d[1:2, :]), "wgt")
        block.gpsimd(f_wd0)
        WGT = sems.val("wgt")
        CST = sems.val("cst")

        for q in range(B):
            qsl = slice(q * LC, (q + 1) * LC)

            prev_nm = sems.val("lnnm")

            def f_ldx(sy, q=q, qsl=qsl, prev_nm=prev_nm):
                if q >= 1:
                    sems.wait(sy, "lnnm", prev_nm)
                else:
                    sems.wait(sy, "st", ST0)
                for chk in range(NCH):
                    sems.dma(sy.dma_start(
                        out=xt[chk][:, :],
                        in_=x_d[128 * chk:128 * (chk + 1), qsl]), "xld")
            block.sync(f_ldx)

            wr = [("qpsum", sems.val("qpsum")), ("kpsum", sems.val("kpsum"))]
            bc = [("kevac", sems.val("kevac"))]
            nm_edge = emit_ln(nc, block, sems, t, xt, lnx, LC, tag="ln",
                              ld_edge=("xld", sems.val("xld")), wr_edges=wr,
                              bc_edges=bc)

            QP, QE = "qpsum", "qevac"
            qp0, qe0 = sems.val(QP), sems.val(QE)
            qs0 = sems.val("qst")

            def f_q(tt, nm_edge=nm_edge, qp0=qp0, qe0=qe0,
                    ke_prev=sems.val("kevac")):
                sems.wait(tt, "wgt", WGT)
                sems.wait(tt, "cst", CST)
                sems.wait(tt, "kevac", ke_prev)
                sems.wait(tt, *nm_edge)
                for co in range(NCH):
                    if co >= 1:
                        sems.wait(tt, QE, qe0 + 2 * co)
                    for cc in range(NCH):
                        w = wq_s[:, cc * C + co * 128:cc * C + co * 128 + 128]
                        for tc in range(2):
                            ins = tt.matmul(pv[tc], w,
                                            lnx[cc][:, tc * 512:(tc + 1) * 512],
                                            start=(cc == 0),
                                            stop=(cc == NCH - 1))
                    sems.inc(ins, QP)
            block.tensor(f_q)

            def f_qev(v, qp0=qp0, qs0=qs0):
                for co in range(NCH):
                    sems.wait(v, QP, qp0 + co + 1)
                    sems.wait(v, "qst", qs0 + 16 * co)
                    for tc in range(2):
                        tsl = slice(tc * 512, (tc + 1) * 512)
                        v.tensor_scalar(qf32[:, :], pv[tc],
                                        bq_s[:, co:co + 1], None, Op.add)
                        cosf = csfm_s[0:64, tsl]
                        sinf_hi = csfm_s[64:128, tsl]
                        sinf_lo = csfm_s[0:64, LC + tc * 512:LC + (tc + 1) * 512]
                        cosf_hi = csfm_s[64:128, LC + tc * 512:LC + (tc + 1) * 512]
                        qr, qi = qf32[0:64, :], qf32[64:128, :]
                        v.tensor_mul(rt1[:, :], qr, cosf)
                        v.tensor_mul(rt2[:, :], qi, sinf_hi)
                        v.tensor_sub(qst[0:64, tsl], rt1[:, :], rt2[:, :])
                        v.tensor_mul(rt1[:, :], qr, sinf_lo)
                        v.tensor_mul(rt2[:, :], qi, cosf_hi)
                        v.willinc(QE)
                        v.tensor_add(qst[64:128, tsl], rt1[:, :], rt2[:, :])
            block.vector(f_qev)

            def f_qst(sy, q=q, qe0=qe0):
                for co in range(NCH):
                    sems.wait(sy, "qevac", qe0 + 2 * (co + 1))
                    sems.dma(sy.dma_start(
                        out=q_scr.ap()[128 * co:128 * (co + 1), qsl],
                        in_=qst[:, :]), "qst")
            block.sync(f_qst)

            KP, KE, KV, KVE = "kpsum", "kevac", "kvgrp", "kvevac"
            kp0, ke0, kv0, kve0 = (sems.val(s) for s in (KP, KE, KV, KVE))
            QE_NOW = sems.val(QE)

            def f_kv_mm(tt, q=q, kp0=kp0, ke0=ke0, kv0=kv0, nm_edge=nm_edge):
                sems.wait(tt, "cstg", sems.val("cstg"))
                sems.wait(tt, *nm_edge)
                sems.wait(tt, QE, QE_NOW)
                if q >= 1:
                    sems.wait(tt, KVE, q)
                for ti in range(8):
                    if ti >= 1:
                        sems.wait(tt, KE, ke0 + 2 * ti)
                    for cc in range(NCH):
                        lhs = lnx[cc][:, ti * 128:(ti + 1) * 128]
                        tt.matmul(pk[0], lhs, wk_s[:, cc * C:cc * C + 512],
                                  start=(cc == 0), stop=False)
                        tt.matmul(pk[1], lhs,
                                  wk_s[:, cc * C + 512:cc * C + 1024],
                                  start=(cc == 0), stop=False)
                        tt.matmul(pv[0], lhs, wv_s[:, cc * C:cc * C + 512],
                                  start=(cc == 0), stop=False)
                        tt.matmul(pv[1], lhs,
                                  wv_s[:, cc * C + 512:cc * C + 1024],
                                  start=(cc == 0), stop=False)
                    tt.matmul(pk[0], t["ones_row_bf"][:, :],
                              bk_s[0:1, 0:512], start=False, stop=True)
                    tt.matmul(pk[1], t["ones_row_bf"][:, :],
                              bk_s[0:1, 512:1024], start=False, stop=True)
                    tt.matmul(pv[0], t["ones_row_bf"][:, :],
                              bv_s[0:1, 0:512], start=False, stop=True)
                    sems.inc(tt.matmul(pv[1], t["ones_row_bf"][:, :],
                                       bv_s[0:1, 512:1024], start=False,
                                       stop=True), KP)
                    sems.wait(tt, KE, ke0 + 2 * ti + 2)
                    for h in range(NHEAD):
                        chk, hh = h // 2, h % 2
                        base = 128 * chk + 64 * hh
                        kap = kbf[:, base:base + 64]
                        vap = vbf[:, base:base + 64]
                        ins = tt.matmul(
                            pkv[h // 8][:, (h % 8) * 64:(h % 8) * 64 + 64],
                            kap, vap, start=(ti == 0 and h % 8 == 0),
                            stop=(ti == 7 and h % 8 == 7),
                            skip_group_check=True)
                    sems.inc(ins, KV)
            block.tensor(f_kv_mm)

            def f_kv_ev(v, q=q, kp0=kp0, kv0=kv0):
                for ti in range(8):
                    sems.wait(v, KP, kp0 + ti + 1)
                    if ti >= 1:
                        sems.wait(v, KV, kv0 + ti)
                    cosv = cstok_s[:, ti * 128:ti * 128 + 32] \
                        .unsqueeze(1).unsqueeze(1) \
                        .broadcast_to([128, NCH, 2, 32])
                    sinv = cstok_s[:, ti * 128 + 64:ti * 128 + 96] \
                        .unsqueeze(1).unsqueeze(1) \
                        .broadcast_to([128, NCH, 2, 32])
                    v.tensor_copy(kf32[:, 0:512], pk[0])
                    v.tensor_copy(kf32[:, 512:1024], pk[1])
                    kr = kf32[:, :].rearrange("p (ch h j) -> p ch h j",
                                              h=2, j=32)
                    krv = kf32[:, :].rearrange("p (ch a) -> p ch a", a=128)[
                        :, :, 0:64].rearrange("p ch (h j) -> p ch h j", j=32)
                    kiv = kf32[:, :].rearrange("p (ch a) -> p ch a", a=128)[
                        :, :, 64:128].rearrange("p ch (h j) -> p ch h j", j=32)
                    kbb = kbf[:, :].rearrange("p (ch h a) -> p ch h a",
                                              h=2, a=64)
                    kbr_o = kbb[:, :, :, 0:32]
                    kbi_o = kbb[:, :, :, 32:64]
                    s1 = st1[:, :].rearrange("p (ch h j) -> p ch h j",
                                             h=2, j=32)
                    s2 = st2[:, :].rearrange("p (ch h j) -> p ch h j",
                                             h=2, j=32)
                    v.tensor_mul(s1, krv, cosv)
                    v.tensor_mul(s2, kiv, sinv)
                    v.tensor_sub(kbr_o, s1, s2)
                    v.tensor_mul(s1, krv, sinv)
                    v.tensor_mul(s2, kiv, cosv)
                    v.willinc(KE)
                    v.tensor_add(kbi_o, s1, s2)
                    v.tensor_copy(vbf[:, 0:512], pv[0])
                    v.willinc(KE)
                    v.tensor_copy(vbf[:, 512:1024], pv[1])
                    if ti == 7:
                        sems.wait(v, KV, kv0 + 8)
                        v.tensor_scalar_mul(
                            kvsb[:, q * 1024:q * 1024 + 512],
                            pkv[0][:, :], 0.125)
                        v.willinc(KVE)
                        v.tensor_scalar_mul(
                            kvsb[:, q * 1024 + 512:(q + 1) * 1024],
                            pkv[1][:, :], 0.125)
            block.vector(f_kv_ev)

        KVE_ALL = sems.val("kvevac")
        KP_ALL = sems.val("kpsum")
        QP_ALL = sems.val("qpsum")

        def f_cc(g):
            sems.wait(g, "kvevac", KVE_ALL)
            sems.dma(g.dma_start(out=kv_in.ap(), in_=kvsb[:, :]), "ccd")
            g.wait_ge(sems.handle("ccd"), sems.val("ccd"))
            sems.inc(g.collective_compute(
                "AllReduce", mybir.AluOpType.add,
                replica_groups=[list(range(NCORES))],
                ins=[kv_in.ap().opt()], outs=[kv_out.ap().opt()]), "cc")
            g.wait_ge(sems.handle("cc"), sems.val("cc"))
            sems.dma(g.dma_start(out=kvsb[:, :], in_=kv_out.ap()), "ccd")
            sems.wait(g, "kpsum", KP_ALL)
            sems.wait(g, "qpsum", QP_ALL)
            sems.dma(g.dma_start(
                out=wq_s[:, :].rearrange("p (ch i) -> p ch i", i=C),
                in_=wo_d.rearrange("(ch p) i -> p ch i", p=128)), "wgt2")
        block.gpsimd(f_cc)
        CCD = sems.val("ccd")
        WGT2 = sems.val("wgt2")

        def f_kv2(v):
            sems.wait(v, "ccd", CCD)
            v.memset(kv2[:, :], 0.0)
            for bb in range(B):
                for chk in range(NCH):
                    for hh in range(2):
                        h = 2 * chk + hh
                        for ri in range(2):
                            for rj in range(2):
                                src = kvsb[32 * ri:32 * ri + 32,
                                           bb * 1024 + h * 64 + 32 * rj:
                                           bb * 1024 + h * 64 + 32 * rj + 32]
                                dst = kv2[64 * ri + 32 * hh:
                                          64 * ri + 32 * hh + 32,
                                          (bb * NCH + chk) * 128 + 64 * hh
                                          + 32 * rj:
                                          (bb * NCH + chk) * 128 + 64 * hh
                                          + 32 * rj + 32]
                                last = (bb == B - 1 and chk == NCH - 1
                                        and hh == 1 and ri == 1 and rj == 1)
                                if last:
                                    v.willinc("kv2e")
                                v.tensor_copy(dst, src)
        block.vector(f_kv2)
        KV2E = sems.val("kv2e")
        QST_ALL = sems.val("qst")

        for q in range(B):
            qsl = slice(q * LC, (q + 1) * LC)
            CP, CE, OP_, XE = "cpsum", "cevac", "opsum", "xevac"
            cp0, ce0, op0, xe0 = (sems.val(s) for s in (CP, CE, OP_, XE))

            def f_ldq(sy, q=q, qsl=qsl, xe_prev=sems.val("xevac"),
                      st_prev=sems.val("st"), nm_all=sems.val("lnnm")):
                if q == 0:
                    sems.wait(sy, "lnnm", nm_all)
                else:
                    sems.wait(sy, "st", st_prev)
                for chk in range(NCH):
                    sems.dma(sy.dma_start(
                        out=xt[chk][:, :],
                        in_=x_d[128 * chk:128 * (chk + 1), qsl]), "xld2")
                    sems.dma(sy.dma_start(
                        out=qld[chk][:, :],
                        in_=q_scr.ap()[128 * chk:128 * (chk + 1), qsl]),
                        "xld2")
            block.sync(f_ldq)
            XLD2 = sems.val("xld2")

            def f_ctx(tt, q=q, cp0=cp0, ce0=ce0,
                      xe_prev=sems.val("xevac")):
                sems.wait(tt, "kv2e", KV2E)
                sems.wait(tt, "qst", QST_ALL)
                sems.wait(tt, "xld2", XLD2)
                if q >= 1:
                    sems.wait(tt, "opsum", sems.val("opsum"))
                    sems.wait(tt, "xevac", xe_prev)
                for chk in range(NCH):
                    if chk >= 2:
                        sems.wait(tt, CE, ce0 + chk - 1)
                    w = kv2[:, (q * NCH + chk) * 128:(q * NCH + chk) * 128
                            + 128]
                    for tc in range(2):
                        ins = tt.matmul(pv[tc] if chk % 2 == 0 else pk[tc], w,
                                        qld[chk][:, tc * 512:(tc + 1) * 512],
                                        start=True, stop=True)
                    sems.inc(ins, CP)
            block.tensor(f_ctx)

            def f_ctx_ev(v, cp0=cp0):
                for chk in range(NCH):
                    sems.wait(v, CP, cp0 + chk + 1)
                    for tc in range(2):
                        so = slice(tc * 512, (tc + 1) * 512)
                        if tc == 1:
                            v.willinc(CE)
                        v.tensor_copy(ctxs[chk][:, so],
                                      pv[tc] if chk % 2 == 0 else pk[tc])
            block.vector(f_ctx_ev)
            CE_ALL = sems.val("cevac")

            def f_oproj(tt, q=q, op0=op0, xe0=xe0):
                sems.wait(tt, "cevac", CE_ALL)
                sems.wait(tt, "wgt2", WGT2)
                for co in range(NCH):
                    if co >= 2:
                        sems.wait(tt, "xevac", xe0 + co - 1)
                    for cc in range(NCH):
                        w = wq_s[:, cc * C + co * 128:cc * C + co * 128 + 128]
                        for tc in range(2):
                            ins = tt.matmul(
                                pv[tc] if co % 2 == 0 else pk[tc], w,
                                ctxs[cc][:, tc * 512:(tc + 1) * 512],
                                start=(cc == 0), stop=(cc == NCH - 1))
                    sems.inc(ins, OP_)
            block.tensor(f_oproj)

            def f_resid(v, op0=op0):
                for co in range(NCH):
                    sems.wait(v, OP_, op0 + co + 1)
                    for tc in range(2):
                        if tc == 1:
                            v.willinc("xevac")
                        v.scalar_tensor_tensor(
                            xt[co][:, tc * 512:(tc + 1) * 512],
                            pv[tc] if co % 2 == 0 else pk[tc],
                            bo_s[:, co:co + 1],
                            xt[co][:, tc * 512:(tc + 1) * 512],
                            Op.add, Op.add)
            block.vector(f_resid)

            def f_store(sy, qsl=qsl, xe0=xe0):
                for co in range(NCH):
                    sems.wait(sy, "xevac", xe0 + co + 1)
                    sems.dma(sy.dma_start(
                        out=xo_d[128 * co:128 * (co + 1), qsl],
                        in_=xt[co][:, :]), "st")
            block.sync(f_store)


def build_a0():
    return _standalone(lambda env, xi, xo: emit_a0(env, xi, xo), None)



_prog_cache = {}


def _get_prog(name, builder):
    if name not in _prog_cache:
        _prog_cache[name] = builder()
    return _prog_cache[name]


def run_a0(x_parts, inputs):
    nc = _get_prog("a0", build_a0)
    in_maps = []
    for k in range(NCORES):
        d = a0_host_inputs(inputs, k)
        d["x"] = x_parts[k]
        in_maps.append(d)
    res = run_bass_kernel_spmd(nc, in_maps, core_ids=list(range(NCORES)))
    return [res.results[k]["xo"] for k in range(NCORES)]


def build_fused():
    nc = bass.Bass("TRN2", target_bir_lowering=False, debug=False)
    with ExitStack() as ctx:
        env = make_env(nc, ctx)
        x_in = env.Dap("x", [C, TOK], F32, "ExternalInput")
        x_out = env.Dap("xo", [C, TOK], F32, "ExternalOutput")
        xs = [env.Dap(f"xs{i}", [C, TOK], F32, "Internal") for i in range(5)]
        seq = [x_in] + xs + [x_out]
        emit_f0(env, seq[0], seq[1])
        emit_a0(env, seq[1], seq[2])
        emit_f0(env, seq[2], seq[3])
        emit_f1(env, seq[3], seq[4])
        emit_a1(env, seq[4], seq[5])
        emit_f1(env, seq[5], seq[6])

        def f_fin(g):
            env.sems.wait(g, "st", env.sems.val("st"))
        env.block.gpsimd(f_fin)
    return nc


def kernel(**inputs):
    x = np.asarray(inputs["L0_attn_x"], np.float32)
    parts = shard_x(x)
    nc = _get_prog("fused", build_fused)
    in_maps = []
    for k in range(NCORES):
        d = {}
        d.update(f0_host_inputs(inputs, k))
        d.update(a0_host_inputs(inputs, k))
        d.update(f1_host_inputs(inputs, k))
        d.update(a1_host_inputs(inputs, k))
        d["x"] = parts[k]
        in_maps.append(d)
    res = run_bass_kernel_spmd(nc, in_maps, core_ids=list(range(NCORES)))
    return unshard_x([res.results[k]["xo"]
                      for k in range(NCORES)]).astype(np.float32)


def kernel_unfused(**inputs):
    x = np.asarray(inputs["L0_attn_x"], np.float32)
    parts = shard_x(x)
    parts = run_f0(parts, inputs)
    parts = run_a0(parts, inputs)
    parts = run_f0(parts, inputs)
    parts = run_f1(parts, inputs)
    parts = run_a1(parts, inputs)
    parts = run_f1(parts, inputs)
    return unshard_x(parts).astype(np.float32)


EXH = 2816
SHH = 1536
NEX = 8


def f1_host_inputs(inputs, core):
    cones = np.ones((128, 128), np.float32)
    cones[1, 1] = EPS
    cones[1, 2] = -0.5
    return {
        "gate_wt": np.ascontiguousarray(np.asarray(inputs["L1_ffn_gate_w"]).T),
        "w1": np.ascontiguousarray(inputs["L1_ffn_w1"]),
        "w2": np.ascontiguousarray(inputs["L1_ffn_w2"]),
        "w3": np.ascontiguousarray(inputs["L1_ffn_w3"]),
        "sw1": np.ascontiguousarray(inputs["L1_ffn_sw1"]),
        "sw2": np.ascontiguousarray(inputs["L1_ffn_sw2"]),
        "sw3": np.ascontiguousarray(inputs["L1_ffn_sw3"]),
        "cones": cones,
    }


def emit_f1(env, x_d, xo_d):
    env.enter_phase()
    nc, sems, t, block, pp4 = env.nc, env.sems, env.t, env.block, env.pp4
    gwt_d = env.Dap("gate_wt", [C, NEX])
    w1_d = env.Dap("w1", [NEX, C, EXH])
    w2_d = env.Dap("w2", [NEX, EXH, C])
    w3_d = env.Dap("w3", [NEX, C, EXH])
    sw1_d = env.Dap("sw1", [C, SHH])
    sw2_d = env.Dap("sw2", [SHH, C])
    sw3_d = env.Dap("sw3", [C, SHH])
    cones_d = env.Dap("cones", [128, 128])

    QT = 1024
    NHG = EXH // 128
    NSG = SHH // 128
    ST0 = sems.val("st")

    with ExitStack() as ctx:
        def sb(name, shape, dt=F32):
            return ctx.enter_context(nc.sbuf_tensor(env.pfx + name, shape,
                                                    dt))

        xt = [sb(f"xt{i}", [128, QT]) for i in range(NCH)]
        lny = [sb(f"lny{i}", [128, QT], BF16) for i in range(NCH)]
        yac = [sb(f"yac{i}", [128, QT]) for i in range(NCH)]
        lnf = yac
        he = [sb(f"he{i}", [128, QT], BF16) for i in range(NHG)]
        gwt_s = sb("gwt_s", [128, NCH * NEX])
        lgt = sb("lgt", [32, QT])
        srow = sb("srow", [32, QT])
        prb = sb("prb", [32, QT])
        combm = sb("combm", [32, QT])
        combr_bf = sb("combr_bf", [1, QT], BF16)
        iocol = sb("iocol", [32, 1])
        wt1 = [sb(f"wt1_{i}", [128, NCH * 128], BF16) for i in range(3)]
        wt3 = [sb(f"wt3_{i}", [128, NCH * 128], BF16) for i in range(3)]
        wt2 = [sb(f"wt2_{i}", [128, NHG * 128], BF16) for i in range(2)]
        wt2f = [sb(f"wt2f_{i}", [128, NHG * 128]) for i in range(2)]
        s1t = [sb(f"s1t{i}", [128, 512]) for i in range(2)]
        s2t = t["lntmp"]

        z1p = [t["ps_bc0"][:, :], t["ps_bc1"][:, :]]
        z3p = [pp4[0][:, :], pp4[1][:, :]]
        z3o = [t["psf"][0][:, :], t["psf"][1][:, :]]
        cbp = [pp4[2][:, :], pp4[3][:, :]]

        emit_const_loads(block, sems, t, cones_d, env)
        CSTG = sems.val("cstg")

        def f_ld0(g):
            sems.wait(g, "st", ST0)
            sems.dma(g.dma_start(
                out=gwt_s[:, :].rearrange("p (ch i) -> p ch i", i=NEX),
                in_=gwt_d.rearrange("(ch p) i -> p ch i", p=128)), "wgt")

            g.willinc("iot")
            g.iota(iocol[:, :], pattern=[[0, 1]], base=0, channel_multiplier=1,
                   allow_small_or_imprecise_dtypes=True)
        block.gpsimd(f_ld0)
        WGT = sems.val("wgt")
        IOT = sems.val("iot")

        experts = [(w1_d[e], w3_d[e], w2_d[e], NHG, e) for e in range(NEX)]
        experts.append((sw1_d, sw3_d, sw2_d, NSG, None))

        for q in range(B):
            qsl = slice(q * QT, (q + 1) * QT)

            def f_ldx(sy, q=q, qsl=qsl, xe_prev=sems.val("xev")):
                sems.wait(sy, "st", ST0)
                sems.wait(sy, "xev", xe_prev)
                for chk in range(NCH):
                    sems.dma(sy.dma_start(
                        out=xt[chk][:, :],
                        in_=x_d[128 * chk:128 * (chk + 1), qsl]), "xld")
            block.sync(f_ldx)

            wr = [("zmm", sems.val("zmm")), ("gmm", sems.val("gmm")),
                  ("hev", sems.val("hev"))]
            nm_edge = emit_ln(nc, block, sems, t, xt, lny, QT, tag="ln",
                              ld_edge=("xld", sems.val("xld")), wr_edges=wr,
                              bc_edges=[("hev", sems.val("hev")),
                                        ("yev", sems.val("yev"))],
                              ve_edges=[("st", sems.val("st"))],
                              out_f32=lnf)

            GM = "gmm"
            gm0 = sems.val(GM)

            def f_gate_mm(tt, nm_edge=nm_edge, hev_prev=sems.val("hev")):
                sems.wait(tt, "wgt", WGT)
                sems.wait(tt, *nm_edge)
                sems.wait(tt, "hev", hev_prev)
                for tc in range(2):
                    for cc in range(NCH):
                        ins = tt.matmul(
                            z3p[tc][0:8, :],
                            gwt_s[:, cc * NEX:(cc + 1) * NEX],
                            lnf[cc][:, tc * 512:(tc + 1) * 512],
                            start=(cc == 0), stop=(cc == NCH - 1))
                    sems.inc(ins, GM)
            block.tensor(f_gate_mm)

            def f_gate_a(v, gm0=gm0):
                sems.wait(v, "iot", IOT)
                v.memset(lgt[:, :], 0.0)
                for tc in range(2):
                    sems.wait(v, GM, gm0 + tc + 1)
                    v.tensor_copy(lgt[0:8, tc * 512:(tc + 1) * 512],
                                  z3p[tc][0:8, :])
                v.stream_shuffle(srow[:, :], lgt[:, :],
                                 mask=[i ^ 4 for i in range(32)])
                v.tensor_max(prb[:, :], lgt[:, :], srow[:, :])
                v.stream_shuffle(srow[:, :], prb[:, :],
                                 mask=[i ^ 2 for i in range(32)])
                v.tensor_max(prb[:, :], prb[:, :], srow[:, :])
                v.stream_shuffle(srow[:, :], prb[:, :],
                                 mask=[i ^ 1 for i in range(32)])
                v.tensor_max(prb[:, :], prb[:, :], srow[:, :])
                v.willinc("gsub")
                v.tensor_sub(srow[:, :], lgt[:, :], prb[:, :])
            block.vector(f_gate_a)
            GSUB = sems.val("gsub")

            def f_gate_e(sc):
                sems.wait(sc, "gsub", GSUB)
                sc.willinc("gexp")
                sc.activation(combm[:, :], srow[:, :], AF.Exp)
            block.scalar(f_gate_e)
            GEXP = sems.val("gexp")

            def f_gate_b(v):
                sems.wait(v, "gexp", GEXP)
                v.stream_shuffle(srow[:, :], combm[:, :],
                                 mask=[i ^ 4 for i in range(32)])
                v.tensor_add(prb[:, :], combm[:, :], srow[:, :])
                v.stream_shuffle(srow[:, :], prb[:, :],
                                 mask=[i ^ 2 for i in range(32)])
                v.tensor_add(prb[:, :], prb[:, :], srow[:, :])
                v.stream_shuffle(srow[:, :], prb[:, :],
                                 mask=[i ^ 1 for i in range(32)])
                v.tensor_add(prb[:, :], prb[:, :], srow[:, :])
                v.reciprocal(srow[:, :], prb[:, :])
                v.tensor_mul(combm[:, :], combm[:, :], srow[:, :])
                v.stream_shuffle(srow[:, :], combm[:, :],
                                 mask=[i ^ 4 for i in range(32)])
                v.tensor_max(prb[:, :], combm[:, :], srow[:, :])
                v.stream_shuffle(srow[:, :], prb[:, :],
                                 mask=[i ^ 2 for i in range(32)])
                v.tensor_max(prb[:, :], prb[:, :], srow[:, :])
                v.stream_shuffle(srow[:, :], prb[:, :],
                                 mask=[i ^ 1 for i in range(32)])
                v.tensor_max(prb[:, :], prb[:, :], srow[:, :])
                v.tensor_tensor(srow[:, :], combm[:, :], prb[:, :],
                                Op.is_equal)
                v.scalar_tensor_tensor(srow[:, :], srow[:, :], -1e9,
                                       combm[:, :], Op.mult, Op.add)
                v.stream_shuffle(lgt[:, :], srow[:, :],
                                 mask=[i ^ 4 for i in range(32)])
                v.tensor_max(prb[:, :], srow[:, :], lgt[:, :])
                v.stream_shuffle(lgt[:, :], prb[:, :],
                                 mask=[i ^ 2 for i in range(32)])
                v.tensor_max(prb[:, :], prb[:, :], lgt[:, :])
                v.stream_shuffle(lgt[:, :], prb[:, :],
                                 mask=[i ^ 1 for i in range(32)])
                v.tensor_max(prb[:, :], prb[:, :], lgt[:, :])
                v.tensor_tensor(srow[:, :], combm[:, :], prb[:, :], Op.is_ge)
                v.willinc("comb")
                v.tensor_mul(combm[:, :], combm[:, :], srow[:, :])
            block.vector(f_gate_b)
            COMB = sems.val("comb")

            for ei, (w1s, w3s, w2s, nhg, emask) in enumerate(experts):
                zm0 = sems.val("zmm")
                sl0 = sems.val("slu")
                he0 = sems.val("hev")
                wm0 = sems.val("wmm")
                ye0 = sems.val("yev")
                cb0 = sems.val("cbm")
                cr0 = sems.val("crb")

                wv = []

                def f_wda(g, w1s=w1s, w3s=w3s, nhg=nhg, zm0=zm0, wv=wv):
                    sems.wait(g, "st", ST0)
                    for hg in range(nhg):
                        sems.wait(g, "zmm", max(0, zm0 + hg - 2))
                        slot = (zm0 + hg) % 3
                        wd = "wda" + str(slot)
                        sems.dma(g.dma_start(
                            out=wt1[slot][:, :].rearrange(
                                "p (ch i) -> p ch i", i=128),
                            in_=w1s[:, 128 * hg:128 * (hg + 1)].rearrange(
                                "(ch p) i -> p ch i", p=128)), wd)
                        v = sems.dma(g.dma_start(
                            out=wt3[slot][:, :].rearrange(
                                "p (ch i) -> p ch i", i=128),
                            in_=w3s[:, 128 * hg:128 * (hg + 1)].rearrange(
                                "(ch p) i -> p ch i", p=128)), wd)
                        wv.append((slot, v))
                block.gpsimd(f_wda)

                wc0 = sems.val("wcst")
                w2v = []

                def f_w2dma(sy, w2s=w2s, nhg=nhg, wc0=wc0, w2v=w2v):
                    sems.wait(sy, "st", ST0)
                    for co in range(NCH):
                        sems.wait(sy, "wcst", max(0, wc0 + co - 1))
                        fs = (wc0 + co) % 2
                        v = sems.dma(sy.dma_start(
                            out=wt2f[fs][:, 0:nhg * 128].rearrange(
                                "p (ch i) -> p ch i", i=128),
                            in_=w2s[:, 128 * co:128 * (co + 1)].rearrange(
                                "(ch p) i -> p ch i", p=128)), "w2d" + str(fs))
                        w2v.append((fs, v))
                block.sync(f_w2dma)

                if emask is not None:
                    def f_crb(v, emask=emask, cbm_now=sems.val("cbm")):
                        sems.wait(v, "comb", COMB)
                        sems.wait(v, "cbm", cbm_now)
                        msk = [(i + emask) % 8 if i < 8 else i
                               for i in range(32)]
                        v.stream_shuffle(lgt[:, :], combm[:, :], mask=msk)
                        v.willinc("crb")
                        v.tensor_copy(combr_bf[0:1, :], lgt[0:1, :])
                    block.vector(f_crb)

                def f_pe(tt, ei=ei, nhg=nhg, emask=emask, zm0=zm0, he0=he0,
                         wm0=wm0, ye0=ye0, wv=wv, wc0=wc0, sl0=sl0,
                         crb_now=sems.val("crb")):
                    if emask is not None:
                        sems.wait(tt, "cstg", CSTG)
                        sems.wait(tt, "crb", crb_now)
                        sems.wait(tt, "yev", ye0)
                        for r in range(2):
                            ins = tt.matmul(cbp[r], t["ones_row_bf"][:, :],
                                            combr_bf[0:1,
                                                     r * 512:(r + 1) * 512],
                                            start=True, stop=True)
                        sems.inc(ins, "cbm")
                    for hg in range(nhg):
                        slot, wvv = wv[hg]
                        sems.wait(tt, "wda" + str(slot), wvv)
                        if hg == 0:
                            sems.wait(tt, "hev", he0)
                        elif hg >= 2:
                            sems.wait(tt, "hev", he0 + 2 * (hg - 1))
                        if hg >= 1:
                            sems.wait(tt, "slu", sl0 + 2 * hg)
                        zb = z3p if hg % 2 == 0 else z3o
                        for cc in range(NCH):
                            w = wt1[slot][:, cc * 128:(cc + 1) * 128]
                            for r in range(2):
                                tt.matmul(z1p[r], w,
                                          lny[cc][:, r * 512:(r + 1) * 512],
                                          start=(cc == 0),
                                          stop=(cc == NCH - 1))
                        for cc in range(NCH):
                            w = wt3[slot][:, cc * 128:(cc + 1) * 128]
                            for r in range(2):
                                ins = tt.matmul(
                                    zb[r], w,
                                    lny[cc][:, r * 512:(r + 1) * 512],
                                    start=(cc == 0), stop=(cc == NCH - 1))
                        sems.inc(ins, "zmm")
                    for co in range(NCH):
                        sems.wait(tt, "wcst", wc0 + co + 1)
                        if co == 0:
                            sems.wait(tt, "hev", he0 + 2 * nhg)
                            sems.wait(tt, "yev", ye0)
                        else:
                            sems.wait(tt, "yev", ye0 + co)
                        for ch in range(nhg):
                            w = wt2[co % 2][:, ch * 128:(ch + 1) * 128]
                            for r in range(2):
                                ins = tt.matmul(
                                    cbp[r], w,
                                    he[ch][:, r * 512:(r + 1) * 512],
                                    start=(ch == 0), stop=(ch == nhg - 1))
                        sems.inc(ins, "wmm")
                block.tensor(f_pe)

                def f_silu(sc, nhg=nhg, zm0=zm0, he0=he0, sl0=sl0):
                    for k in range(2 * nhg):
                        hg, r = k // 2, k % 2
                        sems.wait(sc, "zmm", zm0 + hg + 1)
                        if k >= 2:
                            sems.wait(sc, "hev", he0 + k - 1)
                        sc.willinc("slu")
                        sc.activation(s1t[k % 2][:, :], z1p[r], AF.Silu)
                    block_scalar_done = None
                block.scalar(f_silu)

                def f_w2cast(sc, nhg=nhg, wm0=wm0, w2v=w2v):
                    for co in range(NCH):
                        fs, v = w2v[co]
                        sems.wait(sc, "w2d" + str(fs), v)
                        sems.wait(sc, "wmm", wm0 + max(0, co - 1))
                        sc.willinc("wcst")
                        sc.activation(wt2[co % 2][:, 0:nhg * 128],
                                      wt2f[fs][:, 0:nhg * 128], AF.Copy)
                block.scalar(f_w2cast)

                def f_he(v, nhg=nhg, emask=emask, sl0=sl0, he0=he0, zm0=zm0,
                         cb_now=sems.val("cbm")):
                    if emask is not None:
                        sems.wait(v, "cbm", cb_now)
                    for k in range(2 * nhg):
                        hg, r = k // 2, k % 2
                        zb = z3p if hg % 2 == 0 else z3o
                        sems.wait(v, "slu", sl0 + k + 1)
                        if emask is not None:
                            v.tensor_mul(s2t[:, :], s1t[k % 2][:, :], zb[r])
                            v.willinc("hev")
                            v.tensor_mul(he[hg][:, r * 512:(r + 1) * 512],
                                         s2t[:, :], cbp[r])
                        else:
                            v.willinc("hev")
                            v.tensor_mul(he[hg][:, r * 512:(r + 1) * 512],
                                         s1t[k % 2][:, :], zb[r])
                block.vector(f_he)

                def f_yacc(v, ei=ei, q=q, wm0=wm0, ye0=ye0,
                           st_prev=sems.val("st")):
                    if ei == 0:
                        sems.wait(v, "st", st_prev)
                    for co in range(NCH):
                        sems.wait(v, "wmm", wm0 + co + 1)
                        for r in range(2):
                            so = slice(r * 512, (r + 1) * 512)
                            if r == 1:
                                v.willinc("yev")
                            if ei == 0:
                                v.tensor_copy(yac[co][:, so], cbp[r])
                            else:
                                v.tensor_add(yac[co][:, so], yac[co][:, so],
                                             cbp[r])
                block.vector(f_yacc)

            XE = "xev"
            xe0 = sems.val(XE)
            ye_all = sems.val("yev")

            def f_resid(v, q=q, xe0=xe0, ye_all=ye_all,
                        xld_now=sems.val("xld")):
                sems.wait(v, "yev", ye_all)
                sems.wait(v, "xld", xld_now)
                for co in range(NCH):
                    for r in range(2):
                        if r == 1:
                            v.willinc(XE)
                        so = slice(r * 512, (r + 1) * 512)
                        v.scalar_tensor_tensor(yac[co][:, so],
                                               yac[co][:, so],
                                               0.5, xt[co][:, so],
                                               Op.mult, Op.add)
            block.vector(f_resid)

            def f_store(sy, qsl=qsl, xe0=xe0):
                for co in range(NCH):
                    sems.wait(sy, XE, xe0 + co + 1)
                    sems.dma(sy.dma_start(
                        out=xo_d[128 * co:128 * (co + 1), qsl],
                        in_=yac[co][:, :]), "st")
            block.sync(f_store)


def build_f1():
    return _standalone(lambda env, xi, xo: emit_f1(env, xi, xo), None)


def run_f1(x_parts, inputs):
    nc = _get_prog("f1", build_f1)
    in_maps = []
    for k in range(NCORES):
        d = f1_host_inputs(inputs, k)
        d["x"] = x_parts[k]
        in_maps.append(d)
    res = run_bass_kernel_spmd(nc, in_maps, core_ids=list(range(NCORES)))
    return [res.results[k]["xo"] for k in range(NCORES)]



def f0_host_inputs(inputs, core):
    cones = np.ones((128, 128), np.float32)
    cones[1, 1] = EPS
    cones[1, 2] = -0.5
    cones[1, 3] = 0.125
    b1c = np.concatenate([col_layout(inputs["L0_ffn_b1"][e])
                          for e in range(8)], axis=1)
    return {
        "ipwt": np.ascontiguousarray(np.asarray(inputs["L0_ffn_in_proj_w"]).T),
        "ipb": np.ascontiguousarray(
            np.asarray(inputs["L0_ffn_in_proj_b"]).reshape(1, 3 * C)),
        "owt": np.ascontiguousarray(np.asarray(inputs["L0_ffn_out_w"]).T),
        "obc": col_layout(inputs["L0_ffn_out_b"]),
        "l1g": col_layout(inputs["L0_ffn_ln1_g"]),
        "l1b": col_layout(inputs["L0_ffn_ln1_b"]),
        "rwt": np.ascontiguousarray(np.asarray(inputs["L0_ffn_router_w"]).T),
        "rbc": np.ascontiguousarray(
            np.asarray(inputs["L0_ffn_router_b"]).reshape(8, 1)),
        "w1t": np.ascontiguousarray(
            np.asarray(inputs["L0_ffn_w1"]).transpose(0, 2, 1)),
        "b1c": np.ascontiguousarray(b1c),
        "w2t": np.ascontiguousarray(
            np.asarray(inputs["L0_ffn_w2"]).transpose(0, 2, 1)),
        "b2r": np.ascontiguousarray(inputs["L0_ffn_b2"]),
        "l2g": col_layout(inputs["L0_ffn_ln2_g"]),
        "l2b": col_layout(inputs["L0_ffn_ln2_b"]),
        "cones": cones,
        "ident": np.eye(128, dtype=np.float32),
    }


def emit_f0(env, x_d, xo_d, h_name="h_scr"):
    env.enter_phase()
    nc, sems, t, block, pp4 = env.nc, env.sems, env.t, env.block, env.pp4
    ipwt_d = env.Dap("ipwt", [C, 3 * C])
    ipb_d = env.Dap("ipb", [1, 3 * C])
    owt_d = env.Dap("owt", [C, C])
    obc_d = env.Dap("obc", [128, NCH])
    l1g_d = env.Dap("l1g", [128, NCH])
    l1b_d = env.Dap("l1b", [128, NCH])
    rwt_d = env.Dap("rwt", [C, 8])
    rbc_d = env.Dap("rbc", [8, 1])
    w1t_d = env.Dap("w1t", [8, C, C])
    b1c_d = env.Dap("b1c", [128, 64])
    w2t_d = env.Dap("w2t", [8, C, C])
    b2r_d = env.Dap("b2r", [8, C])
    l2g_d = env.Dap("l2g", [128, NCH])
    l2b_d = env.Dap("l2b", [128, NCH])
    cones_d = env.Dap("cones", [128, 128])
    ident_d = env.Dap("ident", [128, 128])
    h_scr = env.D(h_name, [C, TOK], F32, "Internal")

    QT = 1024
    BT = 512
    ST0 = sems.val("st")

    def blkap(dram, chk, blk, width=128):
        return dram[128 * chk:128 * (chk + 1), :].rearrange(
            "p (b l) -> p b l", b=B)[:, :, blk * width:(blk + 1) * width]

    with ExitStack() as ctx:
        bc0, bc1 = t["ps_bc0"][:, :], t["ps_bc1"][:, :]
        qkvp = [bc0, bc1, pp4[0][:, :], pp4[1][:, :], pp4[2][:, :],
                pp4[3][:, :]]
        tpp = [pp4[0][0:128, 0:128], pp4[1][0:128, 0:128]]
        opp = [pp4[2][:, :], pp4[3][:, :]]

        emit_const_loads(block, sems, t, cones_d, env)

        def sb(name, shape, dt=F32):
            return ctx.enter_context(nc.sbuf_tensor(env.pfx + name, shape,
                                                    dt))

        identf = sb("identf", [128, 128])
        obc_s = sb("obc_s", [128, NCH])
        l1g_s = sb("l1g_s", [128, NCH])
        l1b_s = sb("l1b_s", [128, NCH])
        l2g_s = sb("l2g_s", [128, NCH])
        l2b_s = sb("l2b_s", [128, NCH])

        def f_ld0(sy):
            sems.wait(sy, "st", ST0)
            sems.dma(sy.dma_start(out=identf[:, :], in_=ident_d), "cst")
            sems.dma(sy.dma_start(out=obc_s[:, :], in_=obc_d), "cst")
            sems.dma(sy.dma_start(out=l1g_s[:, :], in_=l1g_d), "cst")
            sems.dma(sy.dma_start(out=l1b_s[:, :], in_=l1b_d), "cst")
            sems.dma(sy.dma_start(out=l2g_s[:, :], in_=l2g_d), "cst")
            sems.dma(sy.dma_start(out=l2b_s[:, :], in_=l2b_d), "cst")
        block.sync(f_ld0)

        with ExitStack() as actx:
            def sba(name, shape, dt=F32):
                return actx.enter_context(
                    nc.sbuf_tensor(env.pfx + "A" + name, shape, dt))

            xtA = [sba(f"xtA{i}", [128, BT]) for i in range(NCH)]
            lnxA = [sba(f"lnxA{i}", [128, BT], BF16) for i in range(NCH)]
            lnfA = [sba(f"lnfA{i}", [128, BT]) for i in range(NCH)]
            haA = [sba(f"haA{i}", [128, BT]) for i in range(NCH)]
            ipwt_s = sba("ipwt_s", [128, NCH * 3 * C], BF16)
            ipb_s = sba("ipb_s", [1, 3 * C], BF16)
            owt_s = sba("owt_s", [128, NCH * C], BF16)
            qkt = [sba(f"qkt{i}", [128, 3 * C], BF16) for i in range(B)]
            prod = sba("prod", [128, C])
            scr = sba("scr", [128, 256])
            mrow = sba("mrow", [128, 16])
            drow = sba("drow", [128, 16])
            ctmp = sba("ctmp", [128, C])
            ctxk = [sba(f"ctxk{i}", [128, C]) for i in range(B)]
            ctxT = [sba(f"ctxT{i}", [128, BT], BF16) for i in range(NCH)]

            def f_wd0(g):
                sems.wait(g, "st", ST0)
                sems.dma(g.dma_start(
                    out=ipwt_s[:, :].rearrange("p (ch i) -> p ch i", i=3 * C),
                    in_=ipwt_d.rearrange("(ch p) i -> p ch i", p=128)), "wgt")
                sems.dma(g.dma_start(out=ipb_s[:, :], in_=ipb_d), "wgt")
                sems.dma(g.dma_start(
                    out=owt_s[:, :].rearrange("p (ch i) -> p ch i", i=C),
                    in_=owt_d.rearrange("(ch p) i -> p ch i", p=128)), "wgt")
            block.gpsimd(f_wd0)
            WGT = sems.val("wgt")
            CSTG = sems.val("cstg")

            for blk in range(NCH):
                def f_ldx(sy, blk=blk, prev=sems.val("lnAnm")):
                    if blk >= 1:
                        sems.wait(sy, "lnAnm", prev)
                    else:
                        sems.wait(sy, "st", ST0)
                    for chk in range(NCH):
                        sems.dma(sy.dma_start(out=xtA[chk][:, :].rearrange(
                            "p (b l) -> p b l", b=B),
                            in_=blkap(x_d, chk, blk)), "xldA")
                block.sync(f_ldx)

                nm_edge = emit_ln(
                    nc, block, sems, t, xtA, lnxA, BT, tag="lnA",
                    ld_edge=("xldA", sems.val("xldA")),
                    wr_edges=[("qpm", sems.val("qpm")),
                              ("hae", sems.val("hae")),
                              ("hst", sems.val("hst"))],
                    bc_edges=[("qke", sems.val("qke")),
                              ("lnBnm", sems.val("lnBnm"))],
                    out_f32=lnfA)

                qp0 = sems.val("qpm")
                qk0 = sems.val("qke")

                def f_qkv(tt, nm_edge=nm_edge, qp0=qp0, qk0=qk0,
                          tev_prev=sems.val("tev"), hae_prev=sems.val("hae")):
                    sems.wait(tt, "wgt", WGT)
                    sems.wait(tt, "cstg", CSTG)
                    sems.wait(tt, *nm_edge)
                    sems.wait(tt, "qke", qk0)
                    sems.wait(tt, "tev", tev_prev)
                    sems.wait(tt, "hae", hae_prev)
                    for b in range(B):
                        if b >= 1:
                            sems.wait(tt, "qke", qk0 + b)
                        for cc in range(NCH):
                            lhs = lnxA[cc][:, b * 128:(b + 1) * 128]
                            for hcol in range(6):
                                tt.matmul(qkvp[hcol], lhs,
                                          ipwt_s[:, cc * 3 * C + hcol * 512:
                                                 cc * 3 * C + (hcol + 1) * 512],
                                          start=(cc == 0), stop=False)
                        for hcol in range(6):
                            ins = tt.matmul(qkvp[hcol],
                                            t["ones_row_bf"][:, :],
                                            ipb_s[0:1, hcol * 512:
                                                  (hcol + 1) * 512],
                                            start=False, stop=True)
                        sems.inc(ins, "qpm")
                block.tensor(f_qkv)

                def f_qev(v, qp0=qp0):
                    for b in range(B):
                        sems.wait(v, "qpm", qp0 + b + 1)
                        for hcol in range(6):
                            if hcol == 5:
                                v.willinc("qke")
                            v.tensor_copy(
                                qkt[b][:, hcol * 512:(hcol + 1) * 512],
                                qkvp[hcol])
                block.vector(f_qev)

                def f_att_a(v):
                    for tb in range(B):
                        for sb_ in range(B):
                            v.tensor_tensor(
                                prod[:, :].rearrange("p (h d) -> p h d", d=64),
                                qkt[tb][:, 0:C].rearrange(
                                    "p (h d) -> p h d", d=64),
                                qkt[sb_][:, C:2 * C].rearrange(
                                    "p (h d) -> p h d", d=64), Op.mult)
                            v.tensor_reduce(
                                scr[:, (tb * 4 + sb_) * 16:
                                    (tb * 4 + sb_) * 16 + 16],
                                prod[:, :].rearrange("p (h d) -> p h d", d=64),
                                AX.X, Op.add)
                    for tb in range(B):
                        base = tb * 64
                        v.tensor_max(mrow[:, :], scr[:, base:base + 16],
                                     scr[:, base + 16:base + 32])
                        v.tensor_max(drow[:, :], scr[:, base + 32:base + 48],
                                     scr[:, base + 48:base + 64])
                        v.tensor_max(mrow[:, :], mrow[:, :], drow[:, :])
                        last = tb == B - 1
                        if last:
                            v.willinc("ats")
                        v.tensor_tensor(
                            scr[:, base:base + 64].rearrange(
                                "p (s h) -> p s h", h=16),
                            scr[:, base:base + 64].rearrange(
                                "p (s h) -> p s h", h=16),
                            mrow[:, :].unsqueeze(1).broadcast_to([128, 4, 16]),
                            Op.subtract)
                block.vector(f_att_a)
                ATS = sems.val("ats")

                def f_att_e(sc):
                    sems.wait(sc, "ats", ATS)
                    sc.willinc("ate")
                    sc.activation(scr[:, :], scr[:, :], AF.Exp, scale=0.125)
                block.scalar(f_att_e)
                ATE = sems.val("ate")

                def f_att_b(v, blk=blk):
                    sems.wait(v, "ate", ATE)
                    for tb in range(B):
                        base = tb * 64
                        v.tensor_add(drow[:, :], scr[:, base:base + 16],
                                     scr[:, base + 16:base + 32])
                        v.tensor_add(mrow[:, :], scr[:, base + 32:base + 48],
                                     scr[:, base + 48:base + 64])
                        v.tensor_add(drow[:, :], drow[:, :], mrow[:, :])
                        v.reciprocal(drow[:, :], drow[:, :])
                        v.tensor_tensor(
                            scr[:, base:base + 64].rearrange(
                                "p (s h) -> p s h", h=16),
                            scr[:, base:base + 64].rearrange(
                                "p (s h) -> p s h", h=16),
                            drow[:, :].unsqueeze(1).broadcast_to([128, 4, 16]),
                            Op.mult)
                    for tb in range(B):
                        for sb_ in range(B):
                            att_v = scr[:, (tb * 4 + sb_) * 16:
                                        (tb * 4 + sb_) * 16 + 16]                                 .unsqueeze(2).broadcast_to([128, 16, 64])
                            vv = qkt[sb_][:, 2 * C:3 * C].rearrange(
                                "p (h d) -> p h d", d=64)
                            dst = ctxk[tb][:, :].rearrange(
                                "p (h d) -> p h d", d=64)
                            if sb_ == 0:
                                v.tensor_tensor(dst, vv, att_v, Op.mult)
                            else:
                                v.tensor_tensor(
                                    ctmp[:, :].rearrange("p (h d) -> p h d",
                                                         d=64),
                                    vv, att_v, Op.mult)
                                if tb == B - 1 and sb_ == B - 1:
                                    v.willinc("ctm")
                                    v.tensor_add(ctxk[tb][:, :], ctxk[tb][:, :],
                                                 ctmp[:, :])
                                else:
                                    v.tensor_add(ctxk[tb][:, :], ctxk[tb][:, :],
                                                 ctmp[:, :])
                block.vector(f_att_b)
                CTM = sems.val("ctm")

                tp0 = sems.val("tpm")
                te0 = sems.val("tev")

                def f_tp(tt, tp0=tp0, te0=te0, qk_now=sems.val("qke")):
                    sems.wait(tt, "ctm", CTM)
                    sems.wait(tt, "qke", qk_now)
                    k = 0
                    for tb in range(B):
                        for ch in range(NCH):
                            if k >= 2:
                                sems.wait(tt, "tev", te0 + k - 1)
                            ins = tt.transpose(tpp[k % 2],
                                               ctxk[tb][:, ch * 128:
                                                        (ch + 1) * 128],
                                               identf[:, :])
                            sems.inc(ins, "tpm")
                            k += 1
                block.tensor(f_tp)

                def f_tev(v, tp0=tp0):
                    k = 0
                    for tb in range(B):
                        for ch in range(NCH):
                            sems.wait(v, "tpm", tp0 + k + 1)
                            v.willinc("tev")
                            v.tensor_copy(ctxT[ch][:, tb * 128:(tb + 1) * 128],
                                          tpp[k % 2])
                            k += 1
                block.vector(f_tev)
                TEV = sems.val("tev")

                op0 = sems.val("opm")
                ha0 = sems.val("hae")

                def f_op(tt, op0=op0, ha0=ha0):
                    sems.wait(tt, "tev", TEV)
                    for co in range(NCH):
                        sems.wait(tt, "hae", ha0 + max(0, co - 1))
                        for cc in range(NCH):
                            w = owt_s[:, cc * C + co * 128:
                                      cc * C + co * 128 + 128]
                            ins = tt.matmul(opp[co % 2], w, ctxT[cc][:, :],
                                            start=(cc == 0),
                                            stop=(cc == NCH - 1))
                        sems.inc(ins, "opm")
                block.tensor(f_op)

                def f_hae(v, op0=op0):
                    for co in range(NCH):
                        sems.wait(v, "opm", op0 + co + 1)
                        v.willinc("hae")
                        v.scalar_tensor_tensor(haA[co][:, :], opp[co % 2],
                                               obc_s[:, co:co + 1],
                                               lnfA[co][:, :], Op.add, Op.add)
                block.vector(f_hae)

                nmB = emit_ln(nc, block, sems, t, haA, None, BT, tag="lnB",
                              ld_edge=("hae", sems.val("hae")),
                              wr_edges=[("hst", sems.val("hst"))],
                              bc_edges=[("qke", sems.val("qke")),
                                        ("tev", sems.val("tev"))],
                              gcol=l1g_s, bcol=l1b_s, out_f32=lnfA)

                def f_hst(sy, blk=blk, nmB=nmB):
                    sems.wait(sy, nmB[0], nmB[1])
                    for chk in range(NCH):
                        sems.dma(sy.dma_start(
                            out=blkap(h_scr.ap(), chk, blk),
                            in_=lnfA[chk][:, :].rearrange(
                                "p (b l) -> p b l", b=B)), "hst")
                block.sync(f_hst)

        HST = sems.val("hst")

        with ExitStack() as bctx:
            def sbb(name, shape, dt=F32):
                return bctx.enter_context(
                    nc.sbuf_tensor(env.pfx + "B" + name, shape, dt))

            xtB = [sbb(f"xtB{i}", [128, QT]) for i in range(NCH)]
            hf = [sbb(f"hf{i}", [128, QT]) for i in range(NCH)]
            hbf = [sbb(f"hbf{i}", [128, QT], BF16) for i in range(NCH)]
            heB = [sbb(f"heB{i}", [128, QT], BF16) for i in range(NCH)]
            yacB = [sbb(f"yacB{i}", [128, QT]) for i in range(NCH)]
            rwt_s = sbb("rwt_s", [128, NCH * 8])
            rbc_s = sbb("rbc_s", [8, 1])
            b1c_s = sbb("b1c_s", [128, 64])
            b2_s = sbb("b2_s", [8, C], BF16)
            lgtB = sbb("lgtB", [32, QT])
            srowB = sbb("srowB", [32, QT])
            prbB = sbb("prbB", [32, QT])
            asgB = sbb("asgB", [32, QT])
            mskB = sbb("mskB", [32, QT])
            msk8 = sbb("msk8", [8, QT], BF16)
            mr_bf = sbb("mr_bf", [1, QT], BF16)
            iocolB = sbb("iocolB", [32, 1])
            wtB1 = [sbb(f"wtB1_{i}", [128, NCH * 128], BF16) for i in range(3)]
            wtB2 = [sbb(f"wtB2_{i}", [128, NCH * 128], BF16) for i in range(4)]
            wtB2f = [sbb(f"wtB2f_{i}", [128, NCH * 128]) for i in range(3)]
            s1tB = [sbb(f"s1tB{i}", [128, 512]) for i in range(2)]

            zhB = [bc0, bc1]
            cbB = [pp4[2][:, :], pp4[3][:, :]]
            w2B = [pp4[0][:, :], pp4[1][:, :]]
            w2Bo = [t["psf"][0][:, :], t["psf"][1][:, :]]

            def f_ldB0(sy):
                sems.wait(sy, "hst", HST)
                sems.dma(sy.dma_start(out=rwt_s[:, :].rearrange(
                    "p (ch i) -> p ch i", i=8),
                    in_=rwt_d.rearrange("(ch p) i -> p ch i", p=128)), "cstB")
                sems.dma(sy.dma_start(out=rbc_s[:, :], in_=rbc_d), "cstB")
                sems.dma(sy.dma_start(out=b1c_s[:, :], in_=b1c_d), "cstB")
            block.sync(f_ldB0)

            def f_ldB0g(g):
                sems.wait(g, "hst", HST)
                sems.dma(g.dma_start(out=b2_s[:, :], in_=b2r_d), "cstBg")
                g.willinc("iotB")
                g.iota(iocolB[:, :], pattern=[[0, 1]], base=0,
                       channel_multiplier=1,
                       allow_small_or_imprecise_dtypes=True)
            block.gpsimd(f_ldB0g)
            CSTB = sems.val("cstB")
            CSTBG = sems.val("cstBg")
            IOTB = sems.val("iotB")
            prev_lnC = None

            for q in range(B):
                qsl = slice(q * QT, (q + 1) * QT)

                def f_ldq(sy, q=q, qsl=qsl, st_prev=sems.val("st")):
                    if q >= 1:
                        sems.wait(sy, "st", st_prev)
                    else:
                        sems.wait(sy, "hst", HST)
                    for chk in range(NCH):
                        sems.dma(sy.dma_start(
                            out=hf[chk][:, :],
                            in_=h_scr.ap()[128 * chk:128 * (chk + 1), qsl]),
                            "hldB")
                        sems.dma(sy.dma_start(
                            out=xtB[chk][:, :],
                            in_=x_d[128 * chk:128 * (chk + 1), qsl]), "hldB")
                block.sync(f_ldq)

                def f_ldqg(g, q=q, qsl=qsl, ze_prev=sems.val("sluB")):
                    sems.wait(g, "hst", HST)
                    if q >= 1:
                        sems.wait(g, "sluB", ze_prev)
                    for chk in range(NCH):
                        sems.dma(g.dma_start(
                            out=hbf[chk][:, :],
                            in_=h_scr.ap()[128 * chk:128 * (chk + 1), qsl]),
                            "hldBg")
                block.gpsimd(f_ldqg)
                HLDB = sems.val("hldB")
                HLDBG = sems.val("hldBg")

                rt0 = sems.val("rtm")

                def f_rt(tt, rt0=rt0, slu_prev=sems.val("sluB"),
                         lnc_prev=sems.val("lnCnm")):
                    sems.wait(tt, "cstB", CSTB)
                    sems.wait(tt, "hldB", HLDB)
                    sems.wait(tt, "sluB", slu_prev)
                    sems.wait(tt, "lnCnm", lnc_prev)
                    for tc in range(2):
                        for cc in range(NCH):
                            ins = tt.matmul(
                                zhB[tc][0:8, :],
                                rwt_s[:, cc * 8:(cc + 1) * 8],
                                hf[cc][:, tc * 512:(tc + 1) * 512],
                                start=(cc == 0), stop=(cc == NCH - 1))
                        sems.inc(ins, "rtm")
                block.tensor(f_rt)

                def f_asg(v, rt0=rt0):
                    sems.wait(v, "iotB", IOTB)
                    v.memset(lgtB[:, :], 0.0)
                    for tc in range(2):
                        sems.wait(v, "rtm", rt0 + tc + 1)
                        v.tensor_scalar(lgtB[0:8, tc * 512:(tc + 1) * 512],
                                        zhB[tc][0:8, :], rbc_s[0:8, 0:1],
                                        None, Op.add)
                    v.stream_shuffle(srowB[:, :], lgtB[:, :],
                                     mask=[i ^ 4 for i in range(32)])
                    v.tensor_max(prbB[:, :], lgtB[:, :], srowB[:, :])
                    v.stream_shuffle(srowB[:, :], prbB[:, :],
                                     mask=[i ^ 2 for i in range(32)])
                    v.tensor_max(prbB[:, :], prbB[:, :], srowB[:, :])
                    v.stream_shuffle(srowB[:, :], prbB[:, :],
                                     mask=[i ^ 1 for i in range(32)])
                    v.tensor_max(prbB[:, :], prbB[:, :], srowB[:, :])
                    v.tensor_tensor(mskB[:, :], lgtB[:, :], prbB[:, :],
                                    Op.is_equal)
                    v.tensor_scalar(asgB[:, :], mskB[:, :], iocolB[:, 0:1],
                                    None, Op.mult)
                    v.stream_shuffle(srowB[:, :], asgB[:, :],
                                     mask=[i ^ 4 for i in range(32)])
                    v.tensor_max(asgB[:, :], asgB[:, :], srowB[:, :])
                    v.stream_shuffle(srowB[:, :], asgB[:, :],
                                     mask=[i ^ 2 for i in range(32)])
                    v.tensor_max(asgB[:, :], asgB[:, :], srowB[:, :])
                    v.stream_shuffle(srowB[:, :], asgB[:, :],
                                     mask=[i ^ 1 for i in range(32)])
                    v.tensor_max(asgB[:, :], asgB[:, :], srowB[:, :])
                    v.scalar_tensor_tensor(prbB[:, :], mskB[:, :], -1e9,
                                           lgtB[:, :], Op.mult, Op.add)
                    v.stream_shuffle(srowB[:, :], prbB[:, :],
                                     mask=[i ^ 4 for i in range(32)])
                    v.tensor_max(srowB[:, :], prbB[:, :], srowB[:, :])
                    v.stream_shuffle(lgtB[:, :], srowB[:, :],
                                     mask=[i ^ 2 for i in range(32)])
                    v.tensor_max(srowB[:, :], srowB[:, :], lgtB[:, :])
                    v.stream_shuffle(lgtB[:, :], srowB[:, :],
                                     mask=[i ^ 1 for i in range(32)])
                    v.tensor_max(srowB[:, :], srowB[:, :], lgtB[:, :])
                    v.tensor_tensor(mskB[:, :], prbB[:, :], srowB[:, :],
                                    Op.is_equal)
                    v.tensor_scalar(mskB[:, :], mskB[:, :], iocolB[:, 0:1],
                                    None, Op.mult)
                    v.stream_shuffle(srowB[:, :], mskB[:, :],
                                     mask=[i ^ 4 for i in range(32)])
                    v.tensor_max(mskB[:, :], mskB[:, :], srowB[:, :])
                    v.stream_shuffle(srowB[:, :], mskB[:, :],
                                     mask=[i ^ 2 for i in range(32)])
                    v.tensor_max(mskB[:, :], mskB[:, :], srowB[:, :])
                    v.stream_shuffle(srowB[:, :], mskB[:, :],
                                     mask=[i ^ 1 for i in range(32)])
                    v.tensor_max(mskB[:, :], mskB[:, :], srowB[:, :])
                    v.tensor_max(asgB[:, :], asgB[:, :], mskB[:, :])
                    v.willinc("asg")
                    v.tensor_scalar(msk8[0:8, :], asgB[0:8, :],
                                    iocolB[0:8, 0:1], None, Op.is_equal)
                block.vector(f_asg)
                ASG = sems.val("asg")

                for e in range(8):
                    zm0 = sems.val("zmmB")
                    sl0 = sems.val("sluB")
                    he0 = sems.val("hevB")
                    wm0 = sems.val("wmmB")
                    ye0 = sems.val("yevB")
                    cb0 = sems.val("cbmB")

                    wvB = []

                    def f_wdB(g, e=e, zm0=zm0, wvB=wvB):
                        sems.wait(g, "st", ST0)
                        for ch in range(NCH):
                            sems.wait(g, "zmmB", max(0, zm0 + ch - 2))
                            slot = (zm0 + ch) % 3
                            v = sems.dma(g.dma_start(
                                out=wtB1[slot][:, :].rearrange(
                                    "p (c i) -> p c i", i=128),
                                in_=w1t_d[e][:, 128 * ch:128 * (ch + 1)]
                                .rearrange("(c p) i -> p c i", p=128)),
                                "wdB" + str(slot))
                            wvB.append((slot, v))
                    block.gpsimd(f_wdB)

                    wcB0 = sems.val("wcsB")
                    w2vB = []

                    def f_w2dB(sy, e=e, wcB0=wcB0, w2vB=w2vB):
                        sems.wait(sy, "st", ST0)
                        for co in range(NCH):
                            sems.wait(sy, "wcsB", max(0, wcB0 + co - 2))
                            fs = (wcB0 + co) % 3
                            v = sems.dma(sy.dma_start(
                                out=wtB2f[fs][:, :].rearrange(
                                    "p (c i) -> p c i", i=128),
                                in_=w2t_d[e][:, 128 * co:128 * (co + 1)]
                                .rearrange("(c p) i -> p c i", p=128)),
                                "w2dB" + str(fs))
                            w2vB.append((fs, v))
                    block.sync(f_w2dB)

                    def f_mr(v, e=e, cbm_now=sems.val("cbmB")):
                        sems.wait(v, "asg", ASG)
                        sems.wait(v, "cbmB", cbm_now)
                        v.tensor_scalar(mskB[:, :], asgB[:, :], float(e),
                                        None, Op.is_equal)
                        v.willinc("mrb")
                        v.tensor_copy(mr_bf[0:1, :], mskB[0:1, :])
                    block.vector(f_mr)

                    def f_peB(tt, e=e, zm0=zm0, he0=he0, wm0=wm0, ye0=ye0,
                              sl0=sl0, wvB=wvB, wcB0=wcB0, plnC=prev_lnC,
                              mrb_now=sems.val("mrb"), hldbg=HLDBG):
                        sems.wait(tt, "cstg", CSTG)
                        sems.wait(tt, "hldBg", hldbg)
                        sems.wait(tt, "mrb", mrb_now)
                        sems.wait(tt, "yevB", ye0)
                        for r in range(2):
                            ins = tt.matmul(cbB[r], t["ones_row_bf"][:, :],
                                            mr_bf[0:1, r * 512:(r + 1) * 512],
                                            start=True, stop=True)
                        sems.inc(ins, "cbmB")
                        for ch in range(NCH):
                            slot, wvv = wvB[ch]
                            sems.wait(tt, "wdB" + str(slot), wvv)
                            if ch >= 1:
                                sems.wait(tt, "sluB", sl0 + 2 * ch)
                            for cc in range(NCH):
                                w = wtB1[slot][:, cc * 128:(cc + 1) * 128]
                                for r in range(2):
                                    ins = tt.matmul(
                                        zhB[r], w,
                                        hbf[cc][:, r * 512:(r + 1) * 512],
                                        start=(cc == 0), stop=(cc == NCH - 1))
                            sems.inc(ins, "zmmB")
                        for co in range(NCH):
                            sems.wait(tt, "wcsB", wcB0 + co + 1)
                            if co == 0:
                                sems.wait(tt, "hevB", he0 + 2 * NCH)
                                sems.wait(tt, "yevB", ye0)
                            else:
                                sems.wait(tt, "yevB", ye0 + max(0, co - 1))
                            if co == 1 and plnC is not None:
                                sems.wait(tt, plnC[0], plnC[1])
                            wb = w2B if co % 2 == 0 else w2Bo
                            for ch in range(NCH):
                                w = wtB2[(wcB0 + co) % 4][:,
                                                          ch * 128:
                                                          (ch + 1) * 128]
                                for r in range(2):
                                    ins = tt.matmul(
                                        wb[r], w,
                                        heB[ch][:, r * 512:(r + 1) * 512],
                                        start=(ch == 0), stop=(ch == NCH - 1))
                            sems.inc(ins, "wmmB")
                    block.tensor(f_peB)

                    def f_siluB(sc, e=e, zm0=zm0, he0=he0):
                        sems.wait(sc, "hldBg", HLDBG)
                        for k in range(2 * NCH):
                            ch, r = k // 2, k % 2
                            sems.wait(sc, "zmmB", zm0 + ch + 1)
                            if k >= 2:
                                sems.wait(sc, "hevB", he0 + k - 1)
                            sc.willinc("sluB")
                            sc.activation(s1tB[k % 2][:, :], zhB[r], AF.Gelu,
                                          bias=b1c_s[:, e * 8 + ch:
                                                     e * 8 + ch + 1])
                    block.scalar(f_siluB)

                    def f_w2cB(sc, e=e, wm0=wm0, w2vB=w2vB, wcB0=wcB0):
                        for co in range(NCH):
                            fs, v = w2vB[co]
                            sems.wait(sc, "w2dB" + str(fs), v)
                            sems.wait(sc, "wmmB", wm0 + max(0, co - 3))
                            sc.willinc("wcsB")
                            sc.activation(wtB2[(wcB0 + co) % 4][:, :],
                                          wtB2f[fs][:, :], AF.Copy)
                    block.scalar(f_w2cB)

                    def f_heB(v, e=e, sl0=sl0, he0=he0,
                              cb_now=sems.val("cbmB")):
                        sems.wait(v, "cbmB", cb_now)
                        for k in range(2 * NCH):
                            ch, r = k // 2, k % 2
                            sems.wait(v, "sluB", sl0 + k + 1)
                            v.willinc("hevB")
                            v.tensor_mul(heB[ch][:, r * 512:(r + 1) * 512],
                                         s1tB[k % 2][:, :], cbB[r])
                    block.vector(f_heB)

                    def f_yaccB(v, e=e, wm0=wm0):
                        for co in range(NCH):
                            sems.wait(v, "wmmB", wm0 + co + 1)
                            wb = w2B if co % 2 == 0 else w2Bo
                            for r in range(2):
                                so = slice(r * 512, (r + 1) * 512)
                                if r == 1:
                                    v.willinc("yevB")
                                if e == 0:
                                    v.tensor_copy(yacB[co][:, so], wb[r])
                                else:
                                    v.tensor_add(yacB[co][:, so],
                                                 yacB[co][:, so], wb[r])
                    block.vector(f_yaccB)

                bt0 = sems.val("btm")

                def f_b2(tt, bt0=bt0, ye_now=sems.val("yevB"),
                         be0=sems.val("b2e")):
                    sems.wait(tt, "cstBg", CSTBG)
                    sems.wait(tt, "yevB", ye_now)
                    for co in range(NCH):
                        if co >= 2:
                            sems.wait(tt, "b2e", be0 + co - 1)
                        for r in range(2):
                            ins = tt.matmul(w2B[r] if co % 2 == 0 else cbB[r],
                                            b2_s[:, 128 * co:128 * (co + 1)],
                                            msk8[:, r * 512:(r + 1) * 512],
                                            start=True, stop=True)
                        sems.inc(ins, "btm")
                block.tensor(f_b2)

                def f_b2e(v, bt0=bt0):
                    for co in range(NCH):
                        sems.wait(v, "btm", bt0 + co + 1)
                        for r in range(2):
                            so = slice(r * 512, (r + 1) * 512)
                            if r == 1:
                                v.willinc("b2e")
                            v.tensor_add(yacB[co][:, so], yacB[co][:, so],
                                         w2B[r] if co % 2 == 0 else cbB[r])
                block.vector(f_b2e)

                def f_hmoe(v, b2e_now=sems.val("b2e")):
                    sems.wait(v, "b2e", b2e_now)
                    for co in range(NCH):
                        if co == NCH - 1:
                            v.willinc("hmo")
                        v.tensor_add(yacB[co][:, :], yacB[co][:, :],
                                     hf[co][:, :])
                block.vector(f_hmoe)

                nmC = emit_ln(nc, block, sems, t, yacB, None, QT, tag="lnC",
                              ld_edge=("hmo", sems.val("hmo")),
                              wr_edges=[("rtm", sems.val("rtm"))],
                              bc_edges=[("hevB", sems.val("hevB")),
                                        ("rtm", sems.val("rtm"))],
                              gcol=l2g_s, bcol=l2b_s, out_f32=hf)
                prev_lnC = nmC

                xe0 = sems.val("xevB")

                def f_residB(v, nmC=nmC, xe0=xe0):
                    sems.wait(v, nmC[0], nmC[1])
                    sems.wait(v, "hldB", HLDB)
                    for co in range(NCH):
                        for r in range(2):
                            if r == 1:
                                v.willinc("xevB")
                            so = slice(r * 512, (r + 1) * 512)
                            v.scalar_tensor_tensor(xtB[co][:, so],
                                                   hf[co][:, so], 0.5,
                                                   xtB[co][:, so],
                                                   Op.mult, Op.add)
                block.vector(f_residB)

                def f_storeB(sy, qsl=qsl, xe0=xe0):
                    for co in range(NCH):
                        sems.wait(sy, "xevB", xe0 + co + 1)
                        sems.dma(sy.dma_start(
                            out=xo_d[128 * co:128 * (co + 1), qsl],
                            in_=xtB[co][:, :]), "st")
                block.sync(f_storeB)


def build_f0():
    return _standalone(lambda env, xi, xo: emit_f0(env, xi, xo), None)


def run_f0(x_parts, inputs):
    nc = _get_prog("f0", build_f0)
    in_maps = []
    for k in range(NCORES):
        d = f0_host_inputs(inputs, k)
        d["x"] = x_parts[k]
        in_maps.append(d)
    res = run_bass_kernel_spmd(nc, in_maps, core_ids=list(range(NCORES)))
    return [res.results[k]["xo"] for k in range(NCORES)]



def _fft_tables():
    n1 = np.arange(128)
    W = lambda M, sgn: np.exp(sgn * 2j * np.pi * np.outer(n1, n1) / M)
    D1 = W(128, -1)[:64, :]
    TW = np.exp(-2j * np.pi * np.outer(n1, n1) / 16384)
    D2 = W(128, -1)
    E1 = W(128, +1) / 128.0
    T2 = np.exp(+2j * np.pi * np.outer(n1, n1) / 16384)
    E2 = (W(128, +1) / 128.0)[:, :64]
    f32 = np.float32
    combo = lambda M: np.ascontiguousarray(np.concatenate(
        [-M.imag, M.real, M.imag], axis=1).astype(f32))
    return {
        "d1c": combo(D1),
        "twr": np.ascontiguousarray(TW.real.astype(f32)),
        "twi": np.ascontiguousarray(TW.imag.astype(f32)),
        "d2r": np.ascontiguousarray(D2.real.astype(f32)),
        "d2i": np.ascontiguousarray(D2.imag.astype(f32)),
        "e1c": combo(E1),
        "t2r": np.ascontiguousarray(T2.real.astype(f32)),
        "t2i": np.ascontiguousarray(T2.imag.astype(f32)),
        "e2r": np.ascontiguousarray(E2.real.astype(f32)),
        "e2i": np.ascontiguousarray(E2.imag.astype(f32)),
    }


def a1_host_inputs(inputs, core):
    cones = np.ones((128, 128), np.float32)
    cones[1, 1] = EPS
    cones[1, 2] = -0.5
    cones[1, 3] = 0.125
    d = {
        "wuv": np.ascontiguousarray(inputs["L1_attn_W_uv"]),
        "wo_": np.ascontiguousarray(inputs["L1_attn_W_o"]),
        "riw": np.ascontiguousarray(inputs["L1_attn_rpe_in_w"]),
        "rhw": np.ascontiguousarray(inputs["L1_attn_rpe_hid_w"]),
        "rlg": np.ascontiguousarray(np.asarray(inputs["L1_attn_rpe_ln_g"]).T),
        "rlb": np.ascontiguousarray(np.asarray(inputs["L1_attn_rpe_ln_b"]).T),
        "rowc": np.ascontiguousarray(
            np.asarray(inputs["L1_attn_rpe_out_w"])[:, core * 128:
                                                    (core + 1) * 128]),
        "dgc": np.ascontiguousarray(
            np.asarray(inputs["L1_attn_decay_gamma"])
            [core * 128:(core + 1) * 128].reshape(128, 1)),
        "cones": cones,
    }
    zidx = np.zeros((64, 64), np.int32)
    for r in range(NCORES):
        for a in range(8):
            for jg in range(16):
                for b in range(B):
                    zidx[r * 8 + a, jg * 4 + b] = (
                        ((r * 4 + b) * 8 + a) * 128 + (core * 16 + jg))
    oidx = np.zeros((128, 32), np.int32)
    for ch in range(NCORES):
        for b in range(B):
            for j in range(128):
                oidx[j, ch * 4 + b] = ((ch * 128 + j) * 4 + b) * 8 + core
    d["zidx"] = zidx
    d["oidx"] = oidx
    d.update(_fft_tables())
    return d


def emit_a1(env, x_d, xo_d):
    env.enter_phase()
    nc, sems, t, block, pp4 = env.nc, env.sems, env.t, env.block, env.pp4
    wuv_d = env.Dap("wuv", [C, 2 * C])
    wo_d = env.Dap("wo_", [C, C])
    riw_d = env.Dap("riw", [1, 32])
    rhw_d = env.Dap("rhw", [3, 32, 32])
    rlg_d = env.Dap("rlg", [32, 3])
    rlb_d = env.Dap("rlb", [32, 3])
    rowc_d = env.Dap("rowc", [32, 128])
    dgc_d = env.Dap("dgc", [128, 1])
    cones_d = env.Dap("cones", [128, 128])
    zidx_d = env.Dap("zidx", [64, 64], I32)
    oidx_d = env.Dap("oidx", [128, 32], I32)
    tab_d = {}
    for nm, shp in (("d1c", [64, 384]), ("twr", [128, 128]),
                    ("twi", [128, 128]), ("d2r", [128, 128]),
                    ("d2i", [128, 128]), ("e1c", [128, 384]),
                    ("t2r", [128, 128]), ("t2i", [128, 128]),
                    ("e2r", [128, 64]), ("e2i", [128, 64])):
        tab_d[nm] = env.Dap(nm, shp)

    u_scr = env.D("u_scr", [C, TOK], BF16, "Internal")
    vex_in = env.D("vex_in", [B, 8, C, 128], BF16, "Internal")
    vex_out = env.D("vex_out", [NCORES, B, 8, C, 128], BF16, "Internal")
    t_scr = env.D("t_scr", [128, LFULL], BF16, "Internal")
    oex_in = env.D("oex_in", [128, B, LFULL], BF16, "Internal")
    oex_out = env.D("oex_out", [NCORES, 128, B, LFULL], BF16, "Internal")
    ST0 = sems.val("st")

    QT = 1024

    with ExitStack() as ctx:
        psf = t["psf"]

        def sb(name, shape, dt=F32):
            return ctx.enter_context(nc.sbuf_tensor(env.pfx + name, shape,
                                                    dt))

        xt = [sb(f"xt{i}", [128, QT]) for i in range(NCH)]
        wuv_s = sb("wuv_s", [128, NCH * 2 * C], BF16)
        tabs = {nm: sb(nm + "_s", [tab_d[nm].shape[0], tab_d[nm].shape[1]],
                       BF16 if nm in ("d1c", "d2r", "d2i", "e1c", "e2r",
                                      "e2i") else F32)
                for nm in tab_d}
        riw_s = sb("riw_s", [1, 32])
        rhw_s = sb("rhw_s", [32, 3 * 32], BF16)
        rlg_s = sb("rlg_s", [32, 3])
        rlb_s = sb("rlb_s", [32, 3])
        rowc_s = sb("rowc_s", [32, 128], BF16)
        dgc_s = sb("dgc_s", [128, 1])
        lngc = sb("lngc", [128, 1])
        murR = sb("murR", [1, 512])
        vrR = sb("vrR", [1, 512])
        rtR = sb("rtR", [1, 512])
        scope1 = ctx.enter_context(ExitStack())

        def sb1(name, shape, dt=F32):
            return scope1.enter_context(nc.sbuf_tensor(env.pfx + name, shape,
                                                       dt))

        lnu = [sb1(f"lnu{i}", [128, QT], BF16) for i in range(NCH)]
        vtile = [sb1(f"vtile{i}", [128, QT], BF16) for i in range(2)]
        ustg = [sb1(f"ustg{i}", [128, QT], BF16) for i in range(2)]
        hM = sb1("hM", [32, LFULL])
        hN = sb1("hN", [32, LFULL])
        mrow = sb1("mrow", [1, 512])
        mbase = sb1("mbase", [1, 512])
        tdec = sb1("tdec", [128, 512])

        psA = [psf[i // 2][:, (i % 2) * 256:(i % 2) * 256 + 256]
               for i in range(8)]
        psB = [pp4[i // 2][:, (i % 2) * 256:(i % 2) * 256 + 256]
               for i in range(8)]

        emit_const_loads(block, sems, t, cones_d, env)
        CSTG = sems.val("cstg")

        def f_ld0(sy):
            sems.wait(sy, "st", ST0)
            for nm in ("twr", "twi", "t2r", "t2i", "rlg", "rlb", "dgc"):
                src = {"rlg": rlg_d, "rlb": rlb_d, "dgc": dgc_d}.get(nm)
                dst = {"rlg": rlg_s, "rlb": rlb_s, "dgc": dgc_s}.get(nm)
                if src is None:
                    src, dst = tab_d[nm], tabs[nm]
                sems.dma(sy.dma_start(out=dst[:, :], in_=src), "cst2")
        block.sync(f_ld0)

        def f_ld0g(g):
            sems.wait(g, "st", ST0)
            for nm in ("d1c", "d2r", "d2i", "e1c", "e2r", "e2i"):
                sems.dma(g.dma_start(out=tabs[nm][:, :], in_=tab_d[nm]),
                         "wgt")
            sems.dma(g.dma_start(
                out=wuv_s[:, :].rearrange("p (ch i) -> p ch i", i=2 * C),
                in_=wuv_d.rearrange("(ch p) i -> p ch i", p=128)), "wgt")
            sems.dma(g.dma_start(out=riw_s[:, :], in_=riw_d), "wgt")
            sems.dma(g.dma_start(
                out=rhw_s[:, :].rearrange("p (l i) -> p l i", i=32),
                in_=rhw_d.rearrange("l p i -> p l i")), "wgt")
            sems.dma(g.dma_start(out=rowc_s[:, :], in_=rowc_d), "wgt")
            g.willinc("iotm")
            g.iota(mbase[0:1, :], pattern=[[1, 512]], base=0,
                   channel_multiplier=0, allow_small_or_imprecise_dtypes=True)
        block.gpsimd(f_ld0g)
        WGT = sems.val("wgt")
        CST2 = sems.val("cst2")
        IOTM = sems.val("iotm")

        for q in range(B):
            qsl = slice(q * QT, (q + 1) * QT)

            def f_ldx(sy, qsl=qsl, prev=sems.val("lnnm")):
                sems.wait(sy, "lnnm", prev)
                sems.wait(sy, "st", ST0)
                for chk in range(NCH):
                    sems.dma(sy.dma_start(
                        out=xt[chk][:, :],
                        in_=x_d[128 * chk:128 * (chk + 1), qsl]), "xld")
            block.sync(f_ldx)

            nm_edge = emit_ln(nc, block, sems, t, xt, lnu, QT, tag="ln",
                              ld_edge=("xld", sems.val("xld")),
                              wr_edges=[("uvm", sems.val("uvm"))],
                              bc_edges=[("uve", sems.val("uve"))])

            uv0 = sems.val("uvm")
            ue0 = sems.val("uve")

            def f_uv(tt, q=q, nm_edge=nm_edge, uv0=uv0, ue0=ue0):
                sems.wait(tt, "wgt", WGT)
                sems.wait(tt, *nm_edge)
                for rc in range(16):
                    if rc >= 2:
                        sems.wait(tt, "uve", ue0 + rc - 1)
                    for cc in range(NCH):
                        w = wuv_s[:, cc * 2 * C + rc * 128:
                                  cc * 2 * C + rc * 128 + 128]
                        for r in range(2):
                            ins = tt.matmul(
                                pp4[2 * (rc % 2) + r][:, :], w,
                                lnu[cc][:, r * 512:(r + 1) * 512],
                                start=(cc == 0), stop=(cc == NCH - 1))
                    sems.inc(ins, "uvm")
            block.tensor(f_uv)

            def f_uve(sc, q=q, uv0=uv0,
                      vs0=[sems.val("vst0"), sems.val("vst1")],
                      us0=[sems.val("ust0"), sems.val("ust1")]):
                for rc in range(16):
                    sems.wait(sc, "uvm", uv0 + rc + 1)
                    if rc < 8:
                        sems.wait(sc, "ust" + str(rc % 2),
                                  us0[rc % 2] + 16 * (rc // 2))
                    else:
                        sems.wait(sc, "vst" + str(rc % 2),
                                  vs0[rc % 2] + 16 * ((rc - 8) // 2))
                    for r in range(2):
                        so = slice(r * 512, (r + 1) * 512)
                        if rc < 8:
                            dst = ustg[rc % 2][:, so]
                        else:
                            dst = vtile[(rc - 8) % 2][:, so]
                        if r == 1:
                            sc.willinc("uve")
                        sc.activation(dst, pp4[2 * (rc % 2) + r][:, :],
                                      AF.Silu)
            block.scalar(f_uve)

            def f_ust(sy, q=q, ue0=ue0):
                for uc in range(NCH):
                    sems.wait(sy, "uve", ue0 + uc + 1)
                    sems.dma(sy.dma_start(
                        out=u_scr.ap()[128 * uc:128 * (uc + 1),
                                       q * QT:(q + 1) * QT],
                        in_=ustg[uc % 2][:, :]), "ust" + str(uc % 2))
            block.sync(f_ust)

            def f_vst(sy, q=q, ue0=ue0):
                for vc in range(NCH):
                    sems.wait(sy, "uve", ue0 + 8 + vc + 1)
                    sems.dma(sy.dma_start(
                        out=vex_in.ap()[q, :, 128 * vc:128 * (vc + 1), :]
                        .rearrange("a c n -> c a n"),
                        in_=vtile[vc % 2][:, :].rearrange(
                            "p (a n) -> p a n", n=128)), "vst" + str(vc % 2))
            block.sync(f_vst)

        hfin = sb1("hfin", [32, LFULL], BF16)
        nrmt = sb1("nrmt", [32, 512], BF16)
        ttmp = [sb1(f"ttmp{i}", [128, 512], BF16) for i in range(2)]

        def f_rpe0(tt):
            sems.wait(tt, "wgt", WGT)
            sems.wait(tt, "cst2", CST2)
            sems.wait(tt, "iotm", IOTM)
            sems.wait(tt, "uvm", sems.val("uvm"))
            sems.wait(tt, "lnnm", sems.val("lnnm"))
            sems.wait(tt, "uve", sems.val("uve"))
        block.tensor(f_rpe0)

        RP_H0 = sems.val("rpeP")
        for mc in range(16):
            def g0(v, mc=mc, pv=sems.val("rpeP"), im=IOTM):
                sems.wait(v, "rpeP", pv)
                sems.wait(v, "iotm", im)
                v.willinc("iotg")
                v.tensor_scalar_add(mrow[0:1, :], mbase[0:1, :],
                                    float(mc * 512))
            block.vector(g0)

            def p0(tt, mc=mc, rv=sems.val("rpeA"), gv=sems.val("iotg")):
                sems.wait(tt, "rpeA", max(0, rv - 1))
                sems.wait(tt, "iotg", gv)
                sems.inc(tt.matmul(psf[mc % 2][0:32, 0:512], riw_s[0:1, :],
                                   mrow[0:1, :],
                                   start=True, stop=True), "rpeP")
            block.tensor(p0)

            def a0_(sc, mc=mc, pv=sems.val("rpeP")):
                sems.wait(sc, "rpeP", pv)
                sc.willinc("rpeA")
                sc.activation(hM[:, mc * 512:(mc + 1) * 512],
                              psf[mc % 2][0:32, 0:512], AF.Silu)
            block.scalar(a0_)

        cur, nxt = hM, hN
        for ly in range(3):
            for mc in range(16):
                sl = slice(mc * 512, (mc + 1) * 512)

                def aq(sc, cur=cur, sl=sl, pv=sems.val("rpeP"),
                       av=sems.val("rpeA")):
                    sems.wait(sc, "rpeP", pv)
                    sc.willinc("rpeA")
                    sc.activation(t["sq2"][0][0:32, :], cur[0:32, sl],
                                  AF.Square)
                block.scalar(aq)

                def p1(tt, cur=cur, sl=sl, av=sems.val("rpeA")):
                    sems.wait(tt, "rpeA", av)
                    tt.matmul(psf[0][0:1, 0:512], t["ones_col"][0:32, 0:1],
                              cur[0:32, sl], start=True, stop=True)
                    sems.inc(tt.matmul(psf[1][0:1, 0:512],
                                       t["ones_col"][0:32, 0:1],
                                       t["sq2"][0][0:32, :],
                                       start=True, stop=True), "rpeP")
                block.tensor(p1)

                def v1(v, pv=sems.val("rpeP")):
                    sems.wait(v, "rpeP", pv)
                    v.tensor_scalar_mul(murR[0:1, :], psf[0][0:1, 0:512],
                                        1.0 / 32)
                    v.tensor_scalar_mul(vrR[0:1, :], psf[1][0:1, 0:512],
                                        1.0 / 32)
                    v.tensor_mul(rtR[0:1, :], murR[0:1, :], murR[0:1, :])
                    v.tensor_sub(vrR[0:1, :], vrR[0:1, :], rtR[0:1, :])
                    v.willinc("rpeV")
                    v.tensor_scalar_add(vrR[0:1, :], vrR[0:1, :], EPS)
                block.vector(v1)

                def a2(sc, vv=sems.val("rpeV")):
                    sems.wait(sc, "rpeV", vv)
                    sc.activation(rtR[0:1, :], vrR[0:1, :], AF.Ln)
                    sc.willinc("rpeA")
                    sc.activation(vrR[0:1, :], rtR[0:1, :], AF.Exp, scale=-0.5)
                block.scalar(a2)

                def p2(tt, av=sems.val("rpeA")):
                    sems.wait(tt, "rpeA", av)
                    tt.matmul(psf[2][0:32, 0:512], t["ones_row"][0:1, 0:32],
                              murR[0:1, :], start=True, stop=True)
                    sems.inc(tt.matmul(psf[3][0:32, 0:512],
                                       t["ones_row"][0:1, 0:32],
                                       vrR[0:1, :], start=True, stop=True),
                             "rpeP")
                block.tensor(p2)

                def v2(v, ly=ly, cur=cur, sl=sl, pv=sems.val("rpeP")):
                    sems.wait(v, "rpeP", pv)
                    v.tensor_sub(t["lntmp"][0:32, :], cur[0:32, sl],
                                 psf[2][0:32, 0:512])
                    v.tensor_mul(t["lntmp"][0:32, :], t["lntmp"][0:32, :],
                                 psf[3][0:32, 0:512])
                    v.willinc("rpeV")
                    v.tensor_scalar(nrmt[0:32, :], t["lntmp"][0:32, :],
                                    rlg_s[0:32, ly:ly + 1],
                                    rlb_s[0:32, ly:ly + 1], Op.mult, Op.add)
                block.vector(v2)

                def p3(tt, ly=ly, vv=sems.val("rpeV")):
                    sems.wait(tt, "rpeV", vv)
                    sems.inc(tt.matmul(pp4[0][0:32, 0:512],
                                       rhw_s[:, ly * 32:(ly + 1) * 32],
                                       nrmt[0:32, :], start=True, stop=True),
                             "rpeP")
                block.tensor(p3)

                def a3(sc, ly=ly, nxt=nxt, sl=sl, pv=sems.val("rpeP")):
                    sems.wait(sc, "rpeP", pv)
                    sc.willinc("rpeA")
                    dst = hfin[0:32, sl] if ly == 2 else nxt[0:32, sl]
                    sc.activation(dst, pp4[0][0:32, 0:512], AF.Silu)
                block.scalar(a3)
            cur, nxt = nxt, cur

        def agam(sc, pv=sems.val("rpeP")):
            sems.wait(sc, "rpeP", pv)
            sc.willinc("rpeA")
            sc.activation(lngc[:, :], dgc_s[:, :], AF.Sigmoid)
        block.scalar(agam)

        def vgam(v, av=sems.val("rpeA")):
            sems.wait(v, "rpeA", av)
            v.willinc("rpeV")
            v.tensor_scalar(lngc[:, :], lngc[:, :], 0.01, 0.99,
                            Op.mult, Op.add)
        block.vector(vgam)

        def agam2(sc, vv=sems.val("rpeV")):
            sems.wait(sc, "rpeV", vv)
            sc.willinc("rpeA")
            sc.activation(lngc[:, :], lngc[:, :], AF.Ln)
        block.scalar(agam2)

        for mc in range(16):
            sl = slice(mc * 512, (mc + 1) * 512)

            def gt(v, mc=mc, pv=sems.val("rpeP")):
                sems.wait(v, "rpeP", pv)
                v.willinc("iotg")
                v.tensor_scalar_add(mrow[0:1, :], mbase[0:1, :],
                                    float(mc * 512))
            block.vector(gt)

            def pt(tt, sl=sl, vv=sems.val("rpeV"), av=sems.val("rpeA"),
                   gv=sems.val("iotg")):
                sems.wait(tt, "rpeA", av)
                sems.wait(tt, "rpeV", vv)
                sems.wait(tt, "iotg", gv)
                tt.matmul(pp4[1][:, :], rowc_s[:, :], hfin[0:32, sl],
                          start=True, stop=True)
                sems.inc(tt.matmul(pp4[2][:, :], t["ones_row"][:, :],
                                   mrow[0:1, :], start=True, stop=True),
                         "rpeP")
            block.tensor(pt)

            def vt1(v, pv=sems.val("rpeP")):
                sems.wait(v, "rpeP", pv)
                v.willinc("rpeV")
                v.tensor_scalar(tdec[:, :], pp4[2][:, :], lngc[:, 0:1], None,
                                Op.mult)
            block.vector(vt1)

            def at1(sc, vv=sems.val("rpeV")):
                sems.wait(sc, "rpeV", vv)
                sc.willinc("rpeA")
                sc.activation(tdec[:, :], tdec[:, :], AF.Exp)
            block.scalar(at1)

            def vt2(v, mc=mc, sl=sl, av=sems.val("rpeA"),
                    ts0=sems.val("tst")):
                sems.wait(v, "rpeA", av)
                if mc >= 2:
                    sems.wait(v, "tst", ts0)
                v.willinc("rpeV")
                v.tensor_tensor(ttmp[mc % 2][:, :], pp4[1][:, :],
                                tdec[:, :], Op.mult)
            block.vector(vt2)

            def f_tstc(sy, mc=mc, sl=sl, vv=sems.val("rpeV")):
                sems.wait(sy, "rpeV", vv)
                sems.dma(sy.dma_start(out=t_scr.ap()[:, sl],
                                      in_=ttmp[mc % 2][:, :]), "tst")
            block.sync(f_tstc)
        TST = sems.val("tst")

        def f_ag1(g, v0=sems.val("vst0"), v1=sems.val("vst1"),
                  u0=sems.val("ust0"), u1=sems.val("ust1")):
            sems.wait(g, "vst0", v0)
            sems.wait(g, "vst1", v1)
            sems.wait(g, "ust0", u0)
            sems.wait(g, "ust1", u1)
            sems.inc(g.collective_compute(
                "AllGather", mybir.AluOpType.bypass,
                replica_groups=[list(range(NCORES))],
                ins=[vex_in.ap().rearrange("b a c n -> (b a c) n").opt()],
                outs=[vex_out.ap().rearrange(
                    "r b a c n -> (r b a c) n").opt()]), "cc")
        block.gpsimd(f_ag1)

        scope1.close()
        scope2 = ctx.enter_context(ExitStack())

        def sb2(name, shape, dt=F32):
            return scope2.enter_context(nc.sbuf_tensor(env.pfx + name, shape,
                                                       dt))

        zr = sb2("zr", [64, 1024], BF16)
        zi = sb2("zi", [64, 1024], BF16)
        zidx_s = sb2("zidx_s", [64, 64], I32)
        ttile = sb2("ttile", [64, 1024], BF16)
        yc = sb2("yc", [128, 8 * 384], BF16)
        s1c = sb2("s1c", [128, 256])
        s2c = sb2("s2c", [128, 256])
        tspec = sb2("tspec", [128, 8 * 256], BF16)
        zspec = sb2("zspec", [128, 8 * 256], BF16)
        pspec = sb2("pspec", [128, 8 * 256], BF16)
        otile = sb2("otile", [64, 8 * 256], BF16)

        CC1 = sems.val("cc")

        def f_pid0(g):
            sems.wait(g, "cc", CC1)
            sems.dma(g.dma_start(out=zidx_s[:, :], in_=zidx_d), "cvLg")
        block.gpsimd(f_pid0)
        ZIDXL = sems.val("cvLg")
        vex2d = vex_out.ap().rearrange("r b a (cg w) n -> (r b a cg) (w n)",
                                       w=8)

        def tw_batch(v, src_ps, dst, ttr, tti):
            for c in range(8):
                orr = src_ps[c][:, 0:128]
                oii = src_ps[c][:, 128:256]
                v.tensor_tensor(s1c[:, 0:128], orr, ttr, Op.mult)
                v.tensor_tensor(s2c[:, 0:128], oii, tti, Op.mult)
                v.tensor_sub(dst[:, c * 384 + 128:c * 384 + 256],
                             s1c[:, 0:128], s2c[:, 0:128])
                v.tensor_tensor(s1c[:, 128:256], orr, tti, Op.mult)
                v.tensor_tensor(s2c[:, 128:256], oii, ttr, Op.mult)
                v.tensor_add(dst[:, c * 384 + 256:c * 384 + 384],
                             s1c[:, 128:256], s2c[:, 128:256])
                v.scalar_tensor_tensor(
                    dst[:, c * 384:c * 384 + 128], s1c[:, 128:256],
                    -1.0, s2c[:, 128:256], Op.mult, Op.subtract)

        for jg in range(16):
            def f_cl(sy, jg=jg, pv=sems.val("cvP")):
                sems.wait(sy, "cc", CC1)
                sems.wait(sy, "cvP", pv)
                sems.wait(sy, "tst", TST)
                sems.dma(sy.dma_start(
                    out=ttile[:, :].rearrange("p (c n) -> p c n", n=128),
                    in_=t_scr.ap()[8 * jg:8 * (jg + 1), :].rearrange(
                        "c (a n) -> a c n", n=128)), "cvL")
            block.sync(f_cl)
            CVL_T = sems.val("cvL")

            def f_ts1(tt, jg=jg, lv=CVL_T, vv=sems.val("cvV")):
                sems.wait(tt, "cvL", lv)
                sems.wait(tt, "tst", TST)
                sems.wait(tt, "cvV", vv)
                for c in range(8):
                    sems.inc(tt.matmul(psA[c],
                                       ttile[:, c * 128:(c + 1) * 128],
                                       tabs["d1c"][:, 128:384],
                                       start=True, stop=True), "cvP")
            block.tensor(f_ts1)

            def f_tw1(v, pv=sems.val("cvP")):
                sems.wait(v, "cvP", pv)
                tw_batch(v, psA, yc, tabs["twr"][:, :], tabs["twi"][:, :])
                v.willinc("cvV")
                v.tensor_copy(s1c[:, 0:2], s1c[:, 0:2])
            block.vector(f_tw1)

            def f_ts2(tt, vv=sems.val("cvV"), sv=sems.val("cvS")):
                sems.wait(tt, "cvV", vv)
                sems.wait(tt, "cvS", sv)
                for c in range(8):
                    tt.matmul(psB[c], tabs["d2r"][:, :],
                              yc[:, c * 384 + 128:c * 384 + 384],
                              start=True, stop=False)
                    sems.inc(tt.matmul(psB[c], tabs["d2i"][:, :],
                                       yc[:, c * 384:c * 384 + 256],
                                       start=False, stop=True), "cvP")
            block.tensor(f_ts2)

            def f_tse(v, pv=sems.val("cvP")):
                sems.wait(v, "cvP", pv)
                for c in range(8):
                    if c == 7:
                        v.willinc("cvS")
                    v.tensor_copy(tspec[:, c * 256:(c + 1) * 256], psB[c])
            block.vector(f_tse)

            for bp in range(2):
                def f_zl(g, jg=jg, bp=bp, pv=sems.val("cvP")):
                    sems.wait(g, "cvP", pv)
                    sems.wait(g, "cvLg", ZIDXL)
                    sems.dma(g.indirect_dma_start(
                        out=zr[:, :], out_offset=None, in_=vex2d,
                        in_offset=IndirectOffsetOnAxis(
                            ap=zidx_s[:, jg * 4 + 2 * bp:jg * 4 + 2 * bp + 1],
                            axis=0)), "cvLg")
                    sems.dma(g.indirect_dma_start(
                        out=zi[:, :], out_offset=None, in_=vex2d,
                        in_offset=IndirectOffsetOnAxis(
                            ap=zidx_s[:, jg * 4 + 2 * bp + 1:
                                      jg * 4 + 2 * bp + 2],
                            axis=0)), "cvLg")
                block.gpsimd(f_zl)
                CVL_Z = sems.val("cvLg")

                def f_z1(tt, lv=CVL_Z, vv=sems.val("cvV")):
                    sems.wait(tt, "cvLg", lv)
                    sems.wait(tt, "cvV", vv)
                    for c in range(8):
                        tt.matmul(psA[c], zr[:, c * 128:(c + 1) * 128],
                                  tabs["d1c"][:, 128:384],
                                  start=True, stop=False)
                        sems.inc(tt.matmul(psA[c],
                                           zi[:, c * 128:(c + 1) * 128],
                                           tabs["d1c"][:, 0:256],
                                           start=False, stop=True), "cvP")
                block.tensor(f_z1)

                def f_zw(v, pv=sems.val("cvP")):
                    sems.wait(v, "cvP", pv)
                    tw_batch(v, psA, yc, tabs["twr"][:, :], tabs["twi"][:, :])
                    v.willinc("cvV")
                    v.tensor_copy(s1c[:, 0:2], s1c[:, 0:2])
                block.vector(f_zw)

                def f_z2(tt, vv=sems.val("cvV"), sv=sems.val("cvS")):
                    sems.wait(tt, "cvV", vv)
                    sems.wait(tt, "cvS", sv)
                    for c in range(8):
                        tt.matmul(psB[c], tabs["d2r"][:, :],
                                  yc[:, c * 384 + 128:c * 384 + 384],
                                  start=True, stop=False)
                        sems.inc(tt.matmul(psB[c], tabs["d2i"][:, :],
                                           yc[:, c * 384:c * 384 + 256],
                                           start=False, stop=True), "cvP")
                block.tensor(f_z2)

                def f_zse(v, pv=sems.val("cvP")):
                    sems.wait(v, "cvP", pv)
                    for c in range(8):
                        v.tensor_copy(zspec[:, c * 256:(c + 1) * 256],
                                      psB[c])
                    zrv = zspec[:, :].rearrange("p (c w n) -> p c w n",
                                                w=2, n=128)[:, :, 0, :]
                    ziv = zspec[:, :].rearrange("p (c w n) -> p c w n",
                                                w=2, n=128)[:, :, 1, :]
                    trv = tspec[:, :].rearrange("p (c w n) -> p c w n",
                                                w=2, n=128)[:, :, 0, :]
                    tiv = tspec[:, :].rearrange("p (c w n) -> p c w n",
                                                w=2, n=128)[:, :, 1, :]
                    prv = pspec[:, :].rearrange("p (c w n) -> p c w n",
                                                w=2, n=128)[:, :, 0, :]
                    piv = pspec[:, :].rearrange("p (c w n) -> p c w n",
                                                w=2, n=128)[:, :, 1, :]
                    sp1 = yc[:, 0:1024].rearrange("p (c n) -> p c n", n=128)
                    sp2 = yc[:, 1024:2048].rearrange("p (c n) -> p c n",
                                                     n=128)
                    v.tensor_tensor(sp1, zrv, trv, Op.mult)
                    v.tensor_tensor(sp2, ziv, tiv, Op.mult)
                    v.tensor_tensor(prv, sp1, sp2, Op.subtract)
                    v.tensor_tensor(sp1, zrv, tiv, Op.mult)
                    v.tensor_tensor(sp2, ziv, trv, Op.mult)
                    v.willinc("cvS")
                    v.tensor_tensor(piv, sp1, sp2, Op.add)
                block.vector(f_zse)

                def f_i1(tt, sv=sems.val("cvS"), vv=sems.val("cvV")):
                    sems.wait(tt, "cvS", sv)
                    sems.wait(tt, "cvV", vv)
                    for c in range(8):
                        tt.matmul(psA[c],
                                  pspec[:, c * 256:c * 256 + 128],
                                  tabs["e1c"][:, 128:384],
                                  start=True, stop=False)
                        sems.inc(tt.matmul(psA[c],
                                           pspec[:, c * 256 + 128:
                                                 (c + 1) * 256],
                                           tabs["e1c"][:, 0:256],
                                           start=False, stop=True), "cvP")
                block.tensor(f_i1)

                def f_iw(v, pv=sems.val("cvP")):
                    sems.wait(v, "cvP", pv)
                    tw_batch(v, psA, yc, tabs["t2r"][:, :], tabs["t2i"][:, :])
                    v.willinc("cvV")
                    v.tensor_copy(s1c[:, 0:2], s1c[:, 0:2])
                block.vector(f_iw)

                def f_i2(tt, vv=sems.val("cvV"), ov=sems.val("cvO")):
                    sems.wait(tt, "cvV", vv)
                    sems.wait(tt, "cvO", ov)
                    for c in range(8):
                        tt.matmul(psB[c][0:64, :], tabs["e2r"][:, :],
                                  yc[:, c * 384 + 128:c * 384 + 384],
                                  start=True, stop=False)
                        sems.inc(tt.matmul(psB[c][0:64, :],
                                           tabs["e2i"][:, :],
                                           yc[:, c * 384:c * 384 + 256],
                                           start=False, stop=True), "cvP")
                block.tensor(f_i2)

                def f_oev(v, pv=sems.val("cvP"), os0=sems.val("ost")):
                    sems.wait(v, "cvP", pv)
                    sems.wait(v, "ost", os0)
                    for c in range(8):
                        if c == 7:
                            v.willinc("cvO")
                        v.tensor_copy(otile[:, c * 256:(c + 1) * 256],
                                      psB[c][0:64, :])
                block.vector(f_oev)

                def f_ost(sy, jg=jg, bp=bp, ov=sems.val("cvO")):
                    sems.wait(sy, "cvO", ov)
                    for pl in range(2):
                        dst = oex_in.ap()[8 * jg:8 * (jg + 1),
                                          2 * bp + pl, :].rearrange(
                            "c (a n) -> a c n", n=128)
                        src = otile[:, :].rearrange(
                            "p (c w n) -> p c w n", w=2, n=128)[:, :, pl, :]
                        sems.dma(sy.dma_start(out=dst, in_=src), "ost")
                block.sync(f_ost)

        def f_ag2(g, ost_all=sems.val("ost")):
            sems.wait(g, "ost", ost_all)
            sems.inc(g.collective_compute(
                "AllGather", mybir.AluOpType.bypass,
                replica_groups=[list(range(NCORES))],
                ins=[oex_in.ap().rearrange("j b n -> (j b) n").opt()],
                outs=[oex_out.ap().rearrange(
                    "r j b n -> (r j b) n").opt()]), "cc")
            g.wait_ge(sems.handle("cc"), sems.val("cc"))
            sems.dma(g.dma_start(
                out=wuv_s[:, 0:NCH * C].rearrange("p (ch i) -> p ch i", i=C),
                in_=wo_d.rearrange("(ch p) i -> p ch i", p=128)), "wgt2")
        block.gpsimd(f_ag2)
        WGT2 = sems.val("wgt2")

        scope2.close()
        obf = [ctx.enter_context(nc.sbuf_tensor(f"obf{i}", [128, TOK], BF16))
               for i in range(NCH)]
        utile = [ctx.enter_context(nc.sbuf_tensor(f"utile{i}", [128, TOK],
                                                  BF16)) for i in range(2)]
        oidx_s = ctx.enter_context(nc.sbuf_tensor("oidx_s", [128, 32], I32))
        oex2d = oex_out.ap().rearrange("r j b (lg n) -> (r j b lg) n",
                                       n=LC)

        CC2 = sems.val("cc")

        def f_ol(g):
            sems.wait(g, "cc", CC2)
            sems.dma(g.dma_start(out=oidx_s[:, :], in_=oidx_d), "old")
            g.wait_ge(sems.handle("old"), sems.val("old"))
            for ch in range(NCH):
                for b in range(B):
                    sems.dma(g.indirect_dma_start(
                        out=obf[ch][:, b * LC:(b + 1) * LC], out_offset=None,
                        in_=oex2d,
                        in_offset=IndirectOffsetOnAxis(
                            ap=oidx_s[:, ch * 4 + b:ch * 4 + b + 1],
                            axis=0)), "old")
        block.gpsimd(f_ol)
        OLD = sems.val("old")

        uld0 = [sems.val("uld0"), sems.val("uld1")]

        def f_ul(sy, zm0=sems.val("zme")):
            sems.wait(sy, "cc", CC2)
            for ch in range(NCH):
                if ch >= 2:
                    sems.wait(sy, "zme", zm0 + ch - 1)
                sems.dma(sy.dma_start(
                    out=utile[ch % 2][:, :],
                    in_=u_scr.ap()[128 * ch:128 * (ch + 1), :]),
                    "uld" + str(ch % 2))
        block.sync(f_ul)

        def f_zmul(v, old=OLD, uld0=uld0):
            sems.wait(v, "old", old)
            for ch in range(NCH):
                sems.wait(v, "uld" + str(ch % 2),
                          uld0[ch % 2] + 16 * (ch // 2 + 1))
                v.willinc("zme")
                v.tensor_mul(obf[ch][:, :], obf[ch][:, :],
                             utile[ch % 2][:, :])
        block.vector(f_zmul)
        ZME = sems.val("zme")

        for q in range(B):
            qsl = slice(q * QT, (q + 1) * QT)

            def f_lxq(sy, qsl=qsl, st_prev=sems.val("st"),
                      nm_all=sems.val("lnnm")):
                if q == 0:
                    sems.wait(sy, "lnnm", nm_all)
                else:
                    sems.wait(sy, "st", st_prev)
                for chk in range(NCH):
                    sems.dma(sy.dma_start(
                        out=xt[chk][:, :],
                        in_=x_d[128 * chk:128 * (chk + 1), qsl]), "xld2")
            block.sync(f_lxq)
            XLD2 = sems.val("xld2")

            op0 = sems.val("opm")
            xe0 = sems.val("xev")

            def f_oproj(tt, q=q, op0=op0, xe0=xe0, cvo=sems.val("cvO"),
                        rpv=sems.val("rpeV")):
                sems.wait(tt, "zme", ZME)
                sems.wait(tt, "wgt2", WGT2)
                if q == 0:
                    sems.wait(tt, "cvO", cvo)
                    sems.wait(tt, "rpeV", rpv)
                for co in range(NCH):
                    sems.wait(tt, "xev", xe0 + max(0, co - 1))
                    for cc in range(NCH):
                        w = wuv_s[:, cc * C + co * 128:cc * C + co * 128
                                  + 128]
                        for r in range(2):
                            ins = tt.matmul(
                                pp4[r if co % 2 == 0 else r + 2][:, :], w,
                                obf[cc][:, q * QT + r * 512:
                                        q * QT + (r + 1) * 512],
                                start=(cc == 0), stop=(cc == NCH - 1))
                    sems.inc(ins, "opm")
            block.tensor(f_oproj)

            def f_res(v, q=q, op0=op0, xld2=XLD2):
                sems.wait(v, "xld2", xld2)
                for co in range(NCH):
                    sems.wait(v, "opm", op0 + co + 1)
                    for r in range(2):
                        if r == 1:
                            v.willinc("xev")
                        so = slice(r * 512, (r + 1) * 512)
                        v.tensor_add(xt[co][:, so], xt[co][:, so],
                                     pp4[r if co % 2 == 0 else r + 2][:, :])
            block.vector(f_res)

            def f_st(sy, qsl=qsl, xe0=xe0):
                for co in range(NCH):
                    sems.wait(sy, "xev", xe0 + co + 1)
                    sems.dma(sy.dma_start(
                        out=xo_d[128 * co:128 * (co + 1), qsl],
                        in_=xt[co][:, :]), "st")
            block.sync(f_st)


def build_a1():
    return _standalone(lambda env, xi, xo: emit_a1(env, xi, xo), None)


def run_a1(x_parts, inputs):
    nc = _get_prog("a1", build_a1)
    in_maps = []
    for k in range(NCORES):
        d = a1_host_inputs(inputs, k)
        d["x"] = x_parts[k]
        in_maps.append(d)
    res = run_bass_kernel_spmd(nc, in_maps, core_ids=list(range(NCORES)))
    return [res.results[k]["xo"] for k in range(NCORES)]


# version 09 (final)
# speedup vs baseline: 1.1661x; 1.1661x over previous
import os

os.environ.setdefault("GAUGE_NTFF_IGNORE_DMA", "1")

import numpy as np
import concourse.bass as bass
import concourse.mybir as mybir
from concourse.bass_utils import run_bass_kernel_spmd
from concourse.alu_op_type import AluOpType as Op
from concourse.bass import IndirectOffsetOnAxis
from contextlib import ExitStack

F32 = mybir.dt.float32
BF16 = mybir.dt.bfloat16
I32 = mybir.dt.int32
AF = mybir.ActivationFunctionType
AX = mybir.AxisListType

NCORES = 8
B = 4
LFULL = 8192
C = 1024
LC = LFULL // NCORES
TOK = B * LC
NCH = C // 128
EPS = 1e-5
NHEAD = 16


def _rope_perm():
    perm = np.zeros(C, dtype=np.int64)
    for chk in range(NCH):
        for half in range(2):
            for hh in range(2):
                for j in range(32):
                    perm[128 * chk + 64 * half + 32 * hh + j] = (
                        128 * chk + 64 * hh + 2 * j + half)
    return perm


def _v_perm():
    perm = np.zeros(C, dtype=np.int64)
    for chk in range(NCH):
        for hh in range(2):
            for half in range(2):
                for j in range(32):
                    perm[128 * chk + 64 * hh + 32 * half + j] = (
                        128 * chk + 64 * hh + 2 * j + half)
    return perm


def _rope_cos_sin():
    scale = np.arange(0, 64, 2, dtype=np.float32)[:32]
    with np.errstate(over="ignore"):
        freqs = np.float32(1.0) / np.float32(10000.0) ** scale
    ang = np.arange(LFULL, dtype=np.float32)[:, None] * freqs[None, :]
    return np.cos(ang), np.sin(ang)


def shard_x(x):
    out = []
    for k in range(NCORES):
        sl = x[:, k * LC:(k + 1) * LC, :]
        out.append(np.ascontiguousarray(sl.transpose(2, 0, 1).reshape(C, TOK)))
    return out


def unshard_x(parts):
    xs = [np.asarray(p).reshape(C, B, LC).transpose(1, 2, 0) for p in parts]
    return np.concatenate(xs, axis=1)


def col_layout(v):
    v = np.asarray(v)
    r = v.shape[-1]
    return np.ascontiguousarray(v.reshape(r // 128, 128).T)


def a0_host_inputs(inputs, core):
    perm = _rope_perm()
    cos, sin = _rope_cos_sin()
    l0 = core * LC
    cos_c, sin_c = cos[l0:l0 + LC], sin[l0:l0 + LC]
    csfm = np.zeros((128, 2 * LC), np.float32)
    csfm[0:64, 0:LC] = cos_c[:, np.tile(np.arange(32), 2)].T
    csfm[64:128, 0:LC] = sin_c[:, np.tile(np.arange(32), 2)].T
    csfm[0:64, LC:2 * LC] = sin_c[:, np.tile(np.arange(32), 2)].T
    csfm[64:128, LC:2 * LC] = cos_c[:, np.tile(np.arange(32), 2)].T
    cst = np.concatenate([cos_c[:, np.tile(np.arange(32), 2)],
                          sin_c[:, np.tile(np.arange(32), 2)]], axis=1)
    cstok = np.ascontiguousarray(
        cst.reshape(LC // 128, 128, 128).transpose(1, 0, 2).reshape(128, LC))
    cones = np.ones((128, 128), np.float32)
    cones[1, 1] = EPS
    cones[1, 2] = -0.5
    vperm = _v_perm()
    d = {
        "wq": np.ascontiguousarray(inputs["L0_attn_Wq"][:, perm]),
        "wk": np.ascontiguousarray(inputs["L0_attn_Wk"][:, perm]),
        "wv": np.ascontiguousarray(inputs["L0_attn_Wv"][:, vperm]),
        "wo": np.ascontiguousarray(inputs["L0_attn_Wo"][vperm, :]),
        "bq": col_layout(inputs["L0_attn_bq"][perm]),
        "bo": col_layout(inputs["L0_attn_bo"]),
        "bkv": np.ascontiguousarray(np.stack([inputs["L0_attn_bk"][perm],
                                              inputs["L0_attn_bv"][vperm]])),
        "csfm": np.ascontiguousarray(csfm),
        "cstok": np.ascontiguousarray(cstok),
        "cones": cones,
    }
    return d



class Sems:

    GLOBAL = {"st", "cc", "cst", "cstg", "chv", "chs", "chg"}

    def __init__(self, nc, ctx):
        self.nc = nc
        self.ctx = ctx
        self.h = {}
        self.c = {}
        self.alias = None

    def begin_phase(self):
        self.alias = {}

    def _phys(self, name):
        if self.alias is None or name in self.GLOBAL:
            return name
        if name not in self.alias:
            self.alias[name] = "S%d" % len(self.alias)
        return self.alias[name]

    def _get(self, phys):
        if phys not in self.h:
            self.h[phys] = self.ctx.enter_context(self.nc.semaphore(phys))
            self.c[phys] = 0
        return self.h[phys]

    def handle(self, name):
        return self._get(self._phys(name))

    def inc(self, inst, name, amt=1):
        phys = self._phys(name)
        inst.then_inc(self._get(phys), amt)
        self.c[phys] += amt
        return self.c[phys]

    def dma(self, inst, name):
        return self.inc(inst, name, 16)

    def wait(self, eng, name, val):
        eng.wait_ge(self.handle(name), val)

    def val(self, name):
        phys = self._phys(name)
        self._get(phys)
        return self.c[phys]




CHAIN_OPS = {
    "tensor_copy", "tensor_tensor", "tensor_add", "tensor_sub", "tensor_mul",
    "tensor_max", "tensor_scalar", "tensor_single_scalar", "tensor_scalar_add",
    "tensor_scalar_sub", "tensor_scalar_mul", "tensor_scalar_max",
    "tensor_scalar_min", "tensor_relu", "tensor_reduce", "reduce_max",
    "reduce_sum", "scalar_tensor_tensor", "reciprocal", "memset", "memzero",
    "select", "copy_predicated", "stream_shuffle", "transpose", "iota",
    "activation", "copy", "add", "mul", "sqrt", "square", "sign",
    "bn_stats", "bn_aggr", "max", "max_index", "pool", "pool_avg", "pool_max",
}


class Chained:

    def __init__(self, eng, sems, name):
        self._e = eng
        self._s = sems
        self._n = name
        self._pend = None
        key = "_last_" + name
        if not hasattr(sems, key):
            setattr(sems, key, None)
        self._key = key

    def willinc(self, name, amt=1):
        self._pend = (name, amt)

    def __getattr__(self, attr):
        f = getattr(self._e, attr)
        if attr in CHAIN_OPS:
            def wrapped(*a, **k):
                last = getattr(self._s, self._key)
                if last is not None:
                    self._e.wait_ge(self._s._get(last[0]), last[1])
                ins = f(*a, **k)
                if self._pend is not None:
                    nm, amt = self._pend
                    self._pend = None
                else:
                    nm, amt = self._n, 1
                phys = self._s._phys(nm)
                v = self._s.inc(ins, nm, amt)
                setattr(self._s, self._key, (phys, v))
                return ins
            return wrapped
        return f


def chain_block(block, sems):
    def vec(f):
        block.vector(lambda v: f(Chained(v, sems, "chv")))

    def sca(f):
        block.scalar(lambda s: f(Chained(s, sems, "chs")))

    def gps(f):
        block.gpsimd(lambda g: f(Chained(g, sems, "chg")))

    class BB:
        vector = staticmethod(vec)
        scalar = staticmethod(sca)
        gpsimd = staticmethod(gps)
        tensor = staticmethod(block.tensor)
        sync = staticmethod(block.sync)
    return BB



def emit_ln(nc, block, sems, t, x_tiles, out_bf, ntok, xs0=0,
            gcol=None, bcol=None, out_f32=None, tag="ln", ld_edge=None,
            wr_edges=(), bc_edges=()):
    ngrp = ntok // 512
    sq2, rowtmp, lntmp = t["sq2"], t["rowtmp"], t["lntmp"]
    murow, vrow = t["murow"], t["vrow"]
    ps_sum, ps_sq = t["ps_sum"], t["ps_sq"]
    pbc0, pbc1 = t["ps_bc0"][:, :], t["ps_bc1"][:, :]
    SQ, MM, STA, STV, BC, NM = (tag + s for s in
                                ("sq", "mm", "sta", "stv", "bc", "nm"))
    sq0, mm0, sta0, stv0, bc0, nm0 = (sems.val(s) for s in
                                      (SQ, MM, STA, STV, BC, NM))
    inv_c = 1.0 / C

    cstv = sems.val("cst")

    def f_ac(s):
        sems.wait(s, "cst", cstv)
        if ld_edge is not None:
            sems.wait(s, *ld_edge)
        for g in range(ngrp):
            sl = slice(xs0 + g * 512, xs0 + (g + 1) * 512)
            for chk in range(NCH):
                k = g * NCH + chk
                if k >= 2:
                    kk = k - 2
                    sems.wait(s, MM, mm0 + kk + 1)
                s.willinc(SQ)
                s.activation(sq2[k % 2][:, :], x_tiles[chk][:, sl], AF.Square)
            sems.wait(s, STV, stv0 + g + 1)
            s.activation(rowtmp[0:1, :], vrow[0:1, :], AF.Ln, bias=EPS)
            s.willinc(STA)
            s.activation(vrow[0:1, :], rowtmp[0:1, :], AF.Exp, scale=-0.5)
    block.scalar(f_ac)

    def f_mm(tt):
        for e in bc_edges:
            sems.wait(tt, *e)
        for g in range(ngrp):
            sl = slice(xs0 + g * 512, xs0 + (g + 1) * 512)
            if g >= 1:
                sems.wait(tt, STV, stv0 + g)
            for chk in range(NCH):
                k = g * NCH + chk
                sems.wait(tt, SQ, sq0 + k + 1)
                tt.matmul(ps_sum[:, :], t["ones_col"][:, 0:1],
                          x_tiles[chk][:, sl],
                          start=(chk == 0), stop=(chk == NCH - 1))
                sems.inc(tt.matmul(ps_sq[:, :], t["ones_col"][:, 0:1],
                                   sq2[k % 2][:, :],
                                   start=(chk == 0), stop=(chk == NCH - 1)), MM)
            sems.wait(tt, STA, sta0 + g + 1)
            if g >= 1:
                sems.wait(tt, NM, nm0 + g)
            tt.matmul(pbc0, t["ones_row"][:, :], murow[0:1, :],
                      start=True, stop=True)
            sems.inc(tt.matmul(pbc1, t["ones_row"][:, :],
                               vrow[0:1, :], start=True, stop=True), BC)
    block.tensor(f_mm)

    def f_ve(v):
        for e in wr_edges:
            sems.wait(v, *e)
        for g in range(ngrp):
            sl = slice(xs0 + g * 512, xs0 + (g + 1) * 512)
            so = slice(g * 512, (g + 1) * 512)
            sems.wait(v, MM, mm0 + g * NCH + NCH)
            v.tensor_scalar_mul(murow[0:1, :], ps_sum[:, :], inv_c)
            v.tensor_scalar_mul(vrow[0:1, :], ps_sq[:, :], inv_c)
            v.tensor_mul(rowtmp[0:1, :], murow[0:1, :], murow[0:1, :])
            v.willinc(STV)
            v.tensor_sub(vrow[0:1, :], vrow[0:1, :], rowtmp[0:1, :])
            sems.wait(v, BC, bc0 + g + 1)
            for chk in range(NCH):
                last_of_group = (chk == NCH - 1)
                v.tensor_sub(lntmp[:, :], x_tiles[chk][:, sl], pbc0)
                if gcol is None and out_f32 is None:
                    if last_of_group:
                        v.willinc(NM)
                    v.tensor_mul(out_bf[chk][:, so], lntmp[:, :], pbc1)
                elif gcol is None:
                    v.tensor_mul(out_f32[chk][:, so], lntmp[:, :], pbc1)
                    if last_of_group:
                        v.willinc(NM)
                    v.tensor_copy(out_bf[chk][:, so], out_f32[chk][:, so])
                else:
                    v.tensor_mul(lntmp[:, :], lntmp[:, :], pbc1)
                    if out_f32 is not None and out_bf is not None:
                        v.tensor_scalar(out_f32[chk][:, so], lntmp[:, :],
                                        gcol[:, chk:chk + 1],
                                        bcol[:, chk:chk + 1], Op.mult, Op.add)
                        if last_of_group:
                            v.willinc(NM)
                        v.tensor_copy(out_bf[chk][:, so], out_f32[chk][:, so])
                    elif out_bf is None:
                        if last_of_group:
                            v.willinc(NM)
                        v.tensor_scalar(out_f32[chk][:, so], lntmp[:, :],
                                        gcol[:, chk:chk + 1],
                                        bcol[:, chk:chk + 1], Op.mult, Op.add)
                    else:
                        if last_of_group:
                            v.willinc(NM)
                        v.tensor_scalar(out_bf[chk][:, so], lntmp[:, :],
                                        gcol[:, chk:chk + 1],
                                        bcol[:, chk:chk + 1],
                                        Op.mult, Op.add)
    block.vector(f_ve)
    return (NM, sems.val(NM))


def alloc_ln_scratch(nc, ctx, t):
    t["cvals"] = ctx.enter_context(nc.sbuf_tensor("cvals", [1, 3], F32))
    nc.const_aps.register(F32, EPS, t["cvals"][0:1, 0:1])
    nc.const_aps.register(F32, -0.5, t["cvals"][0:1, 1:2])
    nc.const_aps.register(F32, 0.125, t["cvals"][0:1, 2:3])
    t["sq2"] = [ctx.enter_context(nc.sbuf_tensor(f"sq{i}", [128, 512], F32))
                for i in range(2)]
    t["murow"] = ctx.enter_context(nc.sbuf_tensor("murow", [1, 512], F32))
    t["vrow"] = ctx.enter_context(nc.sbuf_tensor("vrow", [1, 512], F32))
    t["rowtmp"] = ctx.enter_context(nc.sbuf_tensor("rowtmp", [1, 512], F32))
    t["lntmp"] = ctx.enter_context(nc.sbuf_tensor("lntmp", [128, 512], F32))
    t["ones_col"] = ctx.enter_context(nc.sbuf_tensor("ones_col", [128, 1], F32))
    t["ones_row"] = ctx.enter_context(nc.sbuf_tensor("ones_row", [1, 128], F32))
    t["ones_row_bf"] = ctx.enter_context(
        nc.sbuf_tensor("ones_row_bf", [1, 128], BF16))
    t["psf"] = [ctx.enter_context(nc.psum_tensor(f"psf{i}", [128, 512], F32))
                for i in range(4)]
    t["ps_sum"] = t["psf"][0][0:1, :]
    t["ps_sq"] = t["psf"][1][0:1, :]
    t["ps_bc0"] = t["psf"][2]
    t["ps_bc1"] = t["psf"][3]


def emit_const_loads(block, sems, t, cones_d, env=None):
    if env is not None:
        if env.consts_loaded:
            return
        env.consts_loaded = True

    def f(sy):
        sems.dma(sy.dma_start(out=t["cvals"][:, :], in_=cones_d[1:2, 1:4]),
                 "cst")
        sems.dma(sy.dma_start(out=t["ones_col"][:, :], in_=cones_d[:, 0:1],
                              allow_slow_non_contiguous=True), "cst")
        sems.dma(sy.dma_start(out=t["ones_row"][:, :], in_=cones_d[0:1, :]),
                 "cst")
    block.sync(f)

    def g(gg):
        sems.dma(gg.dma_start(out=t["ones_row_bf"][:, :], in_=cones_d[0:1, :]),
                 "cstg")
    block.gpsimd(g)




class Env:

    def __init__(self, nc, ctx, sems, t, block, pp4):
        self.nc = nc
        self.ctx = ctx
        self.sems = sems
        self.t = t
        self.block = block
        self.pp4 = pp4
        self.dcache = {}
        self.nphase = 0
        self.pfx = ""
        self.consts_loaded = False

    def enter_phase(self):
        self.pfx = f"ph{self.nphase}_"
        self.nphase += 1
        self.sems.begin_phase()
        if self.nphase > 1:
            st0 = self.sems.val("st")
            for sec in (self.block.vector, self.block.scalar,
                        self.block.gpsimd, self.block.tensor,
                        self.block.sync):
                sec(lambda e, st0=st0: self.sems.wait(e, "st", st0))

    def D(self, name, shape, dt=F32, kind="ExternalInput"):
        if name not in self.dcache:
            self.dcache[name] = self.nc.dram_tensor(name, list(shape), dt,
                                                    kind=kind)
        return self.dcache[name]

    def Dap(self, name, shape, dt=F32, kind="ExternalInput"):
        return self.D(name, shape, dt, kind).ap()


def make_env(nc, ctx):
    sems = Sems(nc, ctx)
    t = {}
    alloc_ln_scratch(nc, ctx, t)
    pp4 = [ctx.enter_context(nc.psum_tensor(f"pp4_{i}", [128, 512], F32))
           for i in range(4)]
    block = chain_block(ctx.enter_context(nc.Block()), sems)
    return Env(nc, ctx, sems, t, block, pp4)


def _standalone(emit, host_fn, extra=None):
    nc = bass.Bass("TRN2", target_bir_lowering=False, debug=False)
    with ExitStack() as ctx:
        env = make_env(nc, ctx)
        x_in = env.Dap("x", [C, TOK], F32, "ExternalInput")
        x_out = env.Dap("xo", [C, TOK], F32, "ExternalOutput")
        emit(env, x_in, x_out)

        def f_fin(g, env=env):
            env.sems.wait(g, "st", env.sems.val("st"))
        env.block.gpsimd(f_fin)
    return nc


def emit_a0(env, x_d, xo_d):
    env.enter_phase()
    nc, sems, t, block, pp4 = env.nc, env.sems, env.t, env.block, env.pp4
    wq_d = env.Dap("wq", [C, C])
    wk_d = env.Dap("wk", [C, C])
    wv_d = env.Dap("wv", [C, C])
    wo_d = env.Dap("wo", [C, C])
    bq_d = env.Dap("bq", [128, NCH])
    bo_d = env.Dap("bo", [128, NCH])
    bkv_d = env.Dap("bkv", [2, C])
    csfm_d = env.Dap("csfm", [128, 2 * LC])
    cstok_d = env.Dap("cstok", [128, LC])
    cones_d = env.Dap("cones", [128, 128])

    kv_in = env.D("kv_in", [64, B * NHEAD * 64], F32, "Internal")
    kv_out = env.D("kv_out", [64, B * NHEAD * 64], F32, "Internal")
    q_scr = env.D("q_scr", [C, TOK], BF16, "Internal")
    ST0 = sems.val("st")

    with ExitStack() as ctx:
        def sb(name, shape, dt=F32):
            return ctx.enter_context(nc.sbuf_tensor(env.pfx + name, shape,
                                                    dt))

        xt = [sb(f"xt{i}", [128, LC]) for i in range(NCH)]
        lnx = [sb(f"lnx{i}", [128, LC], BF16) for i in range(NCH)]
        wq_s = sb("wq_s", [128, NCH * C], BF16)
        wk_s = sb("wk_s", [128, NCH * C], BF16)
        wv_s = sb("wv_s", [128, NCH * C], BF16)
        bq_s = sb("bq_s", [128, NCH])
        bo_s = sb("bo_s", [128, NCH])
        bk_s = sb("bk_s", [1, C], BF16)
        bv_s = sb("bv_s", [1, C], BF16)
        csfm_s = sb("csfm_s", [128, 2 * LC])
        cstok_s = sb("cstok_s", [128, LC])
        qf32 = sb("qf32", [128, 512])
        rt1 = sb("rt1", [64, 512])
        rt2 = sb("rt2", [64, 512])
        qst = sb("qst", [128, LC], BF16)
        kf32 = sb("kf32", [128, C])
        st1 = sb("st1", [128, 512])
        st2 = sb("st2", [128, 512])
        kbf = sb("kbf", [128, C], BF16)
        vbf = sb("vbf", [128, C], BF16)
        kvsb = sb("kvsb", [64, B * NHEAD * 64])
        kv2 = sb("kv2", [128, B * NCH * 128], BF16)
        qld = [sb(f"qld{i}", [128, LC], BF16) for i in range(NCH)]
        ctxs = [sb(f"ctx{i}", [128, LC], BF16) for i in range(NCH)]

        pk = [t["ps_bc0"][:, :], t["ps_bc1"][:, :]]
        pv = [pp4[0][:, :], pp4[1][:, :]]
        pkv = [pp4[2][0:64, :], pp4[3][0:64, :]]

        emit_const_loads(block, sems, t, cones_d, env)

        def f_ld0(sy):
            sems.wait(sy, "st", ST0)
            sems.dma(sy.dma_start(out=bq_s[:, :], in_=bq_d), "cst")
            sems.dma(sy.dma_start(out=bo_s[:, :], in_=bo_d), "cst")
            sems.dma(sy.dma_start(out=csfm_s[:, :], in_=csfm_d), "cst")
            sems.dma(sy.dma_start(out=cstok_s[:, :], in_=cstok_d), "cst")
        block.sync(f_ld0)

        def f_wd0(g):
            sems.wait(g, "st", ST0)
            for w_d, w_s in ((wq_d, wq_s), (wk_d, wk_s), (wv_d, wv_s)):
                sems.dma(g.dma_start(
                    out=w_s[:, :].rearrange("p (ch i) -> p ch i", i=C),
                    in_=w_d.rearrange("(ch p) i -> p ch i", p=128)), "wgt")
            sems.dma(g.dma_start(out=bk_s[:, :], in_=bkv_d[0:1, :]), "wgt")
            sems.dma(g.dma_start(out=bv_s[:, :], in_=bkv_d[1:2, :]), "wgt")
        block.gpsimd(f_wd0)
        WGT = sems.val("wgt")
        CST = sems.val("cst")

        for q in range(B):
            qsl = slice(q * LC, (q + 1) * LC)

            prev_nm = sems.val("lnnm")

            def f_ldx(sy, q=q, qsl=qsl, prev_nm=prev_nm):
                if q >= 1:
                    sems.wait(sy, "lnnm", prev_nm)
                else:
                    sems.wait(sy, "st", ST0)
                for chk in range(NCH):
                    sems.dma(sy.dma_start(
                        out=xt[chk][:, :],
                        in_=x_d[128 * chk:128 * (chk + 1), qsl]), "xld")
            block.sync(f_ldx)

            wr = [("qpsum", sems.val("qpsum")), ("kpsum", sems.val("kpsum"))]
            bc = [("kevac", sems.val("kevac"))]
            nm_edge = emit_ln(nc, block, sems, t, xt, lnx, LC, tag="ln",
                              ld_edge=("xld", sems.val("xld")), wr_edges=wr,
                              bc_edges=bc)

            QP, QE = "qpsum", "qevac"
            qp0, qe0 = sems.val(QP), sems.val(QE)
            qs0 = sems.val("qst")

            def f_q(tt, nm_edge=nm_edge, qp0=qp0, qe0=qe0,
                    ke_prev=sems.val("kevac")):
                sems.wait(tt, "wgt", WGT)
                sems.wait(tt, "cst", CST)
                sems.wait(tt, "kevac", ke_prev)
                sems.wait(tt, *nm_edge)
                for co in range(NCH):
                    if co >= 1:
                        sems.wait(tt, QE, qe0 + 2 * co)
                    for cc in range(NCH):
                        w = wq_s[:, cc * C + co * 128:cc * C + co * 128 + 128]
                        for tc in range(2):
                            ins = tt.matmul(pv[tc], w,
                                            lnx[cc][:, tc * 512:(tc + 1) * 512],
                                            start=(cc == 0),
                                            stop=(cc == NCH - 1))
                    sems.inc(ins, QP)
            block.tensor(f_q)

            def f_qev(v, qp0=qp0, qs0=qs0):
                for co in range(NCH):
                    sems.wait(v, QP, qp0 + co + 1)
                    sems.wait(v, "qst", qs0 + 16 * co)
                    for tc in range(2):
                        tsl = slice(tc * 512, (tc + 1) * 512)
                        v.tensor_scalar(qf32[:, :], pv[tc],
                                        bq_s[:, co:co + 1], None, Op.add)
                        cosf = csfm_s[0:64, tsl]
                        sinf_hi = csfm_s[64:128, tsl]
                        sinf_lo = csfm_s[0:64, LC + tc * 512:LC + (tc + 1) * 512]
                        cosf_hi = csfm_s[64:128, LC + tc * 512:LC + (tc + 1) * 512]
                        qr, qi = qf32[0:64, :], qf32[64:128, :]
                        v.tensor_mul(rt1[:, :], qr, cosf)
                        v.tensor_mul(rt2[:, :], qi, sinf_hi)
                        v.tensor_sub(qst[0:64, tsl], rt1[:, :], rt2[:, :])
                        v.tensor_mul(rt1[:, :], qr, sinf_lo)
                        v.tensor_mul(rt2[:, :], qi, cosf_hi)
                        v.willinc(QE)
                        v.tensor_add(qst[64:128, tsl], rt1[:, :], rt2[:, :])
            block.vector(f_qev)

            def f_qst(sy, q=q, qe0=qe0):
                for co in range(NCH):
                    sems.wait(sy, "qevac", qe0 + 2 * (co + 1))
                    sems.dma(sy.dma_start(
                        out=q_scr.ap()[128 * co:128 * (co + 1), qsl],
                        in_=qst[:, :]), "qst")
            block.sync(f_qst)

            KP, KE, KV, KVE = "kpsum", "kevac", "kvgrp", "kvevac"
            kp0, ke0, kv0, kve0 = (sems.val(s) for s in (KP, KE, KV, KVE))
            QE_NOW = sems.val(QE)

            def f_kv_mm(tt, q=q, kp0=kp0, ke0=ke0, kv0=kv0, nm_edge=nm_edge):
                sems.wait(tt, "cstg", sems.val("cstg"))
                sems.wait(tt, *nm_edge)
                sems.wait(tt, QE, QE_NOW)
                if q >= 1:
                    sems.wait(tt, KVE, q)
                for ti in range(8):
                    if ti >= 1:
                        sems.wait(tt, KE, ke0 + 2 * ti)
                    for cc in range(NCH):
                        lhs = lnx[cc][:, ti * 128:(ti + 1) * 128]
                        tt.matmul(pk[0], lhs, wk_s[:, cc * C:cc * C + 512],
                                  start=(cc == 0), stop=False)
                        tt.matmul(pk[1], lhs,
                                  wk_s[:, cc * C + 512:cc * C + 1024],
                                  start=(cc == 0), stop=False)
                        tt.matmul(pv[0], lhs, wv_s[:, cc * C:cc * C + 512],
                                  start=(cc == 0), stop=False)
                        tt.matmul(pv[1], lhs,
                                  wv_s[:, cc * C + 512:cc * C + 1024],
                                  start=(cc == 0), stop=False)
                    tt.matmul(pk[0], t["ones_row_bf"][:, :],
                              bk_s[0:1, 0:512], start=False, stop=True)
                    tt.matmul(pk[1], t["ones_row_bf"][:, :],
                              bk_s[0:1, 512:1024], start=False, stop=True)
                    tt.matmul(pv[0], t["ones_row_bf"][:, :],
                              bv_s[0:1, 0:512], start=False, stop=True)
                    sems.inc(tt.matmul(pv[1], t["ones_row_bf"][:, :],
                                       bv_s[0:1, 512:1024], start=False,
                                       stop=True), KP)
                    sems.wait(tt, KE, ke0 + 2 * ti + 2)
                    for h in range(NHEAD):
                        chk, hh = h // 2, h % 2
                        base = 128 * chk + 64 * hh
                        kap = kbf[:, base:base + 64]
                        vap = vbf[:, base:base + 64]
                        ins = tt.matmul(
                            pkv[h // 8][:, (h % 8) * 64:(h % 8) * 64 + 64],
                            kap, vap, start=(ti == 0 and h % 8 == 0),
                            stop=(ti == 7 and h % 8 == 7),
                            skip_group_check=True)
                    sems.inc(ins, KV)
            block.tensor(f_kv_mm)

            def f_kv_ev(v, q=q, kp0=kp0, kv0=kv0):
                for ti in range(8):
                    sems.wait(v, KP, kp0 + ti + 1)
                    if ti >= 1:
                        sems.wait(v, KV, kv0 + ti)
                    cosv = cstok_s[:, ti * 128:ti * 128 + 32] \
                        .unsqueeze(1).unsqueeze(1) \
                        .broadcast_to([128, NCH, 2, 32])
                    sinv = cstok_s[:, ti * 128 + 64:ti * 128 + 96] \
                        .unsqueeze(1).unsqueeze(1) \
                        .broadcast_to([128, NCH, 2, 32])
                    v.tensor_copy(kf32[:, 0:512], pk[0])
                    v.tensor_copy(kf32[:, 512:1024], pk[1])
                    kr = kf32[:, :].rearrange("p (ch h j) -> p ch h j",
                                              h=2, j=32)
                    krv = kf32[:, :].rearrange("p (ch a) -> p ch a", a=128)[
                        :, :, 0:64].rearrange("p ch (h j) -> p ch h j", j=32)
                    kiv = kf32[:, :].rearrange("p (ch a) -> p ch a", a=128)[
                        :, :, 64:128].rearrange("p ch (h j) -> p ch h j", j=32)
                    kbb = kbf[:, :].rearrange("p (ch h a) -> p ch h a",
                                              h=2, a=64)
                    kbr_o = kbb[:, :, :, 0:32]
                    kbi_o = kbb[:, :, :, 32:64]
                    s1 = st1[:, :].rearrange("p (ch h j) -> p ch h j",
                                             h=2, j=32)
                    s2 = st2[:, :].rearrange("p (ch h j) -> p ch h j",
                                             h=2, j=32)
                    v.tensor_mul(s1, krv, cosv)
                    v.tensor_mul(s2, kiv, sinv)
                    v.tensor_sub(kbr_o, s1, s2)
                    v.tensor_mul(s1, krv, sinv)
                    v.tensor_mul(s2, kiv, cosv)
                    v.willinc(KE)
                    v.tensor_add(kbi_o, s1, s2)
                    v.tensor_copy(vbf[:, 0:512], pv[0])
                    v.willinc(KE)
                    v.tensor_copy(vbf[:, 512:1024], pv[1])
                    if ti == 7:
                        sems.wait(v, KV, kv0 + 8)
                        v.tensor_scalar_mul(
                            kvsb[:, q * 1024:q * 1024 + 512],
                            pkv[0][:, :], 0.125)
                        v.willinc(KVE)
                        v.tensor_scalar_mul(
                            kvsb[:, q * 1024 + 512:(q + 1) * 1024],
                            pkv[1][:, :], 0.125)
            block.vector(f_kv_ev)

        KVE_ALL = sems.val("kvevac")
        KP_ALL = sems.val("kpsum")
        QP_ALL = sems.val("qpsum")

        def f_cc(g):
            sems.wait(g, "kvevac", KVE_ALL)
            sems.dma(g.dma_start(out=kv_in.ap(), in_=kvsb[:, :]), "ccd")
            g.wait_ge(sems.handle("ccd"), sems.val("ccd"))
            sems.inc(g.collective_compute(
                "AllReduce", mybir.AluOpType.add,
                replica_groups=[list(range(NCORES))],
                ins=[kv_in.ap().opt()], outs=[kv_out.ap().opt()]), "cc")
            g.wait_ge(sems.handle("cc"), sems.val("cc"))
            sems.dma(g.dma_start(out=kvsb[:, :], in_=kv_out.ap()), "ccd")
            sems.wait(g, "kpsum", KP_ALL)
            sems.wait(g, "qpsum", QP_ALL)
            sems.dma(g.dma_start(
                out=wq_s[:, :].rearrange("p (ch i) -> p ch i", i=C),
                in_=wo_d.rearrange("(ch p) i -> p ch i", p=128)), "wgt2")
        block.gpsimd(f_cc)
        CCD = sems.val("ccd")
        WGT2 = sems.val("wgt2")

        def f_kv2(v):
            sems.wait(v, "ccd", CCD)
            v.memset(kv2[:, :], 0.0)
            for bb in range(B):
                for chk in range(NCH):
                    for hh in range(2):
                        h = 2 * chk + hh
                        for ri in range(2):
                            for rj in range(2):
                                src = kvsb[32 * ri:32 * ri + 32,
                                           bb * 1024 + h * 64 + 32 * rj:
                                           bb * 1024 + h * 64 + 32 * rj + 32]
                                dst = kv2[64 * ri + 32 * hh:
                                          64 * ri + 32 * hh + 32,
                                          (bb * NCH + chk) * 128 + 64 * hh
                                          + 32 * rj:
                                          (bb * NCH + chk) * 128 + 64 * hh
                                          + 32 * rj + 32]
                                last = (bb == B - 1 and chk == NCH - 1
                                        and hh == 1 and ri == 1 and rj == 1)
                                if last:
                                    v.willinc("kv2e")
                                v.tensor_copy(dst, src)
        block.vector(f_kv2)
        KV2E = sems.val("kv2e")
        QST_ALL = sems.val("qst")

        for q in range(B):
            qsl = slice(q * LC, (q + 1) * LC)
            CP, CE, OP_, XE = "cpsum", "cevac", "opsum", "xevac"
            cp0, ce0, op0, xe0 = (sems.val(s) for s in (CP, CE, OP_, XE))

            def f_ldq(sy, q=q, qsl=qsl, xe_prev=sems.val("xevac"),
                      st_prev=sems.val("st"), nm_all=sems.val("lnnm")):
                if q == 0:
                    sems.wait(sy, "lnnm", nm_all)
                else:
                    sems.wait(sy, "st", st_prev)
                for chk in range(NCH):
                    sems.dma(sy.dma_start(
                        out=xt[chk][:, :],
                        in_=x_d[128 * chk:128 * (chk + 1), qsl]), "xld2")
                    sems.dma(sy.dma_start(
                        out=qld[chk][:, :],
                        in_=q_scr.ap()[128 * chk:128 * (chk + 1), qsl]),
                        "xld2")
            block.sync(f_ldq)
            XLD2 = sems.val("xld2")

            def f_ctx(tt, q=q, cp0=cp0, ce0=ce0,
                      xe_prev=sems.val("xevac")):
                sems.wait(tt, "kv2e", KV2E)
                sems.wait(tt, "qst", QST_ALL)
                sems.wait(tt, "xld2", XLD2)
                if q >= 1:
                    sems.wait(tt, "opsum", sems.val("opsum"))
                    sems.wait(tt, "xevac", xe_prev)
                for chk in range(NCH):
                    if chk >= 2:
                        sems.wait(tt, CE, ce0 + chk - 1)
                    w = kv2[:, (q * NCH + chk) * 128:(q * NCH + chk) * 128
                            + 128]
                    for tc in range(2):
                        ins = tt.matmul(pv[tc] if chk % 2 == 0 else pk[tc], w,
                                        qld[chk][:, tc * 512:(tc + 1) * 512],
                                        start=True, stop=True)
                    sems.inc(ins, CP)
            block.tensor(f_ctx)

            def f_ctx_ev(v, cp0=cp0):
                for chk in range(NCH):
                    sems.wait(v, CP, cp0 + chk + 1)
                    for tc in range(2):
                        so = slice(tc * 512, (tc + 1) * 512)
                        if tc == 1:
                            v.willinc(CE)
                        v.tensor_copy(ctxs[chk][:, so],
                                      pv[tc] if chk % 2 == 0 else pk[tc])
            block.vector(f_ctx_ev)
            CE_ALL = sems.val("cevac")

            def f_oproj(tt, q=q, op0=op0, xe0=xe0):
                sems.wait(tt, "cevac", CE_ALL)
                sems.wait(tt, "wgt2", WGT2)
                for co in range(NCH):
                    if co >= 2:
                        sems.wait(tt, "xevac", xe0 + co - 1)
                    for cc in range(NCH):
                        w = wq_s[:, cc * C + co * 128:cc * C + co * 128 + 128]
                        for tc in range(2):
                            ins = tt.matmul(
                                pv[tc] if co % 2 == 0 else pk[tc], w,
                                ctxs[cc][:, tc * 512:(tc + 1) * 512],
                                start=(cc == 0), stop=(cc == NCH - 1))
                    sems.inc(ins, OP_)
            block.tensor(f_oproj)

            def f_resid(v, op0=op0):
                for co in range(NCH):
                    sems.wait(v, OP_, op0 + co + 1)
                    for tc in range(2):
                        if tc == 1:
                            v.willinc("xevac")
                        v.scalar_tensor_tensor(
                            xt[co][:, tc * 512:(tc + 1) * 512],
                            pv[tc] if co % 2 == 0 else pk[tc],
                            bo_s[:, co:co + 1],
                            xt[co][:, tc * 512:(tc + 1) * 512],
                            Op.add, Op.add)
            block.vector(f_resid)

            def f_store(sy, qsl=qsl, xe0=xe0):
                for co in range(NCH):
                    sems.wait(sy, "xevac", xe0 + co + 1)
                    sems.dma(sy.dma_start(
                        out=xo_d[128 * co:128 * (co + 1), qsl],
                        in_=xt[co][:, :]), "st")
            block.sync(f_store)


def build_a0():
    return _standalone(lambda env, xi, xo: emit_a0(env, xi, xo), None)



_prog_cache = {}


def _get_prog(name, builder):
    if name not in _prog_cache:
        _prog_cache[name] = builder()
    return _prog_cache[name]


def run_a0(x_parts, inputs):
    nc = _get_prog("a0", build_a0)
    in_maps = []
    for k in range(NCORES):
        d = a0_host_inputs(inputs, k)
        d["x"] = x_parts[k]
        in_maps.append(d)
    res = run_bass_kernel_spmd(nc, in_maps, core_ids=list(range(NCORES)))
    return [res.results[k]["xo"] for k in range(NCORES)]


def build_fused():
    nc = bass.Bass("TRN2", target_bir_lowering=False, debug=False)
    with ExitStack() as ctx:
        env = make_env(nc, ctx)
        x_in = env.Dap("x", [C, TOK], F32, "ExternalInput")
        x_out = env.Dap("xo", [C, TOK], F32, "ExternalOutput")
        xs = [env.Dap(f"xs{i}", [C, TOK], F32, "Internal") for i in range(5)]
        seq = [x_in] + xs + [x_out]
        emit_f0(env, seq[0], seq[1])
        emit_a0(env, seq[1], seq[2])
        emit_f0(env, seq[2], seq[3])
        emit_f1(env, seq[3], seq[4])
        emit_a1(env, seq[4], seq[5])
        emit_f1(env, seq[5], seq[6])

        def f_fin(g):
            env.sems.wait(g, "st", env.sems.val("st"))
        env.block.gpsimd(f_fin)
    return nc


def kernel(**inputs):
    x = np.asarray(inputs["L0_attn_x"], np.float32)
    parts = shard_x(x)
    nc = _get_prog("fused", build_fused)
    in_maps = []
    for k in range(NCORES):
        d = {}
        d.update(f0_host_inputs(inputs, k))
        d.update(a0_host_inputs(inputs, k))
        d.update(f1_host_inputs(inputs, k))
        d.update(a1_host_inputs(inputs, k))
        d["x"] = parts[k]
        in_maps.append(d)
    res = run_bass_kernel_spmd(nc, in_maps, core_ids=list(range(NCORES)))
    return unshard_x([res.results[k]["xo"]
                      for k in range(NCORES)]).astype(np.float32)


def kernel_unfused(**inputs):
    x = np.asarray(inputs["L0_attn_x"], np.float32)
    parts = shard_x(x)
    parts = run_f0(parts, inputs)
    parts = run_a0(parts, inputs)
    parts = run_f0(parts, inputs)
    parts = run_f1(parts, inputs)
    parts = run_a1(parts, inputs)
    parts = run_f1(parts, inputs)
    return unshard_x(parts).astype(np.float32)


EXH = 2816
SHH = 1536
NEX = 8


def f1_host_inputs(inputs, core):
    cones = np.ones((128, 128), np.float32)
    cones[1, 1] = EPS
    cones[1, 2] = -0.5
    return {
        "gate_wt": np.ascontiguousarray(np.asarray(inputs["L1_ffn_gate_w"]).T),
        "w1": np.ascontiguousarray(inputs["L1_ffn_w1"]),
        "w2": np.ascontiguousarray(inputs["L1_ffn_w2"]),
        "w3": np.ascontiguousarray(inputs["L1_ffn_w3"]),
        "sw1": np.ascontiguousarray(inputs["L1_ffn_sw1"]),
        "sw2": np.ascontiguousarray(inputs["L1_ffn_sw2"]),
        "sw3": np.ascontiguousarray(inputs["L1_ffn_sw3"]),
        "cones": cones,
    }


def emit_f1(env, x_d, xo_d):
    env.enter_phase()
    nc, sems, t, block, pp4 = env.nc, env.sems, env.t, env.block, env.pp4
    gwt_d = env.Dap("gate_wt", [C, NEX])
    w1_d = env.Dap("w1", [NEX, C, EXH])
    w2_d = env.Dap("w2", [NEX, EXH, C])
    w3_d = env.Dap("w3", [NEX, C, EXH])
    sw1_d = env.Dap("sw1", [C, SHH])
    sw2_d = env.Dap("sw2", [SHH, C])
    sw3_d = env.Dap("sw3", [C, SHH])
    cones_d = env.Dap("cones", [128, 128])

    QT = 1024
    NHG = EXH // 128
    NSG = SHH // 128
    ST0 = sems.val("st")

    with ExitStack() as ctx:
        def sb(name, shape, dt=F32):
            return ctx.enter_context(nc.sbuf_tensor(env.pfx + name, shape,
                                                    dt))

        xt = [sb(f"xt{i}", [128, QT]) for i in range(NCH)]
        lny = [sb(f"lny{i}", [128, QT], BF16) for i in range(NCH)]
        yac = [sb(f"yac{i}", [128, QT]) for i in range(NCH)]
        lnf = yac
        he = [sb(f"he{i}", [128, QT], BF16) for i in range(NHG)]
        gwt_s = sb("gwt_s", [128, NCH * NEX])
        lgt = sb("lgt", [32, QT])
        srow = sb("srow", [32, QT])
        prb = sb("prb", [32, QT])
        combm = sb("combm", [32, QT])
        combr_bf = sb("combr_bf", [1, QT], BF16)
        iocol = sb("iocol", [32, 1])
        wt1 = [sb(f"wt1_{i}", [128, NCH * 128], BF16) for i in range(3)]
        wt3 = [sb(f"wt3_{i}", [128, NCH * 128], BF16) for i in range(3)]
        wt2 = [sb(f"wt2_{i}", [128, NHG * 128], BF16) for i in range(2)]
        wt2f = [sb(f"wt2f_{i}", [128, NHG * 128]) for i in range(2)]
        s1t = [sb(f"s1t{i}", [128, 512]) for i in range(2)]
        s2t = t["lntmp"]

        z1p = [t["ps_bc0"][:, :], t["ps_bc1"][:, :]]
        z3p = [pp4[0][:, :], pp4[1][:, :]]
        z3o = [t["psf"][0][:, :], t["psf"][1][:, :]]
        cbp = [pp4[2][:, :], pp4[3][:, :]]

        emit_const_loads(block, sems, t, cones_d, env)
        CSTG = sems.val("cstg")

        def f_ld0(g):
            sems.wait(g, "st", ST0)
            sems.dma(g.dma_start(
                out=gwt_s[:, :].rearrange("p (ch i) -> p ch i", i=NEX),
                in_=gwt_d.rearrange("(ch p) i -> p ch i", p=128)), "wgt")

            g.willinc("iot")
            g.iota(iocol[:, :], pattern=[[0, 1]], base=0, channel_multiplier=1,
                   allow_small_or_imprecise_dtypes=True)
        block.gpsimd(f_ld0)
        WGT = sems.val("wgt")
        IOT = sems.val("iot")

        experts = [(w1_d[e], w3_d[e], w2_d[e], NHG, e) for e in range(NEX)]
        experts.append((sw1_d, sw3_d, sw2_d, NSG, None))

        for q in range(B):
            qsl = slice(q * QT, (q + 1) * QT)

            def f_ldx(sy, q=q, qsl=qsl, st_prev=max(sems.val("st"), ST0)):
                sems.wait(sy, "st", st_prev)
                for chk in range(NCH):
                    sems.dma(sy.dma_start(
                        out=xt[chk][:, :],
                        in_=x_d[128 * chk:128 * (chk + 1), qsl]), "xld")
            block.sync(f_ldx)

            wr = [("zmm", sems.val("zmm")), ("gmm", sems.val("gmm")),
                  ("hev", sems.val("hev"))]
            nm_edge = emit_ln(nc, block, sems, t, xt, lny, QT, tag="ln",
                              ld_edge=("xld", sems.val("xld")), wr_edges=wr,
                              bc_edges=[("hev", sems.val("hev")),
                                        ("yev", sems.val("yev"))],
                              out_f32=lnf)

            GM = "gmm"
            gm0 = sems.val(GM)

            def f_gate_mm(tt, nm_edge=nm_edge, hev_prev=sems.val("hev")):
                sems.wait(tt, "wgt", WGT)
                sems.wait(tt, *nm_edge)
                sems.wait(tt, "hev", hev_prev)
                for tc in range(2):
                    for cc in range(NCH):
                        ins = tt.matmul(
                            z3p[tc][0:8, :],
                            gwt_s[:, cc * NEX:(cc + 1) * NEX],
                            lnf[cc][:, tc * 512:(tc + 1) * 512],
                            start=(cc == 0), stop=(cc == NCH - 1))
                    sems.inc(ins, GM)
            block.tensor(f_gate_mm)

            def f_gate_a(v, gm0=gm0):
                sems.wait(v, "iot", IOT)
                v.memset(lgt[:, :], 0.0)
                for tc in range(2):
                    sems.wait(v, GM, gm0 + tc + 1)
                    v.tensor_copy(lgt[0:8, tc * 512:(tc + 1) * 512],
                                  z3p[tc][0:8, :])
                v.stream_shuffle(srow[:, :], lgt[:, :],
                                 mask=[i ^ 4 for i in range(32)])
                v.tensor_max(prb[:, :], lgt[:, :], srow[:, :])
                v.stream_shuffle(srow[:, :], prb[:, :],
                                 mask=[i ^ 2 for i in range(32)])
                v.tensor_max(prb[:, :], prb[:, :], srow[:, :])
                v.stream_shuffle(srow[:, :], prb[:, :],
                                 mask=[i ^ 1 for i in range(32)])
                v.tensor_max(prb[:, :], prb[:, :], srow[:, :])
                v.willinc("gsub")
                v.tensor_sub(srow[:, :], lgt[:, :], prb[:, :])
            block.vector(f_gate_a)
            GSUB = sems.val("gsub")

            def f_gate_e(sc):
                sems.wait(sc, "gsub", GSUB)
                sc.willinc("gexp")
                sc.activation(combm[:, :], srow[:, :], AF.Exp)
            block.scalar(f_gate_e)
            GEXP = sems.val("gexp")

            def f_gate_b(v):
                sems.wait(v, "gexp", GEXP)
                v.stream_shuffle(srow[:, :], combm[:, :],
                                 mask=[i ^ 4 for i in range(32)])
                v.tensor_add(prb[:, :], combm[:, :], srow[:, :])
                v.stream_shuffle(srow[:, :], prb[:, :],
                                 mask=[i ^ 2 for i in range(32)])
                v.tensor_add(prb[:, :], prb[:, :], srow[:, :])
                v.stream_shuffle(srow[:, :], prb[:, :],
                                 mask=[i ^ 1 for i in range(32)])
                v.tensor_add(prb[:, :], prb[:, :], srow[:, :])
                v.reciprocal(srow[:, :], prb[:, :])
                v.tensor_mul(combm[:, :], combm[:, :], srow[:, :])
                v.stream_shuffle(srow[:, :], combm[:, :],
                                 mask=[i ^ 4 for i in range(32)])
                v.tensor_max(prb[:, :], combm[:, :], srow[:, :])
                v.stream_shuffle(srow[:, :], prb[:, :],
                                 mask=[i ^ 2 for i in range(32)])
                v.tensor_max(prb[:, :], prb[:, :], srow[:, :])
                v.stream_shuffle(srow[:, :], prb[:, :],
                                 mask=[i ^ 1 for i in range(32)])
                v.tensor_max(prb[:, :], prb[:, :], srow[:, :])
                v.tensor_tensor(srow[:, :], combm[:, :], prb[:, :],
                                Op.is_equal)
                v.scalar_tensor_tensor(srow[:, :], srow[:, :], -1e9,
                                       combm[:, :], Op.mult, Op.add)
                v.stream_shuffle(lgt[:, :], srow[:, :],
                                 mask=[i ^ 4 for i in range(32)])
                v.tensor_max(prb[:, :], srow[:, :], lgt[:, :])
                v.stream_shuffle(lgt[:, :], prb[:, :],
                                 mask=[i ^ 2 for i in range(32)])
                v.tensor_max(prb[:, :], prb[:, :], lgt[:, :])
                v.stream_shuffle(lgt[:, :], prb[:, :],
                                 mask=[i ^ 1 for i in range(32)])
                v.tensor_max(prb[:, :], prb[:, :], lgt[:, :])
                v.tensor_tensor(srow[:, :], combm[:, :], prb[:, :], Op.is_ge)
                v.willinc("comb")
                v.tensor_mul(combm[:, :], combm[:, :], srow[:, :])
            block.vector(f_gate_b)
            COMB = sems.val("comb")

            for ei, (w1s, w3s, w2s, nhg, emask) in enumerate(experts):
                zm0 = sems.val("zmm")
                z10 = sems.val("z1m")
                sl0 = sems.val("slu")
                he0 = sems.val("hev")
                wm0 = sems.val("wmm")
                ye0 = sems.val("yev")
                cb0 = sems.val("cbm")
                cr0 = sems.val("crb")

                wv = []

                def f_wda(g, w1s=w1s, w3s=w3s, nhg=nhg, zm0=zm0, wv=wv):
                    sems.wait(g, "st", ST0)
                    for hg in range(nhg):
                        sems.wait(g, "zmm", max(0, zm0 + hg - 2))
                        slot = (zm0 + hg) % 3
                        wd = "wda" + str(slot)
                        sems.dma(g.dma_start(
                            out=wt1[slot][:, :].rearrange(
                                "p (ch i) -> p ch i", i=128),
                            in_=w1s[:, 128 * hg:128 * (hg + 1)].rearrange(
                                "(ch p) i -> p ch i", p=128)), wd)
                        v = sems.dma(g.dma_start(
                            out=wt3[slot][:, :].rearrange(
                                "p (ch i) -> p ch i", i=128),
                            in_=w3s[:, 128 * hg:128 * (hg + 1)].rearrange(
                                "(ch p) i -> p ch i", p=128)), wd)
                        wv.append((slot, v))
                block.gpsimd(f_wda)

                wc0 = sems.val("wcst")
                w2v = []

                def f_w2dma(sy, w2s=w2s, nhg=nhg, wc0=wc0, w2v=w2v):
                    sems.wait(sy, "st", ST0)
                    for co in range(NCH):
                        sems.wait(sy, "wcst", max(0, wc0 + co - 1))
                        fs = (wc0 + co) % 2
                        v = sems.dma(sy.dma_start(
                            out=wt2f[fs][:, 0:nhg * 128].rearrange(
                                "p (ch i) -> p ch i", i=128),
                            in_=w2s[:, 128 * co:128 * (co + 1)].rearrange(
                                "(ch p) i -> p ch i", p=128)), "w2d" + str(fs))
                        w2v.append((fs, v))
                block.sync(f_w2dma)

                if emask is not None:
                    def f_crb(v, emask=emask, cbm_now=sems.val("cbm")):
                        sems.wait(v, "comb", COMB)
                        sems.wait(v, "cbm", cbm_now)
                        msk = [(i + emask) % 8 if i < 8 else i
                               for i in range(32)]
                        v.stream_shuffle(lgt[:, :], combm[:, :], mask=msk)
                        v.willinc("crb")
                        v.tensor_copy(combr_bf[0:1, :], lgt[0:1, :])
                    block.vector(f_crb)

                def f_pe(tt, ei=ei, nhg=nhg, emask=emask, zm0=zm0, he0=he0,
                         wm0=wm0, ye0=ye0, wv=wv, wc0=wc0, sl0=sl0,
                         crb_now=sems.val("crb")):
                    if emask is not None:
                        sems.wait(tt, "cstg", CSTG)
                        sems.wait(tt, "crb", crb_now)
                        sems.wait(tt, "yev", ye0)
                        for r in range(2):
                            ins = tt.matmul(cbp[r], t["ones_row_bf"][:, :],
                                            combr_bf[0:1,
                                                     r * 512:(r + 1) * 512],
                                            start=True, stop=True)
                        sems.inc(ins, "cbm")
                    for hg in range(nhg):
                        slot, wvv = wv[hg]
                        sems.wait(tt, "wda" + str(slot), wvv)
                        if hg == 0:
                            sems.wait(tt, "hev", he0)
                        elif hg >= 2:
                            sems.wait(tt, "hev", he0 + 2 * (hg - 1))
                        if hg >= 1:
                            sems.wait(tt, "slu", sl0 + 2 * hg)
                        zb = z3p if hg % 2 == 0 else z3o
                        for cc in range(NCH):
                            w = wt1[slot][:, cc * 128:(cc + 1) * 128]
                            for r in range(2):
                                ins = tt.matmul(
                                    z1p[r], w,
                                    lny[cc][:, r * 512:(r + 1) * 512],
                                    start=(cc == 0),
                                    stop=(cc == NCH - 1))
                        sems.inc(ins, "z1m")
                        for cc in range(NCH):
                            w = wt3[slot][:, cc * 128:(cc + 1) * 128]
                            for r in range(2):
                                ins = tt.matmul(
                                    zb[r], w,
                                    lny[cc][:, r * 512:(r + 1) * 512],
                                    start=(cc == 0), stop=(cc == NCH - 1))
                        sems.inc(ins, "zmm")
                    for co in range(NCH):
                        sems.wait(tt, "wcst", wc0 + co + 1)
                        if co == 0:
                            sems.wait(tt, "hev", he0 + 2 * nhg)
                            sems.wait(tt, "yev", ye0)
                        else:
                            sems.wait(tt, "yev", ye0 + co)
                        for ch in range(nhg):
                            w = wt2[co % 2][:, ch * 128:(ch + 1) * 128]
                            for r in range(2):
                                ins = tt.matmul(
                                    cbp[r], w,
                                    he[ch][:, r * 512:(r + 1) * 512],
                                    start=(ch == 0), stop=(ch == nhg - 1))
                        sems.inc(ins, "wmm")
                block.tensor(f_pe)

                def f_silu(sc, nhg=nhg, z10=z10, he0=he0, sl0=sl0):
                    for k in range(2 * nhg):
                        hg, r = k // 2, k % 2
                        sems.wait(sc, "z1m", z10 + hg + 1)
                        if k >= 2:
                            sems.wait(sc, "hev", he0 + k - 1)
                        sc.willinc("slu")
                        sc.activation(s1t[k % 2][:, :], z1p[r], AF.Silu)
                    block_scalar_done = None
                block.scalar(f_silu)

                def f_w2cast(sc, nhg=nhg, wm0=wm0, w2v=w2v):
                    for co in range(NCH):
                        fs, v = w2v[co]
                        sems.wait(sc, "w2d" + str(fs), v)
                        sems.wait(sc, "wmm", wm0 + max(0, co - 1))
                        sc.willinc("wcst")
                        sc.activation(wt2[co % 2][:, 0:nhg * 128],
                                      wt2f[fs][:, 0:nhg * 128], AF.Copy)
                block.scalar(f_w2cast)

                def f_he(v, nhg=nhg, emask=emask, sl0=sl0, he0=he0, zm0=zm0,
                         cb_now=sems.val("cbm")):
                    if emask is not None:
                        sems.wait(v, "cbm", cb_now)
                    for k in range(2 * nhg):
                        hg, r = k // 2, k % 2
                        zb = z3p if hg % 2 == 0 else z3o
                        sems.wait(v, "slu", sl0 + k + 1)
                        sems.wait(v, "zmm", zm0 + hg + 1)
                        if emask is not None:
                            v.tensor_mul(s2t[:, :], s1t[k % 2][:, :], zb[r])
                            v.willinc("hev")
                            v.tensor_mul(he[hg][:, r * 512:(r + 1) * 512],
                                         s2t[:, :], cbp[r])
                        else:
                            v.willinc("hev")
                            v.tensor_mul(he[hg][:, r * 512:(r + 1) * 512],
                                         s1t[k % 2][:, :], zb[r])
                block.vector(f_he)

                def f_yacc(v, ei=ei, q=q, wm0=wm0, ye0=ye0):
                    for co in range(NCH):
                        sems.wait(v, "wmm", wm0 + co + 1)
                        for r in range(2):
                            so = slice(r * 512, (r + 1) * 512)
                            if r == 1:
                                v.willinc("yev")
                            if ei == 0:
                                v.tensor_copy(yac[co][:, so], cbp[r])
                            else:
                                v.tensor_add(yac[co][:, so], yac[co][:, so],
                                             cbp[r])
                block.vector(f_yacc)

            XE = "xev"
            xe0 = sems.val(XE)
            ye_all = sems.val("yev")

            def f_resid(v, q=q, xe0=xe0, ye_all=ye_all,
                        xld_now=sems.val("xld")):
                sems.wait(v, "yev", ye_all)
                sems.wait(v, "xld", xld_now)
                for co in range(NCH):
                    for r in range(2):
                        if r == 1:
                            v.willinc(XE)
                        so = slice(r * 512, (r + 1) * 512)
                        v.scalar_tensor_tensor(xt[co][:, so], yac[co][:, so],
                                               0.5, xt[co][:, so],
                                               Op.mult, Op.add)
            block.vector(f_resid)

            def f_store(sy, qsl=qsl, xe0=xe0):
                for co in range(NCH):
                    sems.wait(sy, XE, xe0 + co + 1)
                    sems.dma(sy.dma_start(
                        out=xo_d[128 * co:128 * (co + 1), qsl],
                        in_=xt[co][:, :]), "st")
            block.sync(f_store)


def build_f1():
    return _standalone(lambda env, xi, xo: emit_f1(env, xi, xo), None)


def run_f1(x_parts, inputs):
    nc = _get_prog("f1", build_f1)
    in_maps = []
    for k in range(NCORES):
        d = f1_host_inputs(inputs, k)
        d["x"] = x_parts[k]
        in_maps.append(d)
    res = run_bass_kernel_spmd(nc, in_maps, core_ids=list(range(NCORES)))
    return [res.results[k]["xo"] for k in range(NCORES)]



def f0_host_inputs(inputs, core):
    cones = np.ones((128, 128), np.float32)
    cones[1, 1] = EPS
    cones[1, 2] = -0.5
    cones[1, 3] = 0.125
    b1c = np.concatenate([col_layout(inputs["L0_ffn_b1"][e])
                          for e in range(8)], axis=1)
    return {
        "ipwt": np.ascontiguousarray(np.asarray(inputs["L0_ffn_in_proj_w"]).T),
        "ipb": np.ascontiguousarray(
            np.asarray(inputs["L0_ffn_in_proj_b"]).reshape(1, 3 * C)),
        "owt": np.ascontiguousarray(np.asarray(inputs["L0_ffn_out_w"]).T),
        "obc": col_layout(inputs["L0_ffn_out_b"]),
        "l1g": col_layout(inputs["L0_ffn_ln1_g"]),
        "l1b": col_layout(inputs["L0_ffn_ln1_b"]),
        "rwt": np.ascontiguousarray(np.asarray(inputs["L0_ffn_router_w"]).T),
        "rbc": np.ascontiguousarray(
            np.asarray(inputs["L0_ffn_router_b"]).reshape(8, 1)),
        "w1t": np.ascontiguousarray(
            np.asarray(inputs["L0_ffn_w1"]).transpose(0, 2, 1)),
        "b1c": np.ascontiguousarray(b1c),
        "w2t": np.ascontiguousarray(
            np.asarray(inputs["L0_ffn_w2"]).transpose(0, 2, 1)),
        "b2r": np.ascontiguousarray(inputs["L0_ffn_b2"]),
        "l2g": col_layout(inputs["L0_ffn_ln2_g"]),
        "l2b": col_layout(inputs["L0_ffn_ln2_b"]),
        "cones": cones,
        "ident": np.eye(128, dtype=np.float32),
    }


def emit_f0(env, x_d, xo_d, h_name="h_scr"):
    env.enter_phase()
    nc, sems, t, block, pp4 = env.nc, env.sems, env.t, env.block, env.pp4
    ipwt_d = env.Dap("ipwt", [C, 3 * C])
    ipb_d = env.Dap("ipb", [1, 3 * C])
    owt_d = env.Dap("owt", [C, C])
    obc_d = env.Dap("obc", [128, NCH])
    l1g_d = env.Dap("l1g", [128, NCH])
    l1b_d = env.Dap("l1b", [128, NCH])
    rwt_d = env.Dap("rwt", [C, 8])
    rbc_d = env.Dap("rbc", [8, 1])
    w1t_d = env.Dap("w1t", [8, C, C])
    b1c_d = env.Dap("b1c", [128, 64])
    w2t_d = env.Dap("w2t", [8, C, C])
    b2r_d = env.Dap("b2r", [8, C])
    l2g_d = env.Dap("l2g", [128, NCH])
    l2b_d = env.Dap("l2b", [128, NCH])
    cones_d = env.Dap("cones", [128, 128])
    ident_d = env.Dap("ident", [128, 128])
    h_scr = env.D(h_name, [C, TOK], F32, "Internal")

    QT = 1024
    BT = 512
    ST0 = sems.val("st")

    def blkap(dram, chk, blk, width=128):
        return dram[128 * chk:128 * (chk + 1), :].rearrange(
            "p (b l) -> p b l", b=B)[:, :, blk * width:(blk + 1) * width]

    with ExitStack() as ctx:
        bc0, bc1 = t["ps_bc0"][:, :], t["ps_bc1"][:, :]
        qkvp = [bc0, bc1, pp4[0][:, :], pp4[1][:, :], pp4[2][:, :],
                pp4[3][:, :]]
        tpp = [pp4[0][0:128, 0:128], pp4[1][0:128, 0:128]]
        opp = [pp4[2][:, :], pp4[3][:, :]]

        emit_const_loads(block, sems, t, cones_d, env)

        def sb(name, shape, dt=F32):
            return ctx.enter_context(nc.sbuf_tensor(env.pfx + name, shape,
                                                    dt))

        identf = sb("identf", [128, 128])
        obc_s = sb("obc_s", [128, NCH])
        l1g_s = sb("l1g_s", [128, NCH])
        l1b_s = sb("l1b_s", [128, NCH])
        l2g_s = sb("l2g_s", [128, NCH])
        l2b_s = sb("l2b_s", [128, NCH])

        def f_ld0(sy):
            sems.wait(sy, "st", ST0)
            sems.dma(sy.dma_start(out=identf[:, :], in_=ident_d), "cst")
            sems.dma(sy.dma_start(out=obc_s[:, :], in_=obc_d), "cst")
            sems.dma(sy.dma_start(out=l1g_s[:, :], in_=l1g_d), "cst")
            sems.dma(sy.dma_start(out=l1b_s[:, :], in_=l1b_d), "cst")
            sems.dma(sy.dma_start(out=l2g_s[:, :], in_=l2g_d), "cst")
            sems.dma(sy.dma_start(out=l2b_s[:, :], in_=l2b_d), "cst")
        block.sync(f_ld0)

        with ExitStack() as actx:
            def sba(name, shape, dt=F32):
                return actx.enter_context(
                    nc.sbuf_tensor(env.pfx + "A" + name, shape, dt))

            xtA = [sba(f"xtA{i}", [128, BT]) for i in range(NCH)]
            lnxA = [sba(f"lnxA{i}", [128, BT], BF16) for i in range(NCH)]
            lnfA = [sba(f"lnfA{i}", [128, BT]) for i in range(NCH)]
            haA = [sba(f"haA{i}", [128, BT]) for i in range(NCH)]
            ipwt_s = sba("ipwt_s", [128, NCH * 3 * C], BF16)
            ipb_s = sba("ipb_s", [1, 3 * C], BF16)
            owt_s = sba("owt_s", [128, NCH * C], BF16)
            qkt = [sba(f"qkt{i}", [128, 3 * C], BF16) for i in range(B)]
            prod = sba("prod", [128, C])
            scr = sba("scr", [128, 256])
            mrow = sba("mrow", [128, 16])
            drow = sba("drow", [128, 16])
            ctmp = sba("ctmp", [128, C])
            ctxk = [sba(f"ctxk{i}", [128, C]) for i in range(B)]
            ctxT = [sba(f"ctxT{i}", [128, BT], BF16) for i in range(NCH)]

            def f_wd0(g):
                sems.wait(g, "st", ST0)
                sems.dma(g.dma_start(
                    out=ipwt_s[:, :].rearrange("p (ch i) -> p ch i", i=3 * C),
                    in_=ipwt_d.rearrange("(ch p) i -> p ch i", p=128)), "wgt")
                sems.dma(g.dma_start(out=ipb_s[:, :], in_=ipb_d), "wgt")
                sems.dma(g.dma_start(
                    out=owt_s[:, :].rearrange("p (ch i) -> p ch i", i=C),
                    in_=owt_d.rearrange("(ch p) i -> p ch i", p=128)), "wgt")
            block.gpsimd(f_wd0)
            WGT = sems.val("wgt")
            CSTG = sems.val("cstg")

            for blk in range(NCH):
                def f_ldx(sy, blk=blk, prev=sems.val("lnAnm")):
                    if blk >= 1:
                        sems.wait(sy, "lnAnm", prev)
                    else:
                        sems.wait(sy, "st", ST0)
                    for chk in range(NCH):
                        sems.dma(sy.dma_start(out=xtA[chk][:, :].rearrange(
                            "p (b l) -> p b l", b=B),
                            in_=blkap(x_d, chk, blk)), "xldA")
                block.sync(f_ldx)

                nm_edge = emit_ln(
                    nc, block, sems, t, xtA, lnxA, BT, tag="lnA",
                    ld_edge=("xldA", sems.val("xldA")),
                    wr_edges=[("qpm", sems.val("qpm")),
                              ("hae", sems.val("hae")),
                              ("hst", sems.val("hst"))],
                    bc_edges=[("qke", sems.val("qke")),
                              ("lnBnm", sems.val("lnBnm"))],
                    out_f32=lnfA)

                qp0 = sems.val("qpm")
                qk0 = sems.val("qke")

                def f_qkv(tt, nm_edge=nm_edge, qp0=qp0, qk0=qk0,
                          tev_prev=sems.val("tev"), hae_prev=sems.val("hae")):
                    sems.wait(tt, "wgt", WGT)
                    sems.wait(tt, "cstg", CSTG)
                    sems.wait(tt, *nm_edge)
                    sems.wait(tt, "qke", qk0)
                    sems.wait(tt, "tev", tev_prev)
                    sems.wait(tt, "hae", hae_prev)
                    for b in range(B):
                        if b >= 1:
                            sems.wait(tt, "qke", qk0 + b)
                        for cc in range(NCH):
                            lhs = lnxA[cc][:, b * 128:(b + 1) * 128]
                            for hcol in range(6):
                                tt.matmul(qkvp[hcol], lhs,
                                          ipwt_s[:, cc * 3 * C + hcol * 512:
                                                 cc * 3 * C + (hcol + 1) * 512],
                                          start=(cc == 0), stop=False)
                        for hcol in range(6):
                            ins = tt.matmul(qkvp[hcol],
                                            t["ones_row_bf"][:, :],
                                            ipb_s[0:1, hcol * 512:
                                                  (hcol + 1) * 512],
                                            start=False, stop=True)
                        sems.inc(ins, "qpm")
                block.tensor(f_qkv)

                def f_qev(v, qp0=qp0):
                    for b in range(B):
                        sems.wait(v, "qpm", qp0 + b + 1)
                        for hcol in range(6):
                            if hcol == 5:
                                v.willinc("qke")
                            v.tensor_copy(
                                qkt[b][:, hcol * 512:(hcol + 1) * 512],
                                qkvp[hcol])
                block.vector(f_qev)

                def f_att_a(v):
                    for tb in range(B):
                        for sb_ in range(B):
                            v.tensor_tensor(
                                prod[:, :].rearrange("p (h d) -> p h d", d=64),
                                qkt[tb][:, 0:C].rearrange(
                                    "p (h d) -> p h d", d=64),
                                qkt[sb_][:, C:2 * C].rearrange(
                                    "p (h d) -> p h d", d=64), Op.mult)
                            v.tensor_reduce(
                                scr[:, (tb * 4 + sb_) * 16:
                                    (tb * 4 + sb_) * 16 + 16],
                                prod[:, :].rearrange("p (h d) -> p h d", d=64),
                                AX.X, Op.add)
                    for tb in range(B):
                        base = tb * 64
                        v.tensor_max(mrow[:, :], scr[:, base:base + 16],
                                     scr[:, base + 16:base + 32])
                        v.tensor_max(drow[:, :], scr[:, base + 32:base + 48],
                                     scr[:, base + 48:base + 64])
                        v.tensor_max(mrow[:, :], mrow[:, :], drow[:, :])
                        last = tb == B - 1
                        if last:
                            v.willinc("ats")
                        v.tensor_tensor(
                            scr[:, base:base + 64].rearrange(
                                "p (s h) -> p s h", h=16),
                            scr[:, base:base + 64].rearrange(
                                "p (s h) -> p s h", h=16),
                            mrow[:, :].unsqueeze(1).broadcast_to([128, 4, 16]),
                            Op.subtract)
                block.vector(f_att_a)
                ATS = sems.val("ats")

                def f_att_e(sc):
                    sems.wait(sc, "ats", ATS)
                    sc.willinc("ate")
                    sc.activation(scr[:, :], scr[:, :], AF.Exp, scale=0.125)
                block.scalar(f_att_e)
                ATE = sems.val("ate")

                def f_att_b(v, blk=blk):
                    sems.wait(v, "ate", ATE)
                    for tb in range(B):
                        base = tb * 64
                        v.tensor_add(drow[:, :], scr[:, base:base + 16],
                                     scr[:, base + 16:base + 32])
                        v.tensor_add(mrow[:, :], scr[:, base + 32:base + 48],
                                     scr[:, base + 48:base + 64])
                        v.tensor_add(drow[:, :], drow[:, :], mrow[:, :])
                        v.reciprocal(drow[:, :], drow[:, :])
                        v.tensor_tensor(
                            scr[:, base:base + 64].rearrange(
                                "p (s h) -> p s h", h=16),
                            scr[:, base:base + 64].rearrange(
                                "p (s h) -> p s h", h=16),
                            drow[:, :].unsqueeze(1).broadcast_to([128, 4, 16]),
                            Op.mult)
                    for tb in range(B):
                        for sb_ in range(B):
                            att_v = scr[:, (tb * 4 + sb_) * 16:
                                        (tb * 4 + sb_) * 16 + 16]                                 .unsqueeze(2).broadcast_to([128, 16, 64])
                            vv = qkt[sb_][:, 2 * C:3 * C].rearrange(
                                "p (h d) -> p h d", d=64)
                            dst = ctxk[tb][:, :].rearrange(
                                "p (h d) -> p h d", d=64)
                            if sb_ == 0:
                                v.tensor_tensor(dst, vv, att_v, Op.mult)
                            else:
                                v.tensor_tensor(
                                    ctmp[:, :].rearrange("p (h d) -> p h d",
                                                         d=64),
                                    vv, att_v, Op.mult)
                                if tb == B - 1 and sb_ == B - 1:
                                    v.willinc("ctm")
                                    v.tensor_add(ctxk[tb][:, :], ctxk[tb][:, :],
                                                 ctmp[:, :])
                                else:
                                    v.tensor_add(ctxk[tb][:, :], ctxk[tb][:, :],
                                                 ctmp[:, :])
                block.vector(f_att_b)
                CTM = sems.val("ctm")

                tp0 = sems.val("tpm")
                te0 = sems.val("tev")

                def f_tp(tt, tp0=tp0, te0=te0, qk_now=sems.val("qke")):
                    sems.wait(tt, "ctm", CTM)
                    sems.wait(tt, "qke", qk_now)
                    k = 0
                    for tb in range(B):
                        for ch in range(NCH):
                            if k >= 2:
                                sems.wait(tt, "tev", te0 + k - 1)
                            ins = tt.transpose(tpp[k % 2],
                                               ctxk[tb][:, ch * 128:
                                                        (ch + 1) * 128],
                                               identf[:, :])
                            sems.inc(ins, "tpm")
                            k += 1
                block.tensor(f_tp)

                def f_tev(v, tp0=tp0):
                    k = 0
                    for tb in range(B):
                        for ch in range(NCH):
                            sems.wait(v, "tpm", tp0 + k + 1)
                            v.willinc("tev")
                            v.tensor_copy(ctxT[ch][:, tb * 128:(tb + 1) * 128],
                                          tpp[k % 2])
                            k += 1
                block.vector(f_tev)
                TEV = sems.val("tev")

                op0 = sems.val("opm")
                ha0 = sems.val("hae")

                def f_op(tt, op0=op0, ha0=ha0):
                    sems.wait(tt, "tev", TEV)
                    for co in range(NCH):
                        sems.wait(tt, "hae", ha0 + max(0, co - 1))
                        for cc in range(NCH):
                            w = owt_s[:, cc * C + co * 128:
                                      cc * C + co * 128 + 128]
                            ins = tt.matmul(opp[co % 2], w, ctxT[cc][:, :],
                                            start=(cc == 0),
                                            stop=(cc == NCH - 1))
                        sems.inc(ins, "opm")
                block.tensor(f_op)

                def f_hae(v, op0=op0):
                    for co in range(NCH):
                        sems.wait(v, "opm", op0 + co + 1)
                        v.willinc("hae")
                        v.scalar_tensor_tensor(haA[co][:, :], opp[co % 2],
                                               obc_s[:, co:co + 1],
                                               lnfA[co][:, :], Op.add, Op.add)
                block.vector(f_hae)

                nmB = emit_ln(nc, block, sems, t, haA, None, BT, tag="lnB",
                              ld_edge=("hae", sems.val("hae")),
                              wr_edges=[("hst", sems.val("hst"))],
                              bc_edges=[("qke", sems.val("qke")),
                                        ("tev", sems.val("tev"))],
                              gcol=l1g_s, bcol=l1b_s, out_f32=lnfA)

                def f_hst(sy, blk=blk, nmB=nmB):
                    sems.wait(sy, nmB[0], nmB[1])
                    for chk in range(NCH):
                        sems.dma(sy.dma_start(
                            out=blkap(h_scr.ap(), chk, blk),
                            in_=lnfA[chk][:, :].rearrange(
                                "p (b l) -> p b l", b=B)), "hst")
                block.sync(f_hst)

        HST = sems.val("hst")

        with ExitStack() as bctx:
            def sbb(name, shape, dt=F32):
                return bctx.enter_context(
                    nc.sbuf_tensor(env.pfx + "B" + name, shape, dt))

            xtB = [sbb(f"xtB{i}", [128, QT]) for i in range(NCH)]
            hf = [sbb(f"hf{i}", [128, QT]) for i in range(NCH)]
            hbf = [sbb(f"hbf{i}", [128, QT], BF16) for i in range(NCH)]
            heB = [sbb(f"heB{i}", [128, QT], BF16) for i in range(NCH)]
            yacB = [sbb(f"yacB{i}", [128, QT]) for i in range(NCH)]
            rwt_s = sbb("rwt_s", [128, NCH * 8])
            rbc_s = sbb("rbc_s", [8, 1])
            b1c_s = sbb("b1c_s", [128, 64])
            b2_s = sbb("b2_s", [8, C], BF16)
            lgtB = sbb("lgtB", [32, QT])
            srowB = sbb("srowB", [32, QT])
            prbB = sbb("prbB", [32, QT])
            asgB = sbb("asgB", [32, QT])
            mskB = sbb("mskB", [32, QT])
            msk8 = sbb("msk8", [8, QT], BF16)
            mr_bf = sbb("mr_bf", [1, QT], BF16)
            iocolB = sbb("iocolB", [32, 1])
            wtB1 = [sbb(f"wtB1_{i}", [128, NCH * 128], BF16) for i in range(3)]
            wtB2 = [sbb(f"wtB2_{i}", [128, NCH * 128], BF16) for i in range(4)]
            wtB2f = [sbb(f"wtB2f_{i}", [128, NCH * 128]) for i in range(3)]
            s1tB = [sbb(f"s1tB{i}", [128, 512]) for i in range(2)]

            zhB = [bc0, bc1]
            cbB = [pp4[2][:, :], pp4[3][:, :]]
            w2B = [pp4[0][:, :], pp4[1][:, :]]
            w2Bo = [t["psf"][0][:, :], t["psf"][1][:, :]]

            def f_ldB0(sy):
                sems.wait(sy, "hst", HST)
                sems.dma(sy.dma_start(out=rwt_s[:, :].rearrange(
                    "p (ch i) -> p ch i", i=8),
                    in_=rwt_d.rearrange("(ch p) i -> p ch i", p=128)), "cstB")
                sems.dma(sy.dma_start(out=rbc_s[:, :], in_=rbc_d), "cstB")
                sems.dma(sy.dma_start(out=b1c_s[:, :], in_=b1c_d), "cstB")
            block.sync(f_ldB0)

            def f_ldB0g(g):
                sems.wait(g, "hst", HST)
                sems.dma(g.dma_start(out=b2_s[:, :], in_=b2r_d), "cstBg")
                g.willinc("iotB")
                g.iota(iocolB[:, :], pattern=[[0, 1]], base=0,
                       channel_multiplier=1,
                       allow_small_or_imprecise_dtypes=True)
            block.gpsimd(f_ldB0g)
            CSTB = sems.val("cstB")
            CSTBG = sems.val("cstBg")
            IOTB = sems.val("iotB")
            prev_lnC = None

            for q in range(B):
                qsl = slice(q * QT, (q + 1) * QT)

                def f_ldq(sy, q=q, qsl=qsl, st_prev=sems.val("st")):
                    if q >= 1:
                        sems.wait(sy, "st", st_prev)
                    else:
                        sems.wait(sy, "hst", HST)
                    for chk in range(NCH):
                        sems.dma(sy.dma_start(
                            out=hf[chk][:, :],
                            in_=h_scr.ap()[128 * chk:128 * (chk + 1), qsl]),
                            "hldB")
                        sems.dma(sy.dma_start(
                            out=xtB[chk][:, :],
                            in_=x_d[128 * chk:128 * (chk + 1), qsl]), "hldB")
                block.sync(f_ldq)

                def f_ldqg(g, q=q, qsl=qsl, ze_prev=sems.val("sluB")):
                    sems.wait(g, "hst", HST)
                    if q >= 1:
                        sems.wait(g, "sluB", ze_prev)
                    for chk in range(NCH):
                        sems.dma(g.dma_start(
                            out=hbf[chk][:, :],
                            in_=h_scr.ap()[128 * chk:128 * (chk + 1), qsl]),
                            "hldBg")
                block.gpsimd(f_ldqg)
                HLDB = sems.val("hldB")
                HLDBG = sems.val("hldBg")

                rt0 = sems.val("rtm")

                def f_rt(tt, rt0=rt0, slu_prev=sems.val("sluB"),
                         lnc_prev=sems.val("lnCnm")):
                    sems.wait(tt, "cstB", CSTB)
                    sems.wait(tt, "hldB", HLDB)
                    sems.wait(tt, "sluB", slu_prev)
                    sems.wait(tt, "lnCnm", lnc_prev)
                    for tc in range(2):
                        for cc in range(NCH):
                            ins = tt.matmul(
                                zhB[tc][0:8, :],
                                rwt_s[:, cc * 8:(cc + 1) * 8],
                                hf[cc][:, tc * 512:(tc + 1) * 512],
                                start=(cc == 0), stop=(cc == NCH - 1))
                        sems.inc(ins, "rtm")
                block.tensor(f_rt)

                def f_asg(v, rt0=rt0):
                    sems.wait(v, "iotB", IOTB)
                    v.memset(lgtB[:, :], 0.0)
                    for tc in range(2):
                        sems.wait(v, "rtm", rt0 + tc + 1)
                        v.tensor_scalar(lgtB[0:8, tc * 512:(tc + 1) * 512],
                                        zhB[tc][0:8, :], rbc_s[0:8, 0:1],
                                        None, Op.add)
                    v.stream_shuffle(srowB[:, :], lgtB[:, :],
                                     mask=[i ^ 4 for i in range(32)])
                    v.tensor_max(prbB[:, :], lgtB[:, :], srowB[:, :])
                    v.stream_shuffle(srowB[:, :], prbB[:, :],
                                     mask=[i ^ 2 for i in range(32)])
                    v.tensor_max(prbB[:, :], prbB[:, :], srowB[:, :])
                    v.stream_shuffle(srowB[:, :], prbB[:, :],
                                     mask=[i ^ 1 for i in range(32)])
                    v.tensor_max(prbB[:, :], prbB[:, :], srowB[:, :])
                    v.tensor_tensor(mskB[:, :], lgtB[:, :], prbB[:, :],
                                    Op.is_equal)
                    v.tensor_scalar(asgB[:, :], mskB[:, :], iocolB[:, 0:1],
                                    None, Op.mult)
                    v.stream_shuffle(srowB[:, :], asgB[:, :],
                                     mask=[i ^ 4 for i in range(32)])
                    v.tensor_max(asgB[:, :], asgB[:, :], srowB[:, :])
                    v.stream_shuffle(srowB[:, :], asgB[:, :],
                                     mask=[i ^ 2 for i in range(32)])
                    v.tensor_max(asgB[:, :], asgB[:, :], srowB[:, :])
                    v.stream_shuffle(srowB[:, :], asgB[:, :],
                                     mask=[i ^ 1 for i in range(32)])
                    v.tensor_max(asgB[:, :], asgB[:, :], srowB[:, :])
                    v.scalar_tensor_tensor(prbB[:, :], mskB[:, :], -1e9,
                                           lgtB[:, :], Op.mult, Op.add)
                    v.stream_shuffle(srowB[:, :], prbB[:, :],
                                     mask=[i ^ 4 for i in range(32)])
                    v.tensor_max(srowB[:, :], prbB[:, :], srowB[:, :])
                    v.stream_shuffle(lgtB[:, :], srowB[:, :],
                                     mask=[i ^ 2 for i in range(32)])
                    v.tensor_max(srowB[:, :], srowB[:, :], lgtB[:, :])
                    v.stream_shuffle(lgtB[:, :], srowB[:, :],
                                     mask=[i ^ 1 for i in range(32)])
                    v.tensor_max(srowB[:, :], srowB[:, :], lgtB[:, :])
                    v.tensor_tensor(mskB[:, :], prbB[:, :], srowB[:, :],
                                    Op.is_equal)
                    v.tensor_scalar(mskB[:, :], mskB[:, :], iocolB[:, 0:1],
                                    None, Op.mult)
                    v.stream_shuffle(srowB[:, :], mskB[:, :],
                                     mask=[i ^ 4 for i in range(32)])
                    v.tensor_max(mskB[:, :], mskB[:, :], srowB[:, :])
                    v.stream_shuffle(srowB[:, :], mskB[:, :],
                                     mask=[i ^ 2 for i in range(32)])
                    v.tensor_max(mskB[:, :], mskB[:, :], srowB[:, :])
                    v.stream_shuffle(srowB[:, :], mskB[:, :],
                                     mask=[i ^ 1 for i in range(32)])
                    v.tensor_max(mskB[:, :], mskB[:, :], srowB[:, :])
                    v.tensor_max(asgB[:, :], asgB[:, :], mskB[:, :])
                    v.willinc("asg")
                    v.tensor_scalar(msk8[0:8, :], asgB[0:8, :],
                                    iocolB[0:8, 0:1], None, Op.is_equal)
                block.vector(f_asg)
                ASG = sems.val("asg")

                for e in range(8):
                    zm0 = sems.val("zmmB")
                    sl0 = sems.val("sluB")
                    he0 = sems.val("hevB")
                    wm0 = sems.val("wmmB")
                    ye0 = sems.val("yevB")
                    cb0 = sems.val("cbmB")

                    wvB = []

                    def f_wdB(g, e=e, zm0=zm0, wvB=wvB):
                        sems.wait(g, "st", ST0)
                        for ch in range(NCH):
                            sems.wait(g, "zmmB", max(0, zm0 + ch - 2))
                            slot = (zm0 + ch) % 3
                            v = sems.dma(g.dma_start(
                                out=wtB1[slot][:, :].rearrange(
                                    "p (c i) -> p c i", i=128),
                                in_=w1t_d[e][:, 128 * ch:128 * (ch + 1)]
                                .rearrange("(c p) i -> p c i", p=128)),
                                "wdB" + str(slot))
                            wvB.append((slot, v))
                    block.gpsimd(f_wdB)

                    wcB0 = sems.val("wcsB")
                    w2vB = []

                    def f_w2dB(sy, e=e, wcB0=wcB0, w2vB=w2vB):
                        sems.wait(sy, "st", ST0)
                        for co in range(NCH):
                            sems.wait(sy, "wcsB", max(0, wcB0 + co - 2))
                            fs = (wcB0 + co) % 3
                            v = sems.dma(sy.dma_start(
                                out=wtB2f[fs][:, :].rearrange(
                                    "p (c i) -> p c i", i=128),
                                in_=w2t_d[e][:, 128 * co:128 * (co + 1)]
                                .rearrange("(c p) i -> p c i", p=128)),
                                "w2dB" + str(fs))
                            w2vB.append((fs, v))
                    block.sync(f_w2dB)

                    def f_mr(v, e=e, cbm_now=sems.val("cbmB")):
                        sems.wait(v, "asg", ASG)
                        sems.wait(v, "cbmB", cbm_now)
                        v.tensor_scalar(mskB[:, :], asgB[:, :], float(e),
                                        None, Op.is_equal)
                        v.willinc("mrb")
                        v.tensor_copy(mr_bf[0:1, :], mskB[0:1, :])
                    block.vector(f_mr)

                    def f_peB(tt, e=e, zm0=zm0, he0=he0, wm0=wm0, ye0=ye0,
                              sl0=sl0, wvB=wvB, wcB0=wcB0, plnC=prev_lnC,
                              mrb_now=sems.val("mrb"), hldbg=HLDBG):
                        sems.wait(tt, "cstg", CSTG)
                        sems.wait(tt, "hldBg", hldbg)
                        sems.wait(tt, "mrb", mrb_now)
                        sems.wait(tt, "yevB", ye0)
                        for r in range(2):
                            ins = tt.matmul(cbB[r], t["ones_row_bf"][:, :],
                                            mr_bf[0:1, r * 512:(r + 1) * 512],
                                            start=True, stop=True)
                        sems.inc(ins, "cbmB")
                        for ch in range(NCH):
                            slot, wvv = wvB[ch]
                            sems.wait(tt, "wdB" + str(slot), wvv)
                            if ch >= 1:
                                sems.wait(tt, "sluB", sl0 + 2 * ch)
                            for cc in range(NCH):
                                w = wtB1[slot][:, cc * 128:(cc + 1) * 128]
                                for r in range(2):
                                    ins = tt.matmul(
                                        zhB[r], w,
                                        hbf[cc][:, r * 512:(r + 1) * 512],
                                        start=(cc == 0), stop=(cc == NCH - 1))
                            sems.inc(ins, "zmmB")
                        for co in range(NCH):
                            sems.wait(tt, "wcsB", wcB0 + co + 1)
                            if co == 0:
                                sems.wait(tt, "hevB", he0 + 2 * NCH)
                                sems.wait(tt, "yevB", ye0)
                            else:
                                sems.wait(tt, "yevB", ye0 + max(0, co - 1))
                            if co == 1 and plnC is not None:
                                sems.wait(tt, plnC[0], plnC[1])
                            wb = w2B if co % 2 == 0 else w2Bo
                            for ch in range(NCH):
                                w = wtB2[(wcB0 + co) % 4][:,
                                                          ch * 128:
                                                          (ch + 1) * 128]
                                for r in range(2):
                                    ins = tt.matmul(
                                        wb[r], w,
                                        heB[ch][:, r * 512:(r + 1) * 512],
                                        start=(ch == 0), stop=(ch == NCH - 1))
                            sems.inc(ins, "wmmB")
                    block.tensor(f_peB)

                    def f_siluB(sc, e=e, zm0=zm0, he0=he0):
                        sems.wait(sc, "hldBg", HLDBG)
                        for k in range(2 * NCH):
                            ch, r = k // 2, k % 2
                            sems.wait(sc, "zmmB", zm0 + ch + 1)
                            if k >= 2:
                                sems.wait(sc, "hevB", he0 + k - 1)
                            sc.willinc("sluB")
                            sc.activation(s1tB[k % 2][:, :], zhB[r], AF.Gelu,
                                          bias=b1c_s[:, e * 8 + ch:
                                                     e * 8 + ch + 1])
                    block.scalar(f_siluB)

                    def f_w2cB(sc, e=e, wm0=wm0, w2vB=w2vB, wcB0=wcB0):
                        for co in range(NCH):
                            fs, v = w2vB[co]
                            sems.wait(sc, "w2dB" + str(fs), v)
                            sems.wait(sc, "wmmB", wm0 + max(0, co - 3))
                            sc.willinc("wcsB")
                            sc.activation(wtB2[(wcB0 + co) % 4][:, :],
                                          wtB2f[fs][:, :], AF.Copy)
                    block.scalar(f_w2cB)

                    def f_heB(v, e=e, sl0=sl0, he0=he0,
                              cb_now=sems.val("cbmB")):
                        sems.wait(v, "cbmB", cb_now)
                        for k in range(2 * NCH):
                            ch, r = k // 2, k % 2
                            sems.wait(v, "sluB", sl0 + k + 1)
                            v.willinc("hevB")
                            v.tensor_mul(heB[ch][:, r * 512:(r + 1) * 512],
                                         s1tB[k % 2][:, :], cbB[r])
                    block.vector(f_heB)

                    def f_yaccB(v, e=e, wm0=wm0):
                        for co in range(NCH):
                            sems.wait(v, "wmmB", wm0 + co + 1)
                            wb = w2B if co % 2 == 0 else w2Bo
                            for r in range(2):
                                so = slice(r * 512, (r + 1) * 512)
                                if r == 1:
                                    v.willinc("yevB")
                                if e == 0:
                                    v.tensor_copy(yacB[co][:, so], wb[r])
                                else:
                                    v.tensor_add(yacB[co][:, so],
                                                 yacB[co][:, so], wb[r])
                    block.vector(f_yaccB)

                bt0 = sems.val("btm")

                def f_b2(tt, bt0=bt0, ye_now=sems.val("yevB"),
                         be0=sems.val("b2e")):
                    sems.wait(tt, "cstBg", CSTBG)
                    sems.wait(tt, "yevB", ye_now)
                    for co in range(NCH):
                        if co >= 2:
                            sems.wait(tt, "b2e", be0 + co - 1)
                        for r in range(2):
                            ins = tt.matmul(w2B[r] if co % 2 == 0 else cbB[r],
                                            b2_s[:, 128 * co:128 * (co + 1)],
                                            msk8[:, r * 512:(r + 1) * 512],
                                            start=True, stop=True)
                        sems.inc(ins, "btm")
                block.tensor(f_b2)

                def f_b2e(v, bt0=bt0):
                    for co in range(NCH):
                        sems.wait(v, "btm", bt0 + co + 1)
                        for r in range(2):
                            so = slice(r * 512, (r + 1) * 512)
                            if r == 1:
                                v.willinc("b2e")
                            v.tensor_add(yacB[co][:, so], yacB[co][:, so],
                                         w2B[r] if co % 2 == 0 else cbB[r])
                block.vector(f_b2e)

                def f_hmoe(v, b2e_now=sems.val("b2e")):
                    sems.wait(v, "b2e", b2e_now)
                    for co in range(NCH):
                        if co == NCH - 1:
                            v.willinc("hmo")
                        v.tensor_add(yacB[co][:, :], yacB[co][:, :],
                                     hf[co][:, :])
                block.vector(f_hmoe)

                nmC = emit_ln(nc, block, sems, t, yacB, None, QT, tag="lnC",
                              ld_edge=("hmo", sems.val("hmo")),
                              wr_edges=[("rtm", sems.val("rtm"))],
                              bc_edges=[("hevB", sems.val("hevB")),
                                        ("rtm", sems.val("rtm"))],
                              gcol=l2g_s, bcol=l2b_s, out_f32=hf)
                prev_lnC = nmC

                xe0 = sems.val("xevB")

                def f_residB(v, nmC=nmC, xe0=xe0):
                    sems.wait(v, nmC[0], nmC[1])
                    sems.wait(v, "hldB", HLDB)
                    for co in range(NCH):
                        for r in range(2):
                            if r == 1:
                                v.willinc("xevB")
                            so = slice(r * 512, (r + 1) * 512)
                            v.scalar_tensor_tensor(xtB[co][:, so],
                                                   hf[co][:, so], 0.5,
                                                   xtB[co][:, so],
                                                   Op.mult, Op.add)
                block.vector(f_residB)

                def f_storeB(sy, qsl=qsl, xe0=xe0):
                    for co in range(NCH):
                        sems.wait(sy, "xevB", xe0 + co + 1)
                        sems.dma(sy.dma_start(
                            out=xo_d[128 * co:128 * (co + 1), qsl],
                            in_=xtB[co][:, :]), "st")
                block.sync(f_storeB)


def build_f0():
    return _standalone(lambda env, xi, xo: emit_f0(env, xi, xo), None)


def run_f0(x_parts, inputs):
    nc = _get_prog("f0", build_f0)
    in_maps = []
    for k in range(NCORES):
        d = f0_host_inputs(inputs, k)
        d["x"] = x_parts[k]
        in_maps.append(d)
    res = run_bass_kernel_spmd(nc, in_maps, core_ids=list(range(NCORES)))
    return [res.results[k]["xo"] for k in range(NCORES)]



def _fft_tables():
    n1 = np.arange(128)
    W = lambda M, sgn: np.exp(sgn * 2j * np.pi * np.outer(n1, n1) / M)
    D1 = W(128, -1)[:64, :]
    TW = np.exp(-2j * np.pi * np.outer(n1, n1) / 16384)
    D2 = W(128, -1)
    E1 = W(128, +1) / 128.0
    T2 = np.exp(+2j * np.pi * np.outer(n1, n1) / 16384)
    E2 = (W(128, +1) / 128.0)[:, :64]
    f32 = np.float32
    combo = lambda M: np.ascontiguousarray(np.concatenate(
        [-M.imag, M.real, M.imag], axis=1).astype(f32))
    return {
        "d1c": combo(D1),
        "twr": np.ascontiguousarray(TW.real.astype(f32)),
        "twi": np.ascontiguousarray(TW.imag.astype(f32)),
        "d2r": np.ascontiguousarray(D2.real.astype(f32)),
        "d2i": np.ascontiguousarray(D2.imag.astype(f32)),
        "e1c": combo(E1),
        "t2r": np.ascontiguousarray(T2.real.astype(f32)),
        "t2i": np.ascontiguousarray(T2.imag.astype(f32)),
        "e2r": np.ascontiguousarray(E2.real.astype(f32)),
        "e2i": np.ascontiguousarray(E2.imag.astype(f32)),
    }


def a1_host_inputs(inputs, core):
    cones = np.ones((128, 128), np.float32)
    cones[1, 1] = EPS
    cones[1, 2] = -0.5
    cones[1, 3] = 0.125
    d = {
        "wuv": np.ascontiguousarray(inputs["L1_attn_W_uv"]),
        "wo_": np.ascontiguousarray(inputs["L1_attn_W_o"]),
        "riw": np.ascontiguousarray(inputs["L1_attn_rpe_in_w"]),
        "rhw": np.ascontiguousarray(inputs["L1_attn_rpe_hid_w"]),
        "rlg": np.ascontiguousarray(np.asarray(inputs["L1_attn_rpe_ln_g"]).T),
        "rlb": np.ascontiguousarray(np.asarray(inputs["L1_attn_rpe_ln_b"]).T),
        "rowc": np.ascontiguousarray(
            np.asarray(inputs["L1_attn_rpe_out_w"])[:, core * 128:
                                                    (core + 1) * 128]),
        "dgc": np.ascontiguousarray(
            np.asarray(inputs["L1_attn_decay_gamma"])
            [core * 128:(core + 1) * 128].reshape(128, 1)),
        "cones": cones,
    }
    zidx = np.zeros((64, 64), np.int32)
    for r in range(NCORES):
        for a in range(8):
            for jg in range(16):
                for b in range(B):
                    zidx[r * 8 + a, jg * 4 + b] = (
                        ((r * 4 + b) * 8 + a) * 128 + (core * 16 + jg))
    oidx = np.zeros((128, 32), np.int32)
    for ch in range(NCORES):
        for b in range(B):
            for j in range(128):
                oidx[j, ch * 4 + b] = ((ch * 128 + j) * 4 + b) * 8 + core
    d["zidx"] = zidx
    d["oidx"] = oidx
    d.update(_fft_tables())
    return d


def emit_a1(env, x_d, xo_d):
    env.enter_phase()
    nc, sems, t, block, pp4 = env.nc, env.sems, env.t, env.block, env.pp4
    wuv_d = env.Dap("wuv", [C, 2 * C])
    wo_d = env.Dap("wo_", [C, C])
    riw_d = env.Dap("riw", [1, 32])
    rhw_d = env.Dap("rhw", [3, 32, 32])
    rlg_d = env.Dap("rlg", [32, 3])
    rlb_d = env.Dap("rlb", [32, 3])
    rowc_d = env.Dap("rowc", [32, 128])
    dgc_d = env.Dap("dgc", [128, 1])
    cones_d = env.Dap("cones", [128, 128])
    zidx_d = env.Dap("zidx", [64, 64], I32)
    oidx_d = env.Dap("oidx", [128, 32], I32)
    tab_d = {}
    for nm, shp in (("d1c", [64, 384]), ("twr", [128, 128]),
                    ("twi", [128, 128]), ("d2r", [128, 128]),
                    ("d2i", [128, 128]), ("e1c", [128, 384]),
                    ("t2r", [128, 128]), ("t2i", [128, 128]),
                    ("e2r", [128, 64]), ("e2i", [128, 64])):
        tab_d[nm] = env.Dap(nm, shp)

    u_scr = env.D("u_scr", [C, TOK], BF16, "Internal")
    vex_in = env.D("vex_in", [B, 8, C, 128], BF16, "Internal")
    vex_out = env.D("vex_out", [NCORES, B, 8, C, 128], BF16, "Internal")
    t_scr = env.D("t_scr", [128, LFULL], BF16, "Internal")
    oex_in = env.D("oex_in", [128, B, LFULL], BF16, "Internal")
    oex_out = env.D("oex_out", [NCORES, 128, B, LFULL], BF16, "Internal")
    ST0 = sems.val("st")

    QT = 1024

    with ExitStack() as ctx:
        psf = t["psf"]

        def sb(name, shape, dt=F32):
            return ctx.enter_context(nc.sbuf_tensor(env.pfx + name, shape,
                                                    dt))

        xt = [sb(f"xt{i}", [128, QT]) for i in range(NCH)]
        wuv_s = sb("wuv_s", [128, NCH * 2 * C], BF16)
        tabs = {nm: sb(nm + "_s", [tab_d[nm].shape[0], tab_d[nm].shape[1]],
                       BF16 if nm in ("d1c", "d2r", "d2i", "e1c", "e2r",
                                      "e2i") else F32)
                for nm in tab_d}
        riw_s = sb("riw_s", [1, 32])
        rhw_s = sb("rhw_s", [32, 3 * 32], BF16)
        rlg_s = sb("rlg_s", [32, 3])
        rlb_s = sb("rlb_s", [32, 3])
        rowc_s = sb("rowc_s", [32, 128], BF16)
        dgc_s = sb("dgc_s", [128, 1])
        lngc = sb("lngc", [128, 1])
        murR = sb("murR", [1, 512])
        vrR = sb("vrR", [1, 512])
        rtR = sb("rtR", [1, 512])
        scope1 = ctx.enter_context(ExitStack())

        def sb1(name, shape, dt=F32):
            return scope1.enter_context(nc.sbuf_tensor(env.pfx + name, shape,
                                                       dt))

        lnu = [sb1(f"lnu{i}", [128, QT], BF16) for i in range(NCH)]
        vtile = [sb1(f"vtile{i}", [128, QT], BF16) for i in range(2)]
        ustg = [sb1(f"ustg{i}", [128, QT], BF16) for i in range(2)]
        hM = sb1("hM", [32, LFULL])
        hN = sb1("hN", [32, LFULL])
        mrow = sb1("mrow", [1, 512])
        mbase = sb1("mbase", [1, 512])
        tdec = sb1("tdec", [128, 512])

        psA = [psf[i // 2][:, (i % 2) * 256:(i % 2) * 256 + 256]
               for i in range(8)]
        psB = [pp4[i // 2][:, (i % 2) * 256:(i % 2) * 256 + 256]
               for i in range(8)]

        emit_const_loads(block, sems, t, cones_d, env)
        CSTG = sems.val("cstg")

        def f_ld0(sy):
            sems.wait(sy, "st", ST0)
            for nm in ("twr", "twi", "t2r", "t2i", "rlg", "rlb", "dgc"):
                src = {"rlg": rlg_d, "rlb": rlb_d, "dgc": dgc_d}.get(nm)
                dst = {"rlg": rlg_s, "rlb": rlb_s, "dgc": dgc_s}.get(nm)
                if src is None:
                    src, dst = tab_d[nm], tabs[nm]
                sems.dma(sy.dma_start(out=dst[:, :], in_=src), "cst2")
        block.sync(f_ld0)

        def f_ld0g(g):
            sems.wait(g, "st", ST0)
            for nm in ("d1c", "d2r", "d2i", "e1c", "e2r", "e2i"):
                sems.dma(g.dma_start(out=tabs[nm][:, :], in_=tab_d[nm]),
                         "wgt")
            sems.dma(g.dma_start(
                out=wuv_s[:, :].rearrange("p (ch i) -> p ch i", i=2 * C),
                in_=wuv_d.rearrange("(ch p) i -> p ch i", p=128)), "wgt")
            sems.dma(g.dma_start(out=riw_s[:, :], in_=riw_d), "wgt")
            sems.dma(g.dma_start(
                out=rhw_s[:, :].rearrange("p (l i) -> p l i", i=32),
                in_=rhw_d.rearrange("l p i -> p l i")), "wgt")
            sems.dma(g.dma_start(out=rowc_s[:, :], in_=rowc_d), "wgt")
            g.willinc("iotm")
            g.iota(mbase[0:1, :], pattern=[[1, 512]], base=0,
                   channel_multiplier=0, allow_small_or_imprecise_dtypes=True)
        block.gpsimd(f_ld0g)
        WGT = sems.val("wgt")
        CST2 = sems.val("cst2")
        IOTM = sems.val("iotm")

        for q in range(B):
            qsl = slice(q * QT, (q + 1) * QT)

            def f_ldx(sy, qsl=qsl, prev=sems.val("lnnm")):
                sems.wait(sy, "lnnm", prev)
                sems.wait(sy, "st", ST0)
                for chk in range(NCH):
                    sems.dma(sy.dma_start(
                        out=xt[chk][:, :],
                        in_=x_d[128 * chk:128 * (chk + 1), qsl]), "xld")
            block.sync(f_ldx)

            nm_edge = emit_ln(nc, block, sems, t, xt, lnu, QT, tag="ln",
                              ld_edge=("xld", sems.val("xld")),
                              wr_edges=[("uvm", sems.val("uvm"))],
                              bc_edges=[("uve", sems.val("uve"))])

            uv0 = sems.val("uvm")
            ue0 = sems.val("uve")

            def f_uv(tt, q=q, nm_edge=nm_edge, uv0=uv0, ue0=ue0):
                sems.wait(tt, "wgt", WGT)
                sems.wait(tt, *nm_edge)
                for rc in range(16):
                    if rc >= 2:
                        sems.wait(tt, "uve", ue0 + rc - 1)
                    for cc in range(NCH):
                        w = wuv_s[:, cc * 2 * C + rc * 128:
                                  cc * 2 * C + rc * 128 + 128]
                        for r in range(2):
                            ins = tt.matmul(
                                pp4[2 * (rc % 2) + r][:, :], w,
                                lnu[cc][:, r * 512:(r + 1) * 512],
                                start=(cc == 0), stop=(cc == NCH - 1))
                    sems.inc(ins, "uvm")
            block.tensor(f_uv)

            def f_uve(sc, q=q, uv0=uv0,
                      vs0=[sems.val("vst0"), sems.val("vst1")],
                      us0=[sems.val("ust0"), sems.val("ust1")]):
                for rc in range(16):
                    sems.wait(sc, "uvm", uv0 + rc + 1)
                    if rc < 8:
                        sems.wait(sc, "ust" + str(rc % 2),
                                  us0[rc % 2] + 16 * (rc // 2))
                    else:
                        sems.wait(sc, "vst" + str(rc % 2),
                                  vs0[rc % 2] + 16 * ((rc - 8) // 2))
                    for r in range(2):
                        so = slice(r * 512, (r + 1) * 512)
                        if rc < 8:
                            dst = ustg[rc % 2][:, so]
                        else:
                            dst = vtile[(rc - 8) % 2][:, so]
                        if r == 1:
                            sc.willinc("uve")
                        sc.activation(dst, pp4[2 * (rc % 2) + r][:, :],
                                      AF.Silu)
            block.scalar(f_uve)

            def f_ust(sy, q=q, ue0=ue0):
                for uc in range(NCH):
                    sems.wait(sy, "uve", ue0 + uc + 1)
                    sems.dma(sy.dma_start(
                        out=u_scr.ap()[128 * uc:128 * (uc + 1),
                                       q * QT:(q + 1) * QT],
                        in_=ustg[uc % 2][:, :]), "ust" + str(uc % 2))
            block.sync(f_ust)

            def f_vst(sy, q=q, ue0=ue0):
                for vc in range(NCH):
                    sems.wait(sy, "uve", ue0 + 8 + vc + 1)
                    sems.dma(sy.dma_start(
                        out=vex_in.ap()[q, :, 128 * vc:128 * (vc + 1), :]
                        .rearrange("a c n -> c a n"),
                        in_=vtile[vc % 2][:, :].rearrange(
                            "p (a n) -> p a n", n=128)), "vst" + str(vc % 2))
            block.sync(f_vst)

        hfin = sb1("hfin", [32, LFULL], BF16)
        nrmt = sb1("nrmt", [32, 512], BF16)
        ttmp = [sb1(f"ttmp{i}", [128, 512], BF16) for i in range(2)]

        def f_rpe0(tt):
            sems.wait(tt, "wgt", WGT)
            sems.wait(tt, "cst2", CST2)
            sems.wait(tt, "iotm", IOTM)
            sems.wait(tt, "uvm", sems.val("uvm"))
            sems.wait(tt, "lnnm", sems.val("lnnm"))
            sems.wait(tt, "uve", sems.val("uve"))
        block.tensor(f_rpe0)

        RP_H0 = sems.val("rpeP")
        for mc in range(16):
            def g0(v, mc=mc, pv=sems.val("rpeP"), im=IOTM):
                sems.wait(v, "rpeP", pv)
                sems.wait(v, "iotm", im)
                v.willinc("iotg")
                v.tensor_scalar_add(mrow[0:1, :], mbase[0:1, :],
                                    float(mc * 512))
            block.vector(g0)

            def p0(tt, mc=mc, rv=sems.val("rpeA"), gv=sems.val("iotg")):
                sems.wait(tt, "rpeA", max(0, rv - 1))
                sems.wait(tt, "iotg", gv)
                sems.inc(tt.matmul(psf[mc % 2][0:32, 0:512], riw_s[0:1, :],
                                   mrow[0:1, :],
                                   start=True, stop=True), "rpeP")
            block.tensor(p0)

            def a0_(sc, mc=mc, pv=sems.val("rpeP")):
                sems.wait(sc, "rpeP", pv)
                sc.willinc("rpeA")
                sc.activation(hM[:, mc * 512:(mc + 1) * 512],
                              psf[mc % 2][0:32, 0:512], AF.Silu)
            block.scalar(a0_)

        cur, nxt = hM, hN
        for ly in range(3):
            for mc in range(16):
                sl = slice(mc * 512, (mc + 1) * 512)

                def aq(sc, cur=cur, sl=sl, pv=sems.val("rpeP"),
                       av=sems.val("rpeA")):
                    sems.wait(sc, "rpeP", pv)
                    sc.willinc("rpeA")
                    sc.activation(t["sq2"][0][0:32, :], cur[0:32, sl],
                                  AF.Square)
                block.scalar(aq)

                def p1(tt, cur=cur, sl=sl, av=sems.val("rpeA")):
                    sems.wait(tt, "rpeA", av)
                    tt.matmul(psf[0][0:1, 0:512], t["ones_col"][0:32, 0:1],
                              cur[0:32, sl], start=True, stop=True)
                    sems.inc(tt.matmul(psf[1][0:1, 0:512],
                                       t["ones_col"][0:32, 0:1],
                                       t["sq2"][0][0:32, :],
                                       start=True, stop=True), "rpeP")
                block.tensor(p1)

                def v1(v, pv=sems.val("rpeP")):
                    sems.wait(v, "rpeP", pv)
                    v.tensor_scalar_mul(murR[0:1, :], psf[0][0:1, 0:512],
                                        1.0 / 32)
                    v.tensor_scalar_mul(vrR[0:1, :], psf[1][0:1, 0:512],
                                        1.0 / 32)
                    v.tensor_mul(rtR[0:1, :], murR[0:1, :], murR[0:1, :])
                    v.tensor_sub(vrR[0:1, :], vrR[0:1, :], rtR[0:1, :])
                    v.willinc("rpeV")
                    v.tensor_scalar_add(vrR[0:1, :], vrR[0:1, :], EPS)
                block.vector(v1)

                def a2(sc, vv=sems.val("rpeV")):
                    sems.wait(sc, "rpeV", vv)
                    sc.activation(rtR[0:1, :], vrR[0:1, :], AF.Ln)
                    sc.willinc("rpeA")
                    sc.activation(vrR[0:1, :], rtR[0:1, :], AF.Exp, scale=-0.5)
                block.scalar(a2)

                def p2(tt, av=sems.val("rpeA")):
                    sems.wait(tt, "rpeA", av)
                    tt.matmul(psf[2][0:32, 0:512], t["ones_row"][0:1, 0:32],
                              murR[0:1, :], start=True, stop=True)
                    sems.inc(tt.matmul(psf[3][0:32, 0:512],
                                       t["ones_row"][0:1, 0:32],
                                       vrR[0:1, :], start=True, stop=True),
                             "rpeP")
                block.tensor(p2)

                def v2(v, ly=ly, cur=cur, sl=sl, pv=sems.val("rpeP")):
                    sems.wait(v, "rpeP", pv)
                    v.tensor_sub(t["lntmp"][0:32, :], cur[0:32, sl],
                                 psf[2][0:32, 0:512])
                    v.tensor_mul(t["lntmp"][0:32, :], t["lntmp"][0:32, :],
                                 psf[3][0:32, 0:512])
                    v.willinc("rpeV")
                    v.tensor_scalar(nrmt[0:32, :], t["lntmp"][0:32, :],
                                    rlg_s[0:32, ly:ly + 1],
                                    rlb_s[0:32, ly:ly + 1], Op.mult, Op.add)
                block.vector(v2)

                def p3(tt, ly=ly, vv=sems.val("rpeV")):
                    sems.wait(tt, "rpeV", vv)
                    sems.inc(tt.matmul(pp4[0][0:32, 0:512],
                                       rhw_s[:, ly * 32:(ly + 1) * 32],
                                       nrmt[0:32, :], start=True, stop=True),
                             "rpeP")
                block.tensor(p3)

                def a3(sc, ly=ly, nxt=nxt, sl=sl, pv=sems.val("rpeP")):
                    sems.wait(sc, "rpeP", pv)
                    sc.willinc("rpeA")
                    dst = hfin[0:32, sl] if ly == 2 else nxt[0:32, sl]
                    sc.activation(dst, pp4[0][0:32, 0:512], AF.Silu)
                block.scalar(a3)
            cur, nxt = nxt, cur

        def agam(sc, pv=sems.val("rpeP")):
            sems.wait(sc, "rpeP", pv)
            sc.willinc("rpeA")
            sc.activation(lngc[:, :], dgc_s[:, :], AF.Sigmoid)
        block.scalar(agam)

        def vgam(v, av=sems.val("rpeA")):
            sems.wait(v, "rpeA", av)
            v.willinc("rpeV")
            v.tensor_scalar(lngc[:, :], lngc[:, :], 0.01, 0.99,
                            Op.mult, Op.add)
        block.vector(vgam)

        def agam2(sc, vv=sems.val("rpeV")):
            sems.wait(sc, "rpeV", vv)
            sc.willinc("rpeA")
            sc.activation(lngc[:, :], lngc[:, :], AF.Ln)
        block.scalar(agam2)

        for mc in range(16):
            sl = slice(mc * 512, (mc + 1) * 512)

            def gt(v, mc=mc, pv=sems.val("rpeP")):
                sems.wait(v, "rpeP", pv)
                v.willinc("iotg")
                v.tensor_scalar_add(mrow[0:1, :], mbase[0:1, :],
                                    float(mc * 512))
            block.vector(gt)

            def pt(tt, sl=sl, vv=sems.val("rpeV"), av=sems.val("rpeA"),
                   gv=sems.val("iotg")):
                sems.wait(tt, "rpeA", av)
                sems.wait(tt, "rpeV", vv)
                sems.wait(tt, "iotg", gv)
                tt.matmul(pp4[1][:, :], rowc_s[:, :], hfin[0:32, sl],
                          start=True, stop=True)
                sems.inc(tt.matmul(pp4[2][:, :], t["ones_row"][:, :],
                                   mrow[0:1, :], start=True, stop=True),
                         "rpeP")
            block.tensor(pt)

            def vt1(v, pv=sems.val("rpeP")):
                sems.wait(v, "rpeP", pv)
                v.willinc("rpeV")
                v.tensor_scalar(tdec[:, :], pp4[2][:, :], lngc[:, 0:1], None,
                                Op.mult)
            block.vector(vt1)

            def at1(sc, vv=sems.val("rpeV")):
                sems.wait(sc, "rpeV", vv)
                sc.willinc("rpeA")
                sc.activation(tdec[:, :], tdec[:, :], AF.Exp)
            block.scalar(at1)

            def vt2(v, mc=mc, sl=sl, av=sems.val("rpeA"),
                    ts0=sems.val("tst")):
                sems.wait(v, "rpeA", av)
                if mc >= 2:
                    sems.wait(v, "tst", ts0)
                v.willinc("rpeV")
                v.tensor_tensor(ttmp[mc % 2][:, :], pp4[1][:, :],
                                tdec[:, :], Op.mult)
            block.vector(vt2)

            def f_tstc(sy, mc=mc, sl=sl, vv=sems.val("rpeV")):
                sems.wait(sy, "rpeV", vv)
                sems.dma(sy.dma_start(out=t_scr.ap()[:, sl],
                                      in_=ttmp[mc % 2][:, :]), "tst")
            block.sync(f_tstc)
        TST = sems.val("tst")

        def f_ag1(g, v0=sems.val("vst0"), v1=sems.val("vst1"),
                  u0=sems.val("ust0"), u1=sems.val("ust1")):
            sems.wait(g, "vst0", v0)
            sems.wait(g, "vst1", v1)
            sems.wait(g, "ust0", u0)
            sems.wait(g, "ust1", u1)
            sems.inc(g.collective_compute(
                "AllGather", mybir.AluOpType.bypass,
                replica_groups=[list(range(NCORES))],
                ins=[vex_in.ap().rearrange("b a c n -> (b a c) n").opt()],
                outs=[vex_out.ap().rearrange(
                    "r b a c n -> (r b a c) n").opt()]), "cc")
        block.gpsimd(f_ag1)

        scope1.close()
        scope2 = ctx.enter_context(ExitStack())

        def sb2(name, shape, dt=F32):
            return scope2.enter_context(nc.sbuf_tensor(env.pfx + name, shape,
                                                       dt))

        zr = sb2("zr", [64, 1024], BF16)
        zi = sb2("zi", [64, 1024], BF16)
        zidx_s = sb2("zidx_s", [64, 64], I32)
        ttile = sb2("ttile", [64, 1024], BF16)
        yc = sb2("yc", [128, 8 * 384], BF16)
        s1c = sb2("s1c", [128, 256])
        s2c = sb2("s2c", [128, 256])
        tspec = sb2("tspec", [128, 8 * 256], BF16)
        zspec = sb2("zspec", [128, 8 * 256], BF16)
        pspec = sb2("pspec", [128, 8 * 256], BF16)
        otile = sb2("otile", [64, 8 * 256], BF16)

        CC1 = sems.val("cc")

        def f_pid0(g):
            sems.wait(g, "cc", CC1)
            sems.dma(g.dma_start(out=zidx_s[:, :], in_=zidx_d), "cvLg")
        block.gpsimd(f_pid0)
        ZIDXL = sems.val("cvLg")
        vex2d = vex_out.ap().rearrange("r b a (cg w) n -> (r b a cg) (w n)",
                                       w=8)

        def tw_batch(v, src_ps, dst, ttr, tti):
            for c in range(8):
                orr = src_ps[c][:, 0:128]
                oii = src_ps[c][:, 128:256]
                v.tensor_tensor(s1c[:, 0:128], orr, ttr, Op.mult)
                v.tensor_tensor(s2c[:, 0:128], oii, tti, Op.mult)
                v.tensor_sub(dst[:, c * 384 + 128:c * 384 + 256],
                             s1c[:, 0:128], s2c[:, 0:128])
                v.tensor_tensor(s1c[:, 128:256], orr, tti, Op.mult)
                v.tensor_tensor(s2c[:, 128:256], oii, ttr, Op.mult)
                v.tensor_add(dst[:, c * 384 + 256:c * 384 + 384],
                             s1c[:, 128:256], s2c[:, 128:256])
                v.scalar_tensor_tensor(
                    dst[:, c * 384:c * 384 + 128], s1c[:, 128:256],
                    -1.0, s2c[:, 128:256], Op.mult, Op.subtract)

        for jg in range(16):
            def f_cl(sy, jg=jg, pv=sems.val("cvP")):
                sems.wait(sy, "cc", CC1)
                sems.wait(sy, "cvP", pv)
                sems.wait(sy, "tst", TST)
                sems.dma(sy.dma_start(
                    out=ttile[:, :].rearrange("p (c n) -> p c n", n=128),
                    in_=t_scr.ap()[8 * jg:8 * (jg + 1), :].rearrange(
                        "c (a n) -> a c n", n=128)), "cvL")
            block.sync(f_cl)
            CVL_T = sems.val("cvL")

            def f_ts1(tt, jg=jg, lv=CVL_T, vv=sems.val("cvV")):
                sems.wait(tt, "cvL", lv)
                sems.wait(tt, "tst", TST)
                sems.wait(tt, "cvV", vv)
                for c in range(8):
                    sems.inc(tt.matmul(psA[c],
                                       ttile[:, c * 128:(c + 1) * 128],
                                       tabs["d1c"][:, 128:384],
                                       start=True, stop=True), "cvP")
            block.tensor(f_ts1)

            def f_tw1(v, pv=sems.val("cvP")):
                sems.wait(v, "cvP", pv)
                tw_batch(v, psA, yc, tabs["twr"][:, :], tabs["twi"][:, :])
                v.willinc("cvV")
                v.tensor_copy(s1c[:, 0:2], s1c[:, 0:2])
            block.vector(f_tw1)

            def f_ts2(tt, vv=sems.val("cvV"), sv=sems.val("cvS")):
                sems.wait(tt, "cvV", vv)
                sems.wait(tt, "cvS", sv)
                for c in range(8):
                    tt.matmul(psB[c], tabs["d2r"][:, :],
                              yc[:, c * 384 + 128:c * 384 + 384],
                              start=True, stop=False)
                    sems.inc(tt.matmul(psB[c], tabs["d2i"][:, :],
                                       yc[:, c * 384:c * 384 + 256],
                                       start=False, stop=True), "cvP")
            block.tensor(f_ts2)

            def f_tse(v, pv=sems.val("cvP")):
                sems.wait(v, "cvP", pv)
                for c in range(8):
                    if c == 7:
                        v.willinc("cvS")
                    v.tensor_copy(tspec[:, c * 256:(c + 1) * 256], psB[c])
            block.vector(f_tse)

            for bp in range(2):
                def f_zl(g, jg=jg, bp=bp, pv=sems.val("cvP")):
                    sems.wait(g, "cvP", pv)
                    sems.wait(g, "cvLg", ZIDXL)
                    sems.dma(g.indirect_dma_start(
                        out=zr[:, :], out_offset=None, in_=vex2d,
                        in_offset=IndirectOffsetOnAxis(
                            ap=zidx_s[:, jg * 4 + 2 * bp:jg * 4 + 2 * bp + 1],
                            axis=0)), "cvLg")
                    sems.dma(g.indirect_dma_start(
                        out=zi[:, :], out_offset=None, in_=vex2d,
                        in_offset=IndirectOffsetOnAxis(
                            ap=zidx_s[:, jg * 4 + 2 * bp + 1:
                                      jg * 4 + 2 * bp + 2],
                            axis=0)), "cvLg")
                block.gpsimd(f_zl)
                CVL_Z = sems.val("cvLg")

                def f_z1(tt, lv=CVL_Z, vv=sems.val("cvV")):
                    sems.wait(tt, "cvLg", lv)
                    sems.wait(tt, "cvV", vv)
                    for c in range(8):
                        tt.matmul(psA[c], zr[:, c * 128:(c + 1) * 128],
                                  tabs["d1c"][:, 128:384],
                                  start=True, stop=False)
                        sems.inc(tt.matmul(psA[c],
                                           zi[:, c * 128:(c + 1) * 128],
                                           tabs["d1c"][:, 0:256],
                                           start=False, stop=True), "cvP")
                block.tensor(f_z1)

                def f_zw(v, pv=sems.val("cvP")):
                    sems.wait(v, "cvP", pv)
                    tw_batch(v, psA, yc, tabs["twr"][:, :], tabs["twi"][:, :])
                    v.willinc("cvV")
                    v.tensor_copy(s1c[:, 0:2], s1c[:, 0:2])
                block.vector(f_zw)

                def f_z2(tt, vv=sems.val("cvV"), sv=sems.val("cvS")):
                    sems.wait(tt, "cvV", vv)
                    sems.wait(tt, "cvS", sv)
                    for c in range(8):
                        tt.matmul(psB[c], tabs["d2r"][:, :],
                                  yc[:, c * 384 + 128:c * 384 + 384],
                                  start=True, stop=False)
                        sems.inc(tt.matmul(psB[c], tabs["d2i"][:, :],
                                           yc[:, c * 384:c * 384 + 256],
                                           start=False, stop=True), "cvP")
                block.tensor(f_z2)

                def f_zse(v, pv=sems.val("cvP")):
                    sems.wait(v, "cvP", pv)
                    for c in range(8):
                        v.tensor_copy(zspec[:, c * 256:(c + 1) * 256],
                                      psB[c])
                    zrv = zspec[:, :].rearrange("p (c w n) -> p c w n",
                                                w=2, n=128)[:, :, 0, :]
                    ziv = zspec[:, :].rearrange("p (c w n) -> p c w n",
                                                w=2, n=128)[:, :, 1, :]
                    trv = tspec[:, :].rearrange("p (c w n) -> p c w n",
                                                w=2, n=128)[:, :, 0, :]
                    tiv = tspec[:, :].rearrange("p (c w n) -> p c w n",
                                                w=2, n=128)[:, :, 1, :]
                    prv = pspec[:, :].rearrange("p (c w n) -> p c w n",
                                                w=2, n=128)[:, :, 0, :]
                    piv = pspec[:, :].rearrange("p (c w n) -> p c w n",
                                                w=2, n=128)[:, :, 1, :]
                    sp1 = yc[:, 0:1024].rearrange("p (c n) -> p c n", n=128)
                    sp2 = yc[:, 1024:2048].rearrange("p (c n) -> p c n",
                                                     n=128)
                    v.tensor_tensor(sp1, zrv, trv, Op.mult)
                    v.tensor_tensor(sp2, ziv, tiv, Op.mult)
                    v.tensor_tensor(prv, sp1, sp2, Op.subtract)
                    v.tensor_tensor(sp1, zrv, tiv, Op.mult)
                    v.tensor_tensor(sp2, ziv, trv, Op.mult)
                    v.willinc("cvS")
                    v.tensor_tensor(piv, sp1, sp2, Op.add)
                block.vector(f_zse)

                def f_i1(tt, sv=sems.val("cvS"), vv=sems.val("cvV")):
                    sems.wait(tt, "cvS", sv)
                    sems.wait(tt, "cvV", vv)
                    for c in range(8):
                        tt.matmul(psA[c],
                                  pspec[:, c * 256:c * 256 + 128],
                                  tabs["e1c"][:, 128:384],
                                  start=True, stop=False)
                        sems.inc(tt.matmul(psA[c],
                                           pspec[:, c * 256 + 128:
                                                 (c + 1) * 256],
                                           tabs["e1c"][:, 0:256],
                                           start=False, stop=True), "cvP")
                block.tensor(f_i1)

                def f_iw(v, pv=sems.val("cvP")):
                    sems.wait(v, "cvP", pv)
                    tw_batch(v, psA, yc, tabs["t2r"][:, :], tabs["t2i"][:, :])
                    v.willinc("cvV")
                    v.tensor_copy(s1c[:, 0:2], s1c[:, 0:2])
                block.vector(f_iw)

                def f_i2(tt, vv=sems.val("cvV"), ov=sems.val("cvO")):
                    sems.wait(tt, "cvV", vv)
                    sems.wait(tt, "cvO", ov)
                    for c in range(8):
                        tt.matmul(psB[c][0:64, :], tabs["e2r"][:, :],
                                  yc[:, c * 384 + 128:c * 384 + 384],
                                  start=True, stop=False)
                        sems.inc(tt.matmul(psB[c][0:64, :],
                                           tabs["e2i"][:, :],
                                           yc[:, c * 384:c * 384 + 256],
                                           start=False, stop=True), "cvP")
                block.tensor(f_i2)

                def f_oev(v, pv=sems.val("cvP"), os0=sems.val("ost")):
                    sems.wait(v, "cvP", pv)
                    sems.wait(v, "ost", os0)
                    for c in range(8):
                        if c == 7:
                            v.willinc("cvO")
                        v.tensor_copy(otile[:, c * 256:(c + 1) * 256],
                                      psB[c][0:64, :])
                block.vector(f_oev)

                def f_ost(sy, jg=jg, bp=bp, ov=sems.val("cvO")):
                    sems.wait(sy, "cvO", ov)
                    for pl in range(2):
                        dst = oex_in.ap()[8 * jg:8 * (jg + 1),
                                          2 * bp + pl, :].rearrange(
                            "c (a n) -> a c n", n=128)
                        src = otile[:, :].rearrange(
                            "p (c w n) -> p c w n", w=2, n=128)[:, :, pl, :]
                        sems.dma(sy.dma_start(out=dst, in_=src), "ost")
                block.sync(f_ost)

        def f_ag2(g, ost_all=sems.val("ost")):
            sems.wait(g, "ost", ost_all)
            sems.inc(g.collective_compute(
                "AllGather", mybir.AluOpType.bypass,
                replica_groups=[list(range(NCORES))],
                ins=[oex_in.ap().rearrange("j b n -> (j b) n").opt()],
                outs=[oex_out.ap().rearrange(
                    "r j b n -> (r j b) n").opt()]), "cc")
            g.wait_ge(sems.handle("cc"), sems.val("cc"))
            sems.dma(g.dma_start(
                out=wuv_s[:, 0:NCH * C].rearrange("p (ch i) -> p ch i", i=C),
                in_=wo_d.rearrange("(ch p) i -> p ch i", p=128)), "wgt2")
        block.gpsimd(f_ag2)
        WGT2 = sems.val("wgt2")

        scope2.close()
        obf = [ctx.enter_context(nc.sbuf_tensor(f"obf{i}", [128, TOK], BF16))
               for i in range(NCH)]
        utile = [ctx.enter_context(nc.sbuf_tensor(f"utile{i}", [128, TOK],
                                                  BF16)) for i in range(2)]
        oidx_s = ctx.enter_context(nc.sbuf_tensor("oidx_s", [128, 32], I32))
        oex2d = oex_out.ap().rearrange("r j b (lg n) -> (r j b lg) n",
                                       n=LC)

        CC2 = sems.val("cc")

        def f_ol(g):
            sems.wait(g, "cc", CC2)
            sems.dma(g.dma_start(out=oidx_s[:, :], in_=oidx_d), "old")
            g.wait_ge(sems.handle("old"), sems.val("old"))
            for ch in range(NCH):
                for b in range(B):
                    sems.dma(g.indirect_dma_start(
                        out=obf[ch][:, b * LC:(b + 1) * LC], out_offset=None,
                        in_=oex2d,
                        in_offset=IndirectOffsetOnAxis(
                            ap=oidx_s[:, ch * 4 + b:ch * 4 + b + 1],
                            axis=0)), "old")
        block.gpsimd(f_ol)
        OLD = sems.val("old")

        uld0 = [sems.val("uld0"), sems.val("uld1")]

        def f_ul(sy, zm0=sems.val("zme")):
            sems.wait(sy, "cc", CC2)
            for ch in range(NCH):
                if ch >= 2:
                    sems.wait(sy, "zme", zm0 + ch - 1)
                sems.dma(sy.dma_start(
                    out=utile[ch % 2][:, :],
                    in_=u_scr.ap()[128 * ch:128 * (ch + 1), :]),
                    "uld" + str(ch % 2))
        block.sync(f_ul)

        def f_zmul(v, old=OLD, uld0=uld0):
            sems.wait(v, "old", old)
            for ch in range(NCH):
                sems.wait(v, "uld" + str(ch % 2),
                          uld0[ch % 2] + 16 * (ch // 2 + 1))
                v.willinc("zme")
                v.tensor_mul(obf[ch][:, :], obf[ch][:, :],
                             utile[ch % 2][:, :])
        block.vector(f_zmul)
        ZME = sems.val("zme")

        for q in range(B):
            qsl = slice(q * QT, (q + 1) * QT)

            def f_lxq(sy, qsl=qsl, st_prev=sems.val("st"),
                      nm_all=sems.val("lnnm")):
                if q == 0:
                    sems.wait(sy, "lnnm", nm_all)
                else:
                    sems.wait(sy, "st", st_prev)
                for chk in range(NCH):
                    sems.dma(sy.dma_start(
                        out=xt[chk][:, :],
                        in_=x_d[128 * chk:128 * (chk + 1), qsl]), "xld2")
            block.sync(f_lxq)
            XLD2 = sems.val("xld2")

            op0 = sems.val("opm")
            xe0 = sems.val("xev")

            def f_oproj(tt, q=q, op0=op0, xe0=xe0, cvo=sems.val("cvO"),
                        rpv=sems.val("rpeV")):
                sems.wait(tt, "zme", ZME)
                sems.wait(tt, "wgt2", WGT2)
                if q == 0:
                    sems.wait(tt, "cvO", cvo)
                    sems.wait(tt, "rpeV", rpv)
                for co in range(NCH):
                    sems.wait(tt, "xev", xe0 + max(0, co - 1))
                    for cc in range(NCH):
                        w = wuv_s[:, cc * C + co * 128:cc * C + co * 128
                                  + 128]
                        for r in range(2):
                            ins = tt.matmul(
                                pp4[r if co % 2 == 0 else r + 2][:, :], w,
                                obf[cc][:, q * QT + r * 512:
                                        q * QT + (r + 1) * 512],
                                start=(cc == 0), stop=(cc == NCH - 1))
                    sems.inc(ins, "opm")
            block.tensor(f_oproj)

            def f_res(v, q=q, op0=op0, xld2=XLD2):
                sems.wait(v, "xld2", xld2)
                for co in range(NCH):
                    sems.wait(v, "opm", op0 + co + 1)
                    for r in range(2):
                        if r == 1:
                            v.willinc("xev")
                        so = slice(r * 512, (r + 1) * 512)
                        v.tensor_add(xt[co][:, so], xt[co][:, so],
                                     pp4[r if co % 2 == 0 else r + 2][:, :])
            block.vector(f_res)

            def f_st(sy, qsl=qsl, xe0=xe0):
                for co in range(NCH):
                    sems.wait(sy, "xev", xe0 + co + 1)
                    sems.dma(sy.dma_start(
                        out=xo_d[128 * co:128 * (co + 1), qsl],
                        in_=xt[co][:, :]), "st")
            block.sync(f_st)


def build_a1():
    return _standalone(lambda env, xi, xo: emit_a1(env, xi, xo), None)


def run_a1(x_parts, inputs):
    nc = _get_prog("a1", build_a1)
    in_maps = []
    for k in range(NCORES):
        d = a1_host_inputs(inputs, k)
        d["x"] = x_parts[k]
        in_maps.append(d)
    res = run_bass_kernel_spmd(nc, in_maps, core_ids=list(range(NCORES)))
    return [res.results[k]["xo"] for k in range(NCORES)]
